# version 19
# baseline (speedup 1.0000x reference)
import numpy as np
import ml_dtypes
from contextlib import ExitStack

import concourse.bass as bass
import concourse.mybir as mybir
from concourse.bass_utils import run_bass_kernel_spmd

F32 = mybir.dt.float32
BF16 = mybir.dt.bfloat16
I32 = mybir.dt.int32
AF = mybir.ActivationFunctionType
ALU = mybir.AluOpType

BF = 2


class Cfg:
    def __init__(self, n_cores=8, batch=16, T=2048, D=2048, H=16, INTER=5504,
                 LUT=16, tblock=1024, eps=1e-5, rope_base=10000.0):
        self.n_cores = n_cores
        self.batch = batch
        self.T = T
        self.D = D
        self.H = H
        self.DH = D // H
        assert self.DH == 128
        self.INTER = INTER
        self.LUT = LUT
        self.TB = tblock
        self.eps = eps
        self.rope_base = rope_base
        self.BPC = batch // n_cores
        self.IC = D // 128
        self.TT = T // 128
        self.QT = T // 512
        self.OT = INTER // 128
        self.DC = (D + 511) // 512
        assert D % 128 == 0 and INTER % 128 == 0 and T % 512 == 0
        assert D % n_cores == 0 and INTER % n_cores == 0 and T % tblock == 0
        self.gslice = D // n_cores
        self.dslice = INTER // n_cores
        self.scale = self.DH ** -0.5
        self.HG = 2
        assert H % self.HG == 0
        self.NTB = T // tblock
        self.TBT = tblock // 128
        assert self.TBT <= 8
        assert D * 4 <= 2 * T * BF


ENGINES = ("pe", "dve", "act", "pool", "sp")


class Em:

    def __init__(self, nc, stack):
        self.nc = nc
        self.stack = stack
        self.ops = {e: [] for e in ENGINES}
        self.cnt = {e: 0 for e in ENGINES}
        self.esem = {e: stack.enter_context(nc.semaphore(f"es_{e}")) for e in ENGINES}
        self.vc = {e: {} for e in ENGINES}
        self.self_waited = {e: 0 for e in ENGINES}
        self.snap = {e: {} for e in ENGINES}
        self.track = {}
        self.queues = {}
        self.n_waits = 0

    def _merge(self, eng, evvc):
        vc = self.vc[eng]
        for k, v in evvc.items():
            if vc.get(k, 0) < v:
                vc[k] = v

    def await_ev(self, eng, ev):
        axis, n = ev
        if axis == eng:
            if eng == "pe":
                return
            if self.self_waited[eng] < n:
                sem = self.esem[eng]
                self.ops[eng].append(lambda h, s=sem, v=n: h.wait_ge(s, v))
                self.self_waited[eng] = n
                self.n_waits += 1
            return
        if self.vc[eng].get(axis, 0) >= n:
            return
        if axis in ENGINES:
            sem, val = self.esem[axis], n
        else:
            qname, slot = axis
            q = self.queues[qname]
            sem, val = q.sems[slot], 16 * n
        self.ops[eng].append(lambda h, s=sem, v=val: h.wait_ge(s, v))
        self.n_waits += 1
        self._merge(eng, self.snap[axis][n])
        if self.vc[eng].get(axis, 0) < n:
            self.vc[eng][axis] = n

    def _t(self, name):
        if name not in self.track:
            self.track[name] = {"w": [], "r": []}
        return self.track[name]

    @staticmethod
    def _ov(lo, hi, rec):
        return rec[0] < hi and lo < rec[1]

    def _deps_for(self, reads, writes):
        deps = []
        for (name, lo, hi) in reads:
            t = self._t(name)
            deps += [rec[2] for rec in t["w"] if self._ov(lo, hi, rec)]
        for (name, lo, hi) in writes:
            t = self._t(name)
            deps += [rec[2] for rec in t["w"] if self._ov(lo, hi, rec)]
            deps += [rec[2] for rec in t["r"] if self._ov(lo, hi, rec)]
        return deps

    def _record(self, reads, writes, ev):
        for (name, lo, hi) in reads:
            self._t(name)["r"].append((lo, hi, ev))
        for (name, lo, hi) in writes:
            t = self._t(name)
            t["w"] = [rec for rec in t["w"] if not (rec[0] >= lo and rec[1] <= hi)]
            t["r"] = [rec for rec in t["r"] if not (rec[0] >= lo and rec[1] <= hi)]
            t["w"].append((lo, hi, ev))

    def _snapshot(self, eng, axis, n):
        vcs = dict(self.vc[eng])
        vcs[eng] = self.cnt[eng]
        vcs[axis] = n
        self.snap[axis][n] = vcs

    def op(self, eng, fns, reads=(), writes=()):
        if not isinstance(fns, (list, tuple)):
            fns = [fns]
        for ev in self._deps_for(reads, writes):
            self.await_ev(eng, ev)
        n = self.cnt[eng] + 1
        self.cnt[eng] = n
        sem = self.esem[eng]
        for f in fns[:-1]:
            self.ops[eng].append(f)
        f = fns[-1]
        self.ops[eng].append(lambda h, f=f, s=sem: f(h).then_inc(s, 1))
        self.vc[eng][eng] = n
        self._snapshot(eng, eng, n)
        ev = (eng, n)
        self._record(reads, writes, ev)
        return ev

    def finalize(self, block):
        hmap = {"pe": block.tensor, "dve": block.vector, "act": block.scalar,
                "pool": block.gpsimd, "sp": block.sync}
        for e in ENGINES:
            lst = self.ops[e]

            def body(h, lst=lst):
                for f in lst:
                    f(h)
            hmap[e](body)


class DmaQ:

    def __init__(self, em, eng, name, K=8):
        self.em = em
        self.eng = eng
        self.name = name
        self.K = K
        self.sems = [em.stack.enter_context(em.nc.semaphore(f"q_{name}{i}"))
                     for i in range(K)]
        self.n = 0
        em.queues[name] = self
        for i in range(K):
            em.snap[(name, i)] = {}

    def _emit(self, fn, reads, writes):
        em, eng = self.em, self.eng
        n = self.n + 1
        slot = (n - 1) % self.K
        occ = (n - 1) // self.K + 1
        axis = (self.name, slot)
        if occ > 1:
            em.await_ev(eng, (axis, occ - 1))
        for ev in em._deps_for(reads, writes):
            em.await_ev(eng, ev)
        sem = self.sems[slot]
        em.ops[eng].append(lambda h, fn=fn, s=sem: fn(h).then_inc(s, 16))
        self.n = n
        em._snapshot(eng, axis, occ)
        ev = (axis, occ)
        em._record(reads, writes, ev)
        return ev

    def issue(self, out_ap, in_ap, reads=(), writes=(), **kw):
        return self._emit(
            lambda h, o=out_ap, i=in_ap, kw=kw: h.dma_start(out=o, in_=i, **kw),
            reads, writes)

    def issue_fn(self, fn, reads=(), writes=()):
        return self._emit(fn, reads, writes)

    def final_waits(self, eng):
        em = self.em
        counts = [0] * self.K
        for j in range(1, self.n + 1):
            counts[(j - 1) % self.K] += 16
        for s in range(self.K):
            if counts[s]:
                em.ops[eng].append(
                    lambda h, sm=self.sems[s], v=counts[s]: h.wait_ge(sm, v))


def ival(name, lo=0, hi=1 << 40):
    return (name, lo, hi)


def build_program(cfg: Cfg):
    nc = bass.Bass("TRN2", target_bir_lowering=False)
    C = cfg
    T, D, H, IC, TT, QT = C.T, C.D, C.H, C.IC, C.TT, C.QT

    dx = nc.dram_tensor("x", [C.BPC, T, D], F32, kind="ExternalInput")
    dwq = nc.dram_tensor("wqT", [H, 128, D], F32, kind="ExternalInput")
    dwk = nc.dram_tensor("wkT", [H, 128, D], F32, kind="ExternalInput")
    dwv = nc.dram_tensor("wvT", [H // 2, 128, 2 * D], F32, kind="ExternalInput")
    dwo = nc.dram_tensor("woT", [D, D], F32, kind="ExternalInput")
    dlut = nc.dram_tensor("lut", [C.LUT], F32, kind="ExternalInput")
    dgid = nc.dram_tensor("gidxT", [D, C.INTER], I32, kind="ExternalInput")
    duid = nc.dram_tensor("uidxT", [D, C.INTER], I32, kind="ExternalInput")
    ddid = nc.dram_tensor("didxT", [C.INTER, D], I32, kind="ExternalInput")
    dcs = nc.dram_tensor("csT", [128, T], F32, kind="ExternalInput")
    dsn = nc.dram_tensor("snT", [128, T], F32, kind="ExternalInput")
    dperm = nc.dram_tensor("permT", [128, 128], F32, kind="ExternalInput")
    dmask = nc.dram_tensor("maskc", [128, 2048], F32, kind="ExternalInput")
    dy = nc.dram_tensor("y", [C.BPC, T, D], F32, kind="ExternalOutput")

    wq_bf = nc.dram_tensor("wq_bf", [H, 128, D], BF16)
    wk_bf = nc.dram_tensor("wk_bf", [H, 128, D], BF16)
    wv_bf = nc.dram_tensor("wv_bf", [H // 2, 128, 2 * D], BF16)
    wo_bf = nc.dram_tensor("wo_bf", [D, D], BF16)
    ln1b = nc.dram_tensor("ln1b", [T, D], BF16)
    ln3b = nc.dram_tensor("ln3b", [C.BPC, T, D], BF16)
    x1s = nc.dram_tensor("x1s", [C.BPC, T, D], F32)
    wg_f = nc.dram_tensor("wg_f", [D, C.INTER], BF16)
    wu_f = nc.dram_tensor("wu_f", [D, C.INTER], BF16)
    wd_f = nc.dram_tensor("wd_f", [C.INTER, D], BF16)

    with ExitStack() as stack:
        e = stack.enter_context
        ln1T = e(nc.sbuf_tensor("ln1T", [128, IC * T], BF16))
        headsT = e(nc.sbuf_tensor("headsT", [128, H * T], BF16))
        vg = e(nc.sbuf_tensor("vg", [128, TT * C.HG * 128], BF16))
        pt = e(nc.sbuf_tensor("pt", [128, 2048], BF16))
        qraw = e(nc.sbuf_tensor("qraw", [128, T], BF16))
        kraw = e(nc.sbuf_tensor("kraw", [128, T], BF16))
        wvs = e(nc.sbuf_tensor("wvs", [128, IC * C.HG * 128], BF16))
        wqk = e(nc.sbuf_tensor("wqk", [128, IC * 128], BF16))
        qr = e(nc.sbuf_tensor("qr", [128, 2 * T], BF16))
        kr = e(nc.sbuf_tensor("kr", [128, 2 * T], BF16))
        tmp = e(nc.sbuf_tensor("tmp", [128, T], BF16))
        csS = e(nc.sbuf_tensor("csS", [128, T], BF16))
        snS = e(nc.sbuf_tensor("snS", [128, T], BF16))
        permS = e(nc.sbuf_tensor("permS", [128, 128], BF16))
        maskS = e(nc.sbuf_tensor("maskS", [128, 2048], BF16))
        onesS = e(nc.sbuf_tensor("onesS", [128, 2], BF16))
        ones1 = e(nc.sbuf_tensor("ones1", [1, 128], BF16))
        rows = e(nc.sbuf_tensor("rows", [128, 128], F32))
        srow = e(nc.sbuf_tensor("srow", [1, 512], F32))
        srbf = e(nc.sbuf_tensor("srbf", [1, 1024], BF16))
        lutS = e(nc.sbuf_tensor("lutS", [128, C.LUT], F32))
        bixb = e(nc.sbuf_tensor("bixb", [128, 2048], BF16))
        bacc = e(nc.sbuf_tensor("bacc", [128, 1024], BF16))
        bsel = e(nc.sbuf_tensor("bsel", [128, 1024], BF16))
        psum = e(nc.psum_tensor("psum", [128, 4096], F32))

        em = Em(nc, stack)
        qld = DmaQ(em, "sp", "qld", 8)
        qst = DmaQ(em, "act", "qst", 8)
        qcast = DmaQ(em, "pool", "qcast", 8)

        PS = "psum"

        def bank(i, w=512):
            return psum[:, 512 * i:512 * i + w]

        def bankiv(i, nb=1):
            return ival(PS, 2048 * i, 2048 * (i + nb))

        xa_f32 = qr.ap().bitcast(F32)[:, 0:D]
        xb_f32 = kr.ap().bitcast(F32)[:, 0:D]
        bcb = tmp.ap().bitcast(F32)[:, 0:512]

        qcast.issue(csS[:, :], dcs.ap(), writes=[ival("csS")])
        qcast.issue(snS[:, :], dsn.ap(), writes=[ival("snS")])
        qcast.issue(permS[:, :], dperm.ap(), writes=[ival("permS")])
        qcast.issue(maskS[:, :], dmask.ap(), writes=[ival("maskS")])
        qcast.issue(lutS[:, :], dlut.ap().unsqueeze(0).partition_broadcast(128),
                    writes=[ival("lutS")])
        em.op("dve", lambda h: h.memset(onesS[:, :], 1.0), writes=[ival("onesS")])
        em.op("dve", lambda h: h.memset(ones1[:, :], 1.0), writes=[ival("ones1")])
        qcast.issue(wo_bf.ap(), dwo.ap(), writes=[ival("wo_bf")])
        qcast.issue(wq_bf.ap(), dwq.ap(), writes=[ival("wq_bf")])
        qcast.issue(wk_bf.ap(), dwk.ap(), writes=[ival("wk_bf")])
        qcast.issue(wv_bf.ap(), dwv.ap(), writes=[ival("wv_bf")])

        def strip_chunks():
            out = []
            c0 = 0
            while c0 < C.INTER:
                w = min(1024, C.INTER - c0)
                for (didx_, dcon) in ((dgid, wg_f), (duid, wu_f)):
                    r0 = 0
                    while r0 < D:
                        p = min(128, D - r0)
                        out.append((didx_, dcon, r0, p, c0, w))
                        r0 += p
                c0 += w
            r0 = 0
            while r0 < C.INTER:
                p = min(128, C.INTER - r0)
                c0 = 0
                while c0 < D:
                    w = min(1024, D - c0)
                    out.append((ddid, wd_f, r0, p, c0, w))
                    c0 += w
                r0 += p
            return out

        def chore_gen():
            ci_par = 0
            for (didx_, dcon, r0, p, c0, w) in strip_chunks():
                par = ci_par
                ci_par ^= 1
                bx = bixb[:, 1024 * par:1024 * par + 1024]
                iv_b = ival("bixb", 1024 * par * BF, (1024 * par + 1024) * BF)
                iv_a, iv_s = ival("bacc"), ival("bsel")
                qcast.issue(bx[:p, :w], didx_.ap()[r0:r0 + p, c0:c0 + w],
                            writes=[iv_b])
                yield 0.3
                for j0 in range(0, C.LUT, 4):
                    for j in range(j0, j0 + 4):
                        dst, iv_d = (bacc, iv_a) if j == 0 else (bsel, iv_s)
                        em.op("dve", lambda h, p=p, w=w, j=j, dst=dst, bx=bx:
                              h.tensor_scalar(dst[:p, :w], bx[:p, :w],
                                              float(j), lutS[:p, j:j + 1],
                                              ALU.is_equal, ALU.mult),
                              reads=[iv_b, ival("lutS")], writes=[iv_d])
                        if j > 0:
                            em.op("dve", lambda h, p=p, w=w:
                                  h.tensor_tensor(bacc[:p, :w], bacc[:p, :w],
                                                  bsel[:p, :w], ALU.add),
                                  reads=[iv_s, iv_a], writes=[iv_a])
                    yield 5.0
                if dcon is wd_f:
                    ivw = (dcon.name, r0 * D * BF, (r0 + p) * D * BF)
                else:
                    ivw = (dcon.name, c0 * BF, (c0 + w) * BF)
                qcast.issue(dcon.ap()[r0:r0 + p, c0:c0 + w], bacc[:p, :w],
                            reads=[iv_a], writes=[ivw])
                if dcon is wd_f:
                    if c0 + w >= D:
                        chore_state["wd_rows"] = r0 + p
                elif dcon is wu_f and r0 + p >= D:
                    chore_state["wgu_cols"] = c0 + w
                chore_state["chunks"] += 1
                yield 0.3
        chore_state = {"done": False, "chunks": 0, "wgu_cols": 0, "wd_rows": 0}
        chore_state["gen"] = chore_gen()

        def do_chore(budget_us):
            while budget_us > 0 and not chore_state["done"]:
                try:
                    budget_us -= next(chore_state["gen"])
                except StopIteration:
                    chore_state["done"] = True

        def ensure_wgu_cols(n):
            while chore_state["wgu_cols"] < n and not chore_state["done"]:
                try:
                    next(chore_state["gen"])
                except StopIteration:
                    chore_state["done"] = True

        def ensure_wd_rows(n):
            while chore_state["wd_rows"] < n and not chore_state["done"]:
                try:
                    next(chore_state["gen"])
                except StopIteration:
                    chore_state["done"] = True

        def finish_chores():
            do_chore(1e9)




        slot_state = {"i": 0}

        def ln_stats(chunks):
            sl = 32 * (slot_state["i"] % 3)
            slot_state["i"] += 1
            nch = len(chunks)
            stat_iv = ival("rows", sl * 4, (sl + 8 + 6 * nch) * 4)
            sc = sl + 8
            for i, (ap, iv) in enumerate(chunks):
                em.op("dve", lambda h, ap=ap, i=i, sc=sc:
                      h.bn_stats(rows[:, sc + 6 * i: sc + 6 * i + 6], ap),
                      reads=[iv], writes=[stat_iv])
            em.op("dve", lambda h, sl=sl, sc=sc, nch=nch:
                  h.bn_aggr(rows[:, sl:sl + 2], rows[:, sc:sc + 6 * nch]),
                  reads=[stat_iv], writes=[stat_iv])
            em.op("dve", lambda h, sl=sl:
                  h.tensor_scalar(rows[:, sl + 2:sl + 3], rows[:, sl + 1:sl + 2],
                                  C.eps, None, ALU.add),
                  reads=[stat_iv], writes=[stat_iv])
            em.op("act", lambda h, sl=sl:
                  h.activation(rows[:, sl + 3:sl + 4], rows[:, sl + 2:sl + 3],
                               AF.Ln),
                  reads=[stat_iv], writes=[stat_iv])
            em.op("act", lambda h, sl=sl:
                  h.activation(rows[:, sl + 4:sl + 5], rows[:, sl + 3:sl + 4],
                               AF.Exp, scale=-0.5),
                  reads=[stat_iv], writes=[stat_iv])
            em.op("dve", lambda h, sl=sl:
                  h.tensor_tensor(rows[:, sl + 5:sl + 6], rows[:, sl:sl + 1],
                                  rows[:, sl + 4:sl + 5], ALU.mult),
                  reads=[stat_iv], writes=[stat_iv])
            em.op("dve", lambda h, sl=sl:
                  h.tensor_scalar(rows[:, sl + 5:sl + 6], rows[:, sl + 5:sl + 6],
                                  -1.0, None, ALU.mult),
                  reads=[stat_iv], writes=[stat_iv])
            return (rows[:, sl + 4:sl + 5], rows[:, sl + 5:sl + 6], stat_iv)

        def sbuf_chunks(ap_maker, name, base_bytes, n_elems, esize):
            out = []
            c0 = 0
            while c0 < n_elems:
                w = min(512, n_elems - c0)
                out.append((ap_maker(c0, w),
                            ival(name, base_bytes + c0 * esize,
                                 base_bytes + (c0 + w) * esize)))
                c0 += w
            return out

        def psum_chunks(bank0, n_elems):
            out = []
            c0 = 0
            while c0 < n_elems:
                w = min(512, n_elems - c0)
                out.append((psum[:, 512 * bank0 + c0: 512 * bank0 + c0 + w],
                            ival(PS, 2048 * bank0 + 4 * c0,
                                 2048 * bank0 + 4 * (c0 + w))))
                c0 += w
            return out

        def phase_A(b):
            for tt in range(TT):
                xap, xnm = (xa_f32, "qr") if tt % 2 == 0 else (xb_f32, "kr")
                iv_x = ival(xnm, 0, D * 4)
                qld.issue(xap, dx.ap()[b, 128 * tt:128 * tt + 128, :],
                          writes=[iv_x])
                ch = sbuf_chunks(lambda c0, w, xap=xap: xap[:, c0:c0 + w],
                                 xnm, 0, D, 4)
                rstd, nbias, stiv = ln_stats(ch)
                iv_l = ival("tmp", 0, D * BF)
                em.op("act", lambda h, xap=xap, r=rstd, nb=nbias:
                      h.activation(tmp[:, 0:D], xap, AF.Identity, bias=nb,
                                   scale=r),
                      reads=[iv_x, stiv], writes=[iv_l])
                qst.issue(ln1b.ap()[128 * tt:128 * tt + 128, :], tmp[:, 0:D],
                          reads=[iv_l],
                          writes=[ival("ln1b", 128 * tt * D * BF,
                                        (128 * tt + 128) * D * BF)])
                do_chore(12.0)

        def load_ln1T():
            for ci in range(IC):
                qld.issue(ln1T[:, T * ci:T * ci + T],
                          ln1b.ap()[:, 128 * ci:128 * ci + 128],
                          reads=[ival("ln1b")],
                          writes=[ival("ln1T", T * BF * ci, T * BF * (ci + 1))],
                          transpose=True)

        def produce_qk(b, h, which):
            dw = wq_bf if which == "q" else wk_bf
            iv_w = ival("wqk")
            qld.issue(wqk[:, :], dw.ap()[h],
                      reads=[ival(dw.name, h * 128 * D * BF,
                                  (h + 1) * 128 * D * BF)],
                      writes=[iv_w])
            raw = qraw if which == "q" else kraw
            rawn = "qraw" if which == "q" else "kraw"
            for tc in range(T // 512):
                pb = tc % 2
                mms, rds = [], [iv_w]
                for ci in range(IC):
                    mms.append(lambda h2, ci=ci, tc=tc, pb=pb:
                               h2.matmul(bank(pb), wqk[:, 128 * ci:128 * ci + 128],
                                         ln1T[:, T * ci + 512 * tc:
                                              T * ci + 512 * tc + 512],
                                         start=(ci == 0), stop=(ci == IC - 1)))
                    rds.append(ival("ln1T", BF * (T * ci + 512 * tc),
                                    BF * (T * ci + 512 * tc + 512)))
                em.op("pe", mms, reads=rds, writes=[bankiv(pb)])
                em.op("act", lambda h2, tc=tc, pb=pb, raw=raw:
                      h2.activation(raw[:, 512 * tc: 512 * tc + 512],
                                    bank(pb), AF.Identity),
                      reads=[bankiv(pb)],
                      writes=[ival(rawn, BF * 512 * tc, BF * (512 * tc + 512))])
            return rawn

        def rope(rawn, dst, dst_name, par):
            src = qraw if rawn == "qraw" else kraw
            stg = kraw if rawn == "qraw" else qraw
            stgn = "kraw" if rawn == "qraw" else "qraw"
            iv_src = ival(rawn)
            iv_stg = ival(stgn)
            dstv = dst[:, T * par:T * par + T]
            iv_dst = ival(dst_name, T * BF * par, T * BF * (par + 1))
            iv_t = ival("tmp", 0, T * BF)
            nh = T // 1024
            em.op("dve", lambda h: h.tensor_tensor(tmp[:, 0:T], src[:, 0:T],
                                                   csS[:, :], ALU.mult),
                  reads=[iv_src, ival("csS")], writes=[iv_t])
            for hb in range(nh):
                mms = []
                for tc in range(2):
                    mms.append(lambda h, hb=hb, tc=tc:
                               h.matmul(bank(6 + tc),
                                        permS[:, :],
                                        src[:, 1024 * hb + 512 * tc:
                                            1024 * hb + 512 * tc + 512],
                                        start=True, stop=True))
                em.op("pe", mms, reads=[iv_src, ival("permS")],
                      writes=[bankiv(6, 2)])
                em.op("act", lambda h, hb=hb, stg=stg:
                      h.activation(stg[:, 1024 * hb:1024 * hb + 1024],
                                   psum[:, 3072:4096], AF.Identity),
                      reads=[bankiv(6, 2)], writes=[iv_stg])
                em.op("dve", lambda h, hb=hb, dstv=dstv, stg=stg:
                      h.tensor_tensor(dstv[:, 1024 * hb:1024 * hb + 1024],
                                      stg[:, 1024 * hb:1024 * hb + 1024],
                                      snS[:, 1024 * hb:1024 * hb + 1024],
                                      ALU.mult),
                      reads=[iv_stg, ival("snS")], writes=[iv_dst])
            em.op("dve", lambda h, dstv=dstv:
                  h.tensor_tensor(dstv, dstv, tmp[:, 0:T], ALU.add),
                  reads=[iv_t, iv_dst], writes=[iv_dst])

        def produce_group_v(b, g):
            W = C.HG * 128
            iv_w = ival("wvs")
            qld.issue(wvs[:, :IC * W], wv_bf.ap()[g],
                      reads=[ival("wv_bf", g * 128 * 2 * D * BF,
                                  (g + 1) * 128 * 2 * D * BF)],
                      writes=[iv_w])
            for ct in range(TT):
                pb = 2 + (ct % 2)
                mms, rds = [], [iv_w]
                for ci in range(IC):
                    mms.append(lambda h2, ci=ci, ct=ct, pb=pb, W=W:
                               h2.matmul(bank(pb, W),
                                         ln1T[:, T * ci + 128 * ct:
                                              T * ci + 128 * ct + 128],
                                         wvs[:, W * ci: W * ci + W],
                                         start=(ci == 0), stop=(ci == IC - 1)))
                    rds.append(ival("ln1T", BF * (T * ci + 128 * ct),
                                    BF * (T * ci + 128 * ct + 128)))
                em.op("pe", mms, reads=rds, writes=[bankiv(pb)])
                em.op("act", lambda h2, ct=ct, pb=pb, W=W:
                      h2.activation(vg[:, W * ct: W * ct + W], bank(pb, W),
                                    AF.Identity),
                      reads=[bankiv(pb)],
                      writes=[ival("vg", BF * W * ct, BF * W * (ct + 1))])

        def attend(b, h, par):
            hh = h % C.HG
            W = C.HG * 128
            qrh = qr[:, T * par: T * par + T]
            krh = kr[:, T * par: T * par + T]
            iv_qr = ival("qr", T * BF * par, T * BF * (par + 1))
            iv_kr = ival("kr", T * BF * par, T * BF * (par + 1))
            for qt in range(QT):
                nch = 4 * qt + 4
                ngrp = nch // 2
                def emit_st(g):
                    pb = 2 * (g % 2)
                    mms = []
                    for cc in range(2):
                        c = 2 * g + cc
                        mms.append(lambda h2, c=c, cc=cc, qt=qt, pb=pb:
                                   h2.matmul(bank(pb + cc),
                                             krh[:, 128 * c:128 * c + 128],
                                             qrh[:, 512 * qt:512 * qt + 512],
                                             start=True, stop=True))
                    em.op("pe", mms, reads=[iv_qr, iv_kr],
                          writes=[bankiv(pb, 2)])

                def emit_exp(g):
                    pb = 2 * (g % 2)
                    pc0 = 1024 * (g % 2)
                    iv_p = ival("pt", pc0 * BF, (pc0 + 1024) * BF)
                    em.op("act", lambda h2, pc0=pc0, pb=pb:
                          h2.activation(pt[:, pc0:pc0 + 1024], bank(pb, 1024),
                                        AF.Exp, scale=C.scale),
                          reads=[bankiv(pb, 2)], writes=[iv_p])
                    if g >= ngrp - 2:
                        mc0 = 1024 * (g - (ngrp - 2))
                        em.op("dve", lambda h2, pc0=pc0, mc0=mc0:
                              h2.tensor_tensor(pt[:, pc0:pc0 + 1024],
                                               pt[:, pc0:pc0 + 1024],
                                               maskS[:, mc0:mc0 + 1024],
                                               ALU.mult),
                              reads=[ival("maskS"), iv_p], writes=[iv_p])
                    return iv_p

                def emit_pv(g, iv_p):
                    pc0 = 1024 * (g % 2)
                    mms = []
                    for cc in range(2):
                        c = 2 * g + cc
                        mms.append(lambda h2, c=c, cc=cc, pc0=pc0, hh=hh,
                                   W=W, nch=nch:
                                   h2.matmul(bank(4),
                                             vg[:, W * c + 128 * hh:
                                                W * c + 128 * hh + 128],
                                             pt[:, pc0 + 512 * cc:
                                                pc0 + 512 * cc + 512],
                                             start=(c == 0),
                                             stop=(c == nch - 1)))
                    em.op("pe", mms, reads=[ival("vg"), iv_p],
                          writes=[bankiv(4)])
                    mms = []
                    for cc in range(2):
                        c = 2 * g + cc
                        mms.append(lambda h2, c=c, cc=cc, pc0=pc0, nch=nch:
                                   h2.matmul(psum[0:1, 2560:3072],
                                             onesS[:, 0:1],
                                             pt[:, pc0 + 512 * cc:
                                                pc0 + 512 * cc + 512],
                                             start=(c == 0),
                                             stop=(c == nch - 1)))
                    em.op("pe", mms, reads=[ival("onesS"), iv_p],
                          writes=[bankiv(5)])

                emit_st(0)
                iv_prev = emit_exp(0)
                for g in range(1, ngrp):
                    emit_st(g)
                    iv_p = emit_exp(g)
                    emit_pv(g - 1, iv_prev)
                    iv_prev = iv_p
                emit_pv(ngrp - 1, iv_prev)
                do_chore(3.0 * (qt + 1))
                iv_sr, iv_sb = ival("srow"), ival("srbf")
                em.op("act", lambda h2:
                      h2.activation(srow[0:1, 0:512], psum[0:1, 2560:3072],
                                    AF.Ln),
                      reads=[bankiv(5)], writes=[iv_sr])
                em.op("act", lambda h2:
                      h2.activation(srow[0:1, 0:512], srow[0:1, 0:512],
                                    AF.Exp, scale=-1.0),
                      reads=[iv_sr], writes=[iv_sr])
                em.op("dve", lambda h2:
                      h2.tensor_copy(srbf[0:1, 0:512], srow[0:1, 0:512]),
                      reads=[iv_sr], writes=[iv_sb])
                em.op("dve", lambda h2:
                      h2.tensor_tensor(srow[0:1, 0:512], srow[0:1, 0:512],
                                       srbf[0:1, 0:512], ALU.subtract),
                      reads=[iv_sr, iv_sb], writes=[iv_sr])
                em.op("dve", lambda h2:
                      h2.tensor_copy(srbf[0:1, 512:1024], srow[0:1, 0:512]),
                      reads=[iv_sr], writes=[iv_sb])
                em.op("pe", [
                    lambda h2: h2.matmul(bank(0), ones1[0:1, :],
                                         srbf[0:1, 0:512], start=True,
                                         stop=False),
                    lambda h2: h2.matmul(bank(0), ones1[0:1, :],
                                         srbf[0:1, 512:1024], start=False,
                                         stop=True)],
                    reads=[iv_sb, ival("ones1")], writes=[bankiv(0)])
                em.op("act", lambda h2:
                      h2.activation(bcb, bank(0), AF.Identity),
                      reads=[bankiv(0)], writes=[ival("tmp", 0, 2048)])
                hc0 = BF * (T * h + 512 * qt)
                em.op("dve", lambda h2, h=h, qt=qt:
                      h2.tensor_tensor(headsT[:, T * h + 512 * qt:
                                              T * h + 512 * qt + 512],
                                       bank(4), bcb, ALU.mult),
                      reads=[bankiv(4), ival("tmp", 0, 2048)],
                      writes=[ival("headsT", hc0, hc0 + 512 * BF)])

        def phase_B(b):
            load_ln1T()
            produce_group_v(b, 0)
            rope(produce_qk(b, 0, "q"), qr, "qr", 0)
            rope(produce_qk(b, 0, "k"), kr, "kr", 0)
            for h in range(H):
                if h + 1 < H:
                    par1 = (h + 1) % 2
                    rope(produce_qk(b, h + 1, "q"), qr, "qr", par1)
                    rope(produce_qk(b, h + 1, "k"), kr, "kr", par1)
                attend(b, h, h % 2)
                if h + 1 < H and (h + 1) % C.HG == 0:
                    produce_group_v(b, (h + 1) // C.HG)
                do_chore(25.0)

        def phase_C(b):
            for ci in range(IC):
                qld.issue(ln1T[:, T * ci: T * ci + D],
                          wo_bf.ap()[128 * ci:128 * ci + 128, :],
                          reads=[ival("wo_bf")],
                          writes=[ival("ln1T", BF * T * ci,
                                       BF * (T * ci + D))])
            nhalf = (D + 1023) // 1024
            rot = {"g": 0}
            for tt in range(TT):
                groups = []
                stat_chunks = []
                for hf in range(nhalf):
                    g = rot["g"]
                    rot["g"] = (g + 1) % 3
                    groups.append(g)
                    hw = min(1024, D - 1024 * hf)
                    noc = (hw + 511) // 512
                    mms, rds = [], []
                    for oc2 in range(noc):
                        w = min(512, hw - 512 * oc2)
                        oc = 2 * hf + oc2
                        for jc in range(IC):
                            mms.append(
                                lambda h2, oc=oc, jc=jc, w=w, tt=tt, g=g,
                                oc2=oc2:
                                h2.matmul(bank(2 * g + oc2, w),
                                          headsT[:, T * jc + 128 * tt:
                                                 T * jc + 128 * tt + 128],
                                          ln1T[:, T * jc + 512 * oc:
                                               T * jc + 512 * oc + w],
                                          start=(jc == 0),
                                          stop=(jc == IC - 1)))
                            rds.append(ival("headsT", BF * (T * jc + 128 * tt),
                                            BF * (T * jc + 128 * tt + 128)))
                            rds.append(ival("ln1T", BF * (T * jc + 512 * oc),
                                            BF * (T * jc + 512 * oc + w)))
                    em.op("pe", mms, reads=rds, writes=[bankiv(2 * g, noc)])
                    for oc2 in range(noc):
                        w = min(512, hw - 512 * oc2)
                        stat_chunks.append(
                            (psum[:, 512 * (2 * g + oc2):
                                  512 * (2 * g + oc2) + w],
                             ival(PS, 2048 * (2 * g + oc2),
                                  2048 * (2 * g + oc2) + 4 * w)))
                rstd, nbias, stiv = ln_stats(stat_chunks)
                iv_x = ival("qr", 0, D * 4)
                qld.issue(xa_f32, dx.ap()[b, 128 * tt:128 * tt + 128, :],
                          writes=[iv_x])
                iv_x1 = ival("kr", 0, D * 4)
                nrm = []
                rdn = [stiv]
                for hf in range(nhalf):
                    g = groups[hf]
                    hw = min(1024, D - 1024 * hf)
                    noc = (hw + 511) // 512
                    em.op("act", lambda h2, r=rstd, nb=nbias, g=g, hf=hf,
                          hw=hw:
                          h2.activation(xb_f32[:, 1024 * hf:1024 * hf + hw],
                                        psum[:, 512 * 2 * g:512 * 2 * g + hw],
                                        AF.Identity, bias=nb, scale=r),
                          reads=[bankiv(2 * g, noc), stiv], writes=[iv_x1])
                em.op("dve", lambda h2:
                      h2.tensor_tensor(xb_f32, xb_f32, xa_f32, ALU.add),
                      reads=[iv_x, iv_x1], writes=[iv_x1])
                qst.issue(x1s.ap()[b, 128 * tt:128 * tt + 128, :], xb_f32,
                          reads=[iv_x1], writes=[ival("x1s")])
                rstd3, nbias3, stiv3 = ln_stats(
                    sbuf_chunks(lambda c0, w: xb_f32[:, c0:c0 + w], "kr", 0,
                                D, 4))
                iv_l3 = ival("tmp", 0, D * BF)
                em.op("act", lambda h2, r=rstd3, nb=nbias3:
                      h2.activation(tmp[:, 0:D], xb_f32, AF.Identity,
                                    bias=nb, scale=r),
                      reads=[iv_x1, stiv3], writes=[iv_l3])
                qst.issue(ln3b.ap()[b, 128 * tt:128 * tt + 128, :], tmp[:, 0:D],
                          reads=[iv_l3], writes=[ival("ln3b")])
                do_chore(14.0)

        TBT, TBW = C.TBT, C.TB

        free_regions = [("ln1T", 0, IC * T), ("headsT", 0, H * T)]
        cur = {"ri": 0, "off": 0}

        def take(ncols):
            while True:
                name, base, cap = free_regions[cur["ri"]]
                if cur["off"] + ncols <= cap:
                    out = (name, base + cur["off"])
                    cur["off"] += ncols
                    return out
                cur["ri"] += 1
                cur["off"] = 0
                assert cur["ri"] < len(free_regions), "phase-D SBUF overflow"

        tens = {"ln1T": ln1T, "headsT": headsT, "pt": pt, "vg": vg,
                "wvs": wvs, "maskS": maskS, "csS": csS, "snS": snS}
        hT_loc = [take(TBW) for _ in range(C.OT)]
        ln3T_loc = [take(TBW) for _ in range(IC)]
        free_regions2 = [("pt", 0, 2048), ("vg", TBW, TT * C.HG * 128),
                         ("wvs", IC * 128, IC * C.HG * 128),
                         ("maskS", 0, 2048), ("csS", 0, T), ("snS", 0, T),
                         free_regions[cur["ri"]][:1] + (
                             free_regions[cur["ri"]][1] + cur["off"],
                             free_regions[cur["ri"]][2])]
        cur2 = {"ri": 0, "off": 0}

        def take2(ncols):
            while True:
                name, base, cap = free_regions2[cur2["ri"]]
                if base + cur2["off"] + ncols <= cap:
                    out = (name, base + cur2["off"])
                    cur2["off"] += ncols
                    return out
                cur2["ri"] += 1
                cur2["off"] = 0
                assert cur2["ri"] < len(free_regions2), "phase-D SBUF overflow2"

        dn_loc = [take2(D) for _ in range(TBT)]

        def view(loc, ncols):
            name, c0 = loc
            t = tens[name]
            return t[:, c0:c0 + ncols], ival(name, BF * c0, BF * (c0 + ncols))

        def phase_D(b, tb):
            t0 = tb * TBW
            for ci in range(IC):
                v, iv = view(ln3T_loc[ci], TBW)
                qld.issue(v, ln3b.ap()[b, t0:t0 + TBW, 128 * ci:128 * ci + 128],
                          reads=[ival("ln3b")], writes=[iv], transpose=True)
            nxc = TBW // 512
            for ot in range(C.OT):
                if b == 0 and tb == 0:
                    ensure_wgu_cols(min(C.INTER, 128 * ot + 128))
                iv_wg = ival("wqk")
                qld.issue(wqk[:, :].rearrange("p (c o) -> p c o", o=128),
                          wg_f.ap()[:, 128 * ot:128 * ot + 128]
                          .rearrange("(c p) o -> p c o", p=128),
                          reads=[ival("wg_f", 128 * ot * BF,
                                       (128 * ot + 128) * BF)],
                          writes=[iv_wg])
                iv_wu = ival("wvs", 0, IC * 128 * BF)
                qld.issue(wvs[:, 0:IC * 128].rearrange("p (c o) -> p c o", o=128),
                          wu_f.ap()[:, 128 * ot:128 * ot + 128]
                          .rearrange("(c p) o -> p c o", p=128),
                          reads=[ival("wu_f", 128 * ot * BF,
                                       (128 * ot + 128) * BF)],
                          writes=[iv_wu])
                for (iv_w, wt, pb) in ((iv_wg, wqk, 0), (iv_wu, wvs, 2)):
                    mms, rds = [], [iv_w]
                    for xc in range(nxc):
                        for ci in range(IC):
                            v3, iv3 = view(ln3T_loc[ci], TBW)
                            mms.append(
                                lambda h2, ci=ci, xc=xc, wt=wt, pb=pb, v3=v3:
                                h2.matmul(bank(pb + xc),
                                          wt[:, 128 * ci:128 * ci + 128],
                                          v3[:, 512 * xc:512 * xc + 512],
                                          start=(ci == 0), stop=(ci == IC - 1)))
                            rds.append(iv3)
                    em.op("pe", mms, reads=rds, writes=[bankiv(pb, nxc)])
                hv, hiv = view(hT_loc[ot], TBW)
                iv_sg = ival("vg", 0, TBW * BF)
                em.op("act", lambda h2:
                      h2.activation(vg[:, 0:TBW], bank(0, TBW), AF.Silu),
                      reads=[bankiv(0, nxc)], writes=[iv_sg])
                em.op("dve", lambda h2, hv=hv:
                      h2.tensor_tensor(hv, vg[:, 0:TBW], bank(2, TBW), ALU.mult),
                      reads=[iv_sg, bankiv(2, nxc)], writes=[hiv])
                do_chore(12.0)
            for dc in range(C.DC):
                w = min(512, D - 512 * dc)
                for oc in range(C.OT):
                    if b == 0 and tb == 0:
                        ensure_wd_rows(min(C.INTER, 128 * oc + 128))
                    par = oc % min(4, T // 512)
                    iv_wd = ival("tmp", 512 * par * BF, (512 * par + w) * BF)
                    qld.issue(tmp[:, 512 * par:512 * par + w],
                              wd_f.ap()[128 * oc:128 * oc + 128,
                                        512 * dc:512 * dc + w],
                              reads=[ival("wd_f", 128 * oc * D * BF,
                                           (128 * oc + 128) * D * BF)],
                              writes=[iv_wd])
                    do_chore(3.0)
                    mms, rds = [], [iv_wd]
                    for t8 in range(TBT):
                        hv, hiv = view(hT_loc[oc], TBW)
                        mms.append(lambda h2, t8=t8, par=par, w=w, hv=hv,
                                   oc=oc:
                                   h2.matmul(bank(t8, w),
                                             hv[:, 128 * t8:128 * t8 + 128],
                                             tmp[:, 512 * par:512 * par + w],
                                             start=(oc == 0),
                                             stop=(oc == C.OT - 1)))
                        rds.append(hiv)
                    em.op("pe", mms, reads=rds,
                          writes=[bankiv(t8) for t8 in range(TBT)])
                for t8 in range(TBT):
                    dv, div = view(dn_loc[t8], D)
                    em.op("act", lambda h2, t8=t8, dc=dc, w=w, dv=dv:
                          h2.activation(dv[:, 512 * dc:512 * dc + w],
                                        bank(t8, w), AF.Identity),
                          reads=[bankiv(t8)],
                          writes=[(div[0], div[1] + BF * 512 * dc,
                                   div[1] + BF * (512 * dc + w))])
            for t8 in range(TBT):
                dv, div = view(dn_loc[t8], D)
                rstd4, nbias4, stiv4 = ln_stats(
                    sbuf_chunks(lambda c0, w2, dv=dv: dv[:, c0:c0 + w2],
                                div[0], div[1], D, BF))
                iv_x1 = ival("qr", 0, D * 4)
                qld.issue(xa_f32,
                          x1s.ap()[b, t0 + 128 * t8:t0 + 128 * t8 + 128, :],
                          reads=[ival("x1s")], writes=[iv_x1])
                iv_o = ival("kr", 0, D * 4)
                em.op("act", lambda h2, r=rstd4, nb=nbias4, dv=dv:
                      h2.activation(xb_f32, dv, AF.Identity, bias=nb, scale=r),
                      reads=[div, stiv4], writes=[iv_o])
                em.op("dve", lambda h2:
                      h2.tensor_tensor(xb_f32, xb_f32, xa_f32, ALU.add),
                      reads=[iv_x1, iv_o], writes=[iv_o])
                qst.issue(dy.ap()[b, t0 + 128 * t8:t0 + 128 * t8 + 128, :],
                          xb_f32, reads=[iv_o], writes=[ival("y")])

        for b in range(C.BPC):
            phase_A(b)
            phase_B(b)
            phase_C(b)
        for b in range(C.BPC):
            for tb in range(C.NTB):
                phase_D(b, tb)
        finish_chores()

        qst.final_waits("sp")
        qcast.final_waits("sp")
        qld.final_waits("sp")

        em.finalize(e(nc.Block()))
    return nc



def _host_consts(C: Cfg):
    j = np.arange(C.DH // 2, dtype=np.float64)
    theta = C.rope_base ** (-2.0 * j / C.DH)
    t = np.arange(C.T, dtype=np.float64)
    ang = np.outer(theta, t)
    cosT = np.concatenate([np.cos(ang), np.cos(ang)], axis=0).astype(np.float32)
    sinT = np.concatenate([-np.sin(ang), np.sin(ang)], axis=0).astype(np.float32)
    permT = np.zeros((128, 128), dtype=np.float32)
    for m in range(64):
        permT[m + 64, m] = 1.0
        permT[m, m + 64] = 1.0
    k = np.arange(128)[:, None]
    q = np.arange(512)[None, :]
    maskc = np.concatenate(
        [(q >= k + 128 * a).astype(np.float32) for a in range(4)], axis=1)
    return cosT, sinT, permT, maskc


def _deinterleave_cols(W_T, H, DH):
    perm = np.concatenate([h * DH + np.concatenate(
        [np.arange(0, DH, 2), np.arange(1, DH, 2)]) for h in range(H)])
    return np.ascontiguousarray(W_T[:, perm])


def _head_major(WT, W, D):
    IC = D // 128
    G = D // W
    return np.ascontiguousarray(
        WT.reshape(IC, 128, G, W).transpose(2, 1, 0, 3).reshape(G, 128, IC * W))


def make_in_maps(C, X, Wq, Wk, Wv, Wo, lut, gidx, uidx, didx):
    cosT, sinT, permT, maskc = _host_consts(C)
    wqT = _head_major(_deinterleave_cols(
        np.ascontiguousarray(Wq.T), C.H, C.DH), 128, C.D)
    wkT = _head_major(_deinterleave_cols(
        np.ascontiguousarray(Wk.T), C.H, C.DH), 128, C.D)
    wvT = _head_major(np.ascontiguousarray(Wv.T), 2 * 128, C.D)
    woT = np.ascontiguousarray(Wo.T)
    gT = np.ascontiguousarray(gidx.T)
    uT = np.ascontiguousarray(uidx.T)
    dT = np.ascontiguousarray(didx.T)
    in_maps = []
    for c in range(C.n_cores):
        in_maps.append({
            "x": np.ascontiguousarray(X[c * C.BPC:(c + 1) * C.BPC]),
            "wqT": wqT, "wkT": wkT, "wvT": wvT, "woT": woT,
            "lut": np.ascontiguousarray(lut),
            "gidxT": gT, "uidxT": uT, "didxT": dT,
            "csT": cosT, "snT": sinT, "permT": permT, "maskc": maskc,
        })
    return in_maps


def kernel(L0_attn_X, L0_attn_Wq, L0_attn_Wk, L0_attn_Wv, L0_attn_Wo,
           L0_ffn_lut, L0_ffn_gate_idx, L0_ffn_up_idx, L0_ffn_down_idx):
    C = Cfg()
    X = np.asarray(L0_attn_X, dtype=np.float32)
    in_maps = make_in_maps(C, X,
                           np.asarray(L0_attn_Wq, dtype=np.float32),
                           np.asarray(L0_attn_Wk, dtype=np.float32),
                           np.asarray(L0_attn_Wv, dtype=np.float32),
                           np.asarray(L0_attn_Wo, dtype=np.float32),
                           np.asarray(L0_ffn_lut, dtype=np.float32),
                           np.asarray(L0_ffn_gate_idx, dtype=np.int32),
                           np.asarray(L0_ffn_up_idx, dtype=np.int32),
                           np.asarray(L0_ffn_down_idx, dtype=np.int32))
    nc = build_program(C)
    res = run_bass_kernel_spmd(nc, in_maps, core_ids=list(range(C.n_cores)))
    return np.concatenate([r["y"] for r in res.results], axis=0)


# version 23
# speedup vs baseline: 1.0259x; 1.0259x over previous
import numpy as np
import ml_dtypes
from contextlib import ExitStack

import concourse.bass as bass
import concourse.mybir as mybir
from concourse.bass_utils import run_bass_kernel_spmd

F32 = mybir.dt.float32
BF16 = mybir.dt.bfloat16
I32 = mybir.dt.int32
AF = mybir.ActivationFunctionType
ALU = mybir.AluOpType

BF = 2


class Cfg:
    def __init__(self, n_cores=8, batch=16, T=2048, D=2048, H=16, INTER=5504,
                 LUT=16, tblock=1024, eps=1e-5, rope_base=10000.0):
        self.n_cores = n_cores
        self.batch = batch
        self.T = T
        self.D = D
        self.H = H
        self.DH = D // H
        assert self.DH == 128
        self.INTER = INTER
        self.LUT = LUT
        self.TB = tblock
        self.eps = eps
        self.rope_base = rope_base
        self.BPC = batch // n_cores
        self.IC = D // 128
        self.TT = T // 128
        self.QT = T // 512
        self.OT = INTER // 128
        self.DC = (D + 511) // 512
        assert D % 128 == 0 and INTER % 128 == 0 and T % 512 == 0
        assert D % n_cores == 0 and INTER % n_cores == 0 and T % tblock == 0
        self.gslice = D // n_cores
        self.dslice = INTER // n_cores
        self.scale = self.DH ** -0.5
        self.HG = 2
        assert H % self.HG == 0
        self.NTB = T // tblock
        self.TBT = tblock // 128
        assert self.TBT <= 8
        assert D * 4 <= 2 * T * BF


ENGINES = ("pe", "dve", "act", "pool", "sp")


class Em:

    def __init__(self, nc, stack):
        self.nc = nc
        self.stack = stack
        self.ops = {e: [] for e in ENGINES}
        self.cnt = {e: 0 for e in ENGINES}
        self.esem = {e: stack.enter_context(nc.semaphore(f"es_{e}")) for e in ENGINES}
        self.vc = {e: {} for e in ENGINES}
        self.self_waited = {e: 0 for e in ENGINES}
        self.snap = {e: {} for e in ENGINES}
        self.track = {}
        self.queues = {}
        self.n_waits = 0

    def _merge(self, eng, evvc):
        vc = self.vc[eng]
        for k, v in evvc.items():
            if vc.get(k, 0) < v:
                vc[k] = v

    def await_ev(self, eng, ev):
        axis, n = ev
        if axis == eng:
            if eng == "pe":
                return
            if self.self_waited[eng] < n:
                sem = self.esem[eng]
                self.ops[eng].append(lambda h, s=sem, v=n: h.wait_ge(s, v))
                self.self_waited[eng] = n
                self.n_waits += 1
            return
        if self.vc[eng].get(axis, 0) >= n:
            return
        if axis in ENGINES:
            sem, val = self.esem[axis], n
        else:
            qname, slot = axis
            q = self.queues[qname]
            sem, val = q.sems[slot], 16 * n
        self.ops[eng].append(lambda h, s=sem, v=val: h.wait_ge(s, v))
        self.n_waits += 1
        self._merge(eng, self.snap[axis][n])
        if self.vc[eng].get(axis, 0) < n:
            self.vc[eng][axis] = n

    def _t(self, name):
        if name not in self.track:
            self.track[name] = {"w": [], "r": []}
        return self.track[name]

    @staticmethod
    def _ov(lo, hi, rec):
        return rec[0] < hi and lo < rec[1]

    def _deps_for(self, reads, writes):
        deps = []
        for (name, lo, hi) in reads:
            t = self._t(name)
            deps += [rec[2] for rec in t["w"] if self._ov(lo, hi, rec)]
        for (name, lo, hi) in writes:
            t = self._t(name)
            deps += [rec[2] for rec in t["w"] if self._ov(lo, hi, rec)]
            deps += [rec[2] for rec in t["r"] if self._ov(lo, hi, rec)]
        return deps

    def _record(self, reads, writes, ev):
        for (name, lo, hi) in reads:
            self._t(name)["r"].append((lo, hi, ev))
        for (name, lo, hi) in writes:
            t = self._t(name)
            t["w"] = [rec for rec in t["w"] if not (rec[0] >= lo and rec[1] <= hi)]
            t["r"] = [rec for rec in t["r"] if not (rec[0] >= lo and rec[1] <= hi)]
            t["w"].append((lo, hi, ev))

    def _snapshot(self, eng, axis, n):
        vcs = dict(self.vc[eng])
        vcs[eng] = self.cnt[eng]
        vcs[axis] = n
        self.snap[axis][n] = vcs

    def op(self, eng, fns, reads=(), writes=()):
        if not isinstance(fns, (list, tuple)):
            fns = [fns]
        for ev in self._deps_for(reads, writes):
            self.await_ev(eng, ev)
        n = self.cnt[eng] + 1
        self.cnt[eng] = n
        sem = self.esem[eng]
        for f in fns[:-1]:
            self.ops[eng].append(f)
        f = fns[-1]
        self.ops[eng].append(lambda h, f=f, s=sem: f(h).then_inc(s, 1))
        self.vc[eng][eng] = n
        self._snapshot(eng, eng, n)
        ev = (eng, n)
        self._record(reads, writes, ev)
        return ev

    def finalize(self, block):
        hmap = {"pe": block.tensor, "dve": block.vector, "act": block.scalar,
                "pool": block.gpsimd, "sp": block.sync}
        for e in ENGINES:
            lst = self.ops[e]

            def body(h, lst=lst):
                for f in lst:
                    f(h)
            hmap[e](body)


class DmaQ:

    def __init__(self, em, eng, name, K=8):
        self.em = em
        self.eng = eng
        self.name = name
        self.K = K
        self.sems = [em.stack.enter_context(em.nc.semaphore(f"q_{name}{i}"))
                     for i in range(K)]
        self.n = 0
        em.queues[name] = self
        for i in range(K):
            em.snap[(name, i)] = {}

    def _emit(self, fn, reads, writes):
        em, eng = self.em, self.eng
        n = self.n + 1
        slot = (n - 1) % self.K
        occ = (n - 1) // self.K + 1
        axis = (self.name, slot)
        if occ > 1:
            em.await_ev(eng, (axis, occ - 1))
        for ev in em._deps_for(reads, writes):
            em.await_ev(eng, ev)
        sem = self.sems[slot]
        em.ops[eng].append(lambda h, fn=fn, s=sem: fn(h).then_inc(s, 16))
        self.n = n
        em._snapshot(eng, axis, occ)
        ev = (axis, occ)
        em._record(reads, writes, ev)
        return ev

    def issue(self, out_ap, in_ap, reads=(), writes=(), **kw):
        return self._emit(
            lambda h, o=out_ap, i=in_ap, kw=kw: h.dma_start(out=o, in_=i, **kw),
            reads, writes)

    def issue_fn(self, fn, reads=(), writes=()):
        return self._emit(fn, reads, writes)

    def final_waits(self, eng):
        em = self.em
        counts = [0] * self.K
        for j in range(1, self.n + 1):
            counts[(j - 1) % self.K] += 16
        for s in range(self.K):
            if counts[s]:
                em.ops[eng].append(
                    lambda h, sm=self.sems[s], v=counts[s]: h.wait_ge(sm, v))


def ival(name, lo=0, hi=1 << 40):
    return (name, lo, hi)


def build_program(cfg: Cfg):
    nc = bass.Bass("TRN2", target_bir_lowering=False)
    C = cfg
    T, D, H, IC, TT, QT = C.T, C.D, C.H, C.IC, C.TT, C.QT

    dx = nc.dram_tensor("x", [C.BPC, T, D], F32, kind="ExternalInput")
    dwq = nc.dram_tensor("wqT", [H, 128, D], F32, kind="ExternalInput")
    dwk = nc.dram_tensor("wkT", [H, 128, D], F32, kind="ExternalInput")
    dwv = nc.dram_tensor("wvT", [H // 2, 128, 2 * D], F32, kind="ExternalInput")
    dwo = nc.dram_tensor("woT", [D, D], F32, kind="ExternalInput")
    dlut = nc.dram_tensor("lut", [C.LUT], F32, kind="ExternalInput")
    dgid = nc.dram_tensor("gidxT", [D, C.INTER], I32, kind="ExternalInput")
    duid = nc.dram_tensor("uidxT", [D, C.INTER], I32, kind="ExternalInput")
    ddid = nc.dram_tensor("didxT", [C.INTER, D], I32, kind="ExternalInput")
    dcs = nc.dram_tensor("csT", [128, T], F32, kind="ExternalInput")
    dsn = nc.dram_tensor("snT", [128, T], F32, kind="ExternalInput")
    dperm = nc.dram_tensor("permT", [128, 128], F32, kind="ExternalInput")
    dmask = nc.dram_tensor("maskc", [128, 2048], F32, kind="ExternalInput")
    dy = nc.dram_tensor("y", [C.BPC, T, D], F32, kind="ExternalOutput")

    wq_bf = nc.dram_tensor("wq_bf", [H, 128, D], BF16)
    wk_bf = nc.dram_tensor("wk_bf", [H, 128, D], BF16)
    wv_bf = nc.dram_tensor("wv_bf", [H // 2, 128, 2 * D], BF16)
    wo_bf = nc.dram_tensor("wo_bf", [D, D], BF16)
    ln1b = nc.dram_tensor("ln1b", [T, D], BF16)
    srd = nc.dram_tensor("srd", [512], F32)
    ln3b = nc.dram_tensor("ln3b", [C.BPC, T, D], BF16)
    x1s = nc.dram_tensor("x1s", [C.BPC, T, D], F32)
    wg_f = nc.dram_tensor("wg_f", [D, C.INTER], BF16)
    wu_f = nc.dram_tensor("wu_f", [D, C.INTER], BF16)
    wd_f = nc.dram_tensor("wd_f", [C.INTER, D], BF16)

    with ExitStack() as stack:
        e = stack.enter_context
        ln1T = e(nc.sbuf_tensor("ln1T", [128, IC * T], BF16))
        headsT = e(nc.sbuf_tensor("headsT", [128, H * T], BF16))
        vg = e(nc.sbuf_tensor("vg", [128, TT * C.HG * 128], BF16))
        pt = e(nc.sbuf_tensor("pt", [128, 2048], BF16))
        qraw = e(nc.sbuf_tensor("qraw", [128, T], BF16))
        kraw = e(nc.sbuf_tensor("kraw", [128, T], BF16))
        wvs = e(nc.sbuf_tensor("wvs", [128, IC * C.HG * 128], BF16))
        wqk = e(nc.sbuf_tensor("wqk", [128, IC * 128], BF16))
        qr = e(nc.sbuf_tensor("qr", [128, 2 * T], BF16))
        kr = e(nc.sbuf_tensor("kr", [128, 2 * T], BF16))
        tmp = e(nc.sbuf_tensor("tmp", [128, T], BF16))
        csS = e(nc.sbuf_tensor("csS", [128, T], BF16))
        snS = e(nc.sbuf_tensor("snS", [128, T], BF16))
        permS = e(nc.sbuf_tensor("permS", [128, 128], BF16))
        maskS = e(nc.sbuf_tensor("maskS", [128, 2048], BF16))
        onesS = e(nc.sbuf_tensor("onesS", [128, 2], BF16))
        rows = e(nc.sbuf_tensor("rows", [128, 128], F32))
        srow = e(nc.sbuf_tensor("srow", [1, 512], F32))
        bcbuf = e(nc.sbuf_tensor("bcbuf", [128, 512], F32))
        lutS = e(nc.sbuf_tensor("lutS", [128, C.LUT], F32))
        bixb = e(nc.sbuf_tensor("bixb", [128, 1536], BF16))
        bacc = e(nc.sbuf_tensor("bacc", [128, 768], BF16))
        bsel = e(nc.sbuf_tensor("bsel", [128, 768], BF16))
        bac2 = e(nc.sbuf_tensor("bac2", [128, 768], BF16))
        bse2 = e(nc.sbuf_tensor("bse2", [128, 768], BF16))
        psum = e(nc.psum_tensor("psum", [128, 4096], F32))

        em = Em(nc, stack)
        qld = DmaQ(em, "sp", "qld", 8)
        qst = DmaQ(em, "act", "qst", 8)
        qcast = DmaQ(em, "pool", "qcast", 8)

        PS = "psum"

        def bank(i, w=512):
            return psum[:, 512 * i:512 * i + w]

        def bankiv(i, nb=1):
            return ival(PS, 2048 * i, 2048 * (i + nb))

        xa_f32 = qr.ap().bitcast(F32)[:, 0:D]
        xb_f32 = kr.ap().bitcast(F32)[:, 0:D]

        qcast.issue(csS[:, :], dcs.ap(), writes=[ival("csS")])
        qcast.issue(snS[:, :], dsn.ap(), writes=[ival("snS")])
        qcast.issue(permS[:, :], dperm.ap(), writes=[ival("permS")])
        qcast.issue(maskS[:, :], dmask.ap(), writes=[ival("maskS")])
        qcast.issue(lutS[:, :], dlut.ap().unsqueeze(0).partition_broadcast(128),
                    writes=[ival("lutS")])
        em.op("dve", lambda h: h.memset(onesS[:, :], 1.0), writes=[ival("onesS")])
        qcast.issue(wo_bf.ap(), dwo.ap(), writes=[ival("wo_bf")])
        qcast.issue(wq_bf.ap(), dwq.ap(), writes=[ival("wq_bf")])
        qcast.issue(wk_bf.ap(), dwk.ap(), writes=[ival("wk_bf")])
        qcast.issue(wv_bf.ap(), dwv.ap(), writes=[ival("wv_bf")])

        CW = 768

        def strip_chunks():
            out = []
            c0 = 0
            while c0 < C.INTER:
                w = min(CW, C.INTER - c0)
                for (didx_, dcon) in ((dgid, wg_f), (duid, wu_f)):
                    r0 = 0
                    while r0 < D:
                        p = min(128, D - r0)
                        out.append((didx_, dcon, r0, p, c0, w))
                        r0 += p
                c0 += w
            r0 = 0
            while r0 < C.INTER:
                p = min(128, C.INTER - r0)
                c0 = 0
                while c0 < D:
                    w = min(CW, D - c0)
                    out.append((ddid, wd_f, r0, p, c0, w))
                    c0 += w
                r0 += p
            return out

        def chore_gen():
            chunks = strip_chunks()

            def issue_load(i):
                didx_, dcon, r0, p, c0, w = chunks[i]
                par = i % 2
                bx = bixb[:, CW * par:CW * par + CW]
                iv_b = ival("bixb", CW * par * BF, (CW * par + CW) * BF)
                qcast.issue(bx[:p, :w], didx_.ap()[r0:r0 + p, c0:c0 + w],
                            writes=[iv_b])

            issue_load(0)
            for i, (didx_, dcon, r0, p, c0, w) in enumerate(chunks):
                par = i % 2
                bx = bixb[:, CW * par:CW * par + CW]
                iv_b = ival("bixb", CW * par * BF, (CW * par + CW) * BF)
                iv_a, iv_s = ival("bacc"), ival("bsel")
                iv_a2, iv_s2 = ival("bac2"), ival("bse2")

                def ts_(j, dst, iv_d):
                    em.op("dve", lambda h, p=p, w=w, j=j, dst=dst, bx=bx:
                          h.tensor_scalar(dst[:p, :w], bx[:p, :w], float(j),
                                          lutS[:p, j:j + 1],
                                          ALU.is_equal, ALU.mult),
                          reads=[iv_b, ival("lutS")], writes=[iv_d])

                def tt_(acc, stg, iv_acc, iv_stg):
                    em.op("dve", lambda h, p=p, w=w, acc=acc, stg=stg:
                          h.tensor_tensor(acc[:p, :w], acc[:p, :w],
                                          stg[:p, :w], ALU.add),
                          reads=[iv_stg, iv_acc], writes=[iv_acc])

                ts_(0, bacc, iv_a)
                ts_(8, bac2, iv_a2)
                ts_(1, bsel, iv_s)
                ts_(9, bse2, iv_s2)
                if i + 1 < len(chunks):
                    issue_load(i + 1)
                yield 1.1
                for k in range(1, 7):
                    tt_(bacc, bsel, iv_a, iv_s)
                    tt_(bac2, bse2, iv_a2, iv_s2)
                    ts_(k + 1, bsel, iv_s)
                    ts_(k + 9, bse2, iv_s2)
                    yield 1.2
                tt_(bacc, bsel, iv_a, iv_s)
                tt_(bac2, bse2, iv_a2, iv_s2)
                em.op("dve", lambda h, p=p, w=w:
                      h.tensor_tensor(bacc[:p, :w], bacc[:p, :w],
                                      bac2[:p, :w], ALU.add),
                      reads=[iv_a2, iv_a], writes=[iv_a])
                yield 1.1
                if dcon is wd_f:
                    ivw = (dcon.name, r0 * D * BF, (r0 + p) * D * BF)
                else:
                    ivw = (dcon.name, c0 * BF, (c0 + w) * BF)
                qcast.issue(dcon.ap()[r0:r0 + p, c0:c0 + w], bacc[:p, :w],
                            reads=[iv_a], writes=[ivw])
                if dcon is wd_f:
                    if c0 + w >= D:
                        chore_state["wd_rows"] = r0 + p
                elif dcon is wu_f and r0 + p >= D:
                    chore_state["wgu_cols"] = c0 + w
                chore_state["chunks"] += 1
                yield 0.3
        chore_state = {"done": False, "chunks": 0, "wgu_cols": 0, "wd_rows": 0}
        chore_state["gen"] = chore_gen()

        def do_chore(budget_us):
            while budget_us > 0 and not chore_state["done"]:
                try:
                    budget_us -= next(chore_state["gen"])
                except StopIteration:
                    chore_state["done"] = True

        def ensure_wgu_cols(n):
            while chore_state["wgu_cols"] < n and not chore_state["done"]:
                try:
                    next(chore_state["gen"])
                except StopIteration:
                    chore_state["done"] = True

        def ensure_wd_rows(n):
            while chore_state["wd_rows"] < n and not chore_state["done"]:
                try:
                    next(chore_state["gen"])
                except StopIteration:
                    chore_state["done"] = True

        def finish_chores():
            do_chore(1e9)




        slot_state = {"i": 0}

        def ln_stats(chunks):
            sl = 32 * (slot_state["i"] % 3)
            slot_state["i"] += 1
            nch = len(chunks)
            stat_iv = ival("rows", sl * 4, (sl + 8 + 6 * nch) * 4)
            sc = sl + 8
            for i, (ap, iv) in enumerate(chunks):
                em.op("dve", lambda h, ap=ap, i=i, sc=sc:
                      h.bn_stats(rows[:, sc + 6 * i: sc + 6 * i + 6], ap),
                      reads=[iv], writes=[stat_iv])
            em.op("dve", lambda h, sl=sl, sc=sc, nch=nch:
                  h.bn_aggr(rows[:, sl:sl + 2], rows[:, sc:sc + 6 * nch]),
                  reads=[stat_iv], writes=[stat_iv])
            em.op("dve", lambda h, sl=sl:
                  h.tensor_scalar(rows[:, sl + 2:sl + 3], rows[:, sl + 1:sl + 2],
                                  C.eps, None, ALU.add),
                  reads=[stat_iv], writes=[stat_iv])
            em.op("act", lambda h, sl=sl:
                  h.activation(rows[:, sl + 3:sl + 4], rows[:, sl + 2:sl + 3],
                               AF.Ln),
                  reads=[stat_iv], writes=[stat_iv])
            em.op("act", lambda h, sl=sl:
                  h.activation(rows[:, sl + 4:sl + 5], rows[:, sl + 3:sl + 4],
                               AF.Exp, scale=-0.5),
                  reads=[stat_iv], writes=[stat_iv])
            em.op("dve", lambda h, sl=sl:
                  h.tensor_tensor(rows[:, sl + 5:sl + 6], rows[:, sl:sl + 1],
                                  rows[:, sl + 4:sl + 5], ALU.mult),
                  reads=[stat_iv], writes=[stat_iv])
            em.op("dve", lambda h, sl=sl:
                  h.tensor_scalar(rows[:, sl + 5:sl + 6], rows[:, sl + 5:sl + 6],
                                  -1.0, None, ALU.mult),
                  reads=[stat_iv], writes=[stat_iv])
            return (rows[:, sl + 4:sl + 5], rows[:, sl + 5:sl + 6], stat_iv)

        def sbuf_chunks(ap_maker, name, base_bytes, n_elems, esize):
            out = []
            c0 = 0
            while c0 < n_elems:
                w = min(512, n_elems - c0)
                out.append((ap_maker(c0, w),
                            ival(name, base_bytes + c0 * esize,
                                 base_bytes + (c0 + w) * esize)))
                c0 += w
            return out

        def psum_chunks(bank0, n_elems):
            out = []
            c0 = 0
            while c0 < n_elems:
                w = min(512, n_elems - c0)
                out.append((psum[:, 512 * bank0 + c0: 512 * bank0 + c0 + w],
                            ival(PS, 2048 * bank0 + 4 * c0,
                                 2048 * bank0 + 4 * (c0 + w))))
                c0 += w
            return out

        def phase_A(b):
            for tt in range(TT):
                xap, xnm = (xa_f32, "qr") if tt % 2 == 0 else (xb_f32, "kr")
                iv_x = ival(xnm, 0, D * 4)
                qld.issue(xap, dx.ap()[b, 128 * tt:128 * tt + 128, :],
                          writes=[iv_x])
                ch = sbuf_chunks(lambda c0, w, xap=xap: xap[:, c0:c0 + w],
                                 xnm, 0, D, 4)
                rstd, nbias, stiv = ln_stats(ch)
                lnb, lnm = (tmp, "tmp") if tt % 2 == 0 else (pt, "pt")
                iv_l = ival(lnm, 0, D * BF)
                em.op("act", lambda h, xap=xap, r=rstd, nb=nbias, lnb=lnb:
                      h.activation(lnb[:, 0:D], xap, AF.Identity, bias=nb,
                                   scale=r),
                      reads=[iv_x, stiv], writes=[iv_l])
                qst.issue(ln1b.ap()[128 * tt:128 * tt + 128, :], lnb[:, 0:D],
                          reads=[iv_l],
                          writes=[ival("ln1b", 128 * tt * D * BF,
                                        (128 * tt + 128) * D * BF)])
                do_chore(4.0)

        def load_ln1T():
            for ci in range(IC):
                qld.issue(ln1T[:, T * ci:T * ci + T],
                          ln1b.ap()[:, 128 * ci:128 * ci + 128],
                          reads=[ival("ln1b")],
                          writes=[ival("ln1T", T * BF * ci, T * BF * (ci + 1))],
                          transpose=True)

        def produce_qk(b, h, which):
            dw = wq_bf if which == "q" else wk_bf
            iv_w = ival("wqk")
            qld.issue(wqk[:, :], dw.ap()[h],
                      reads=[ival(dw.name, h * 128 * D * BF,
                                  (h + 1) * 128 * D * BF)],
                      writes=[iv_w])
            raw = qraw if which == "q" else kraw
            rawn = "qraw" if which == "q" else "kraw"
            for tc in range(T // 512):
                pb = tc % 2
                mms, rds = [], [iv_w]
                for ci in range(IC):
                    mms.append(lambda h2, ci=ci, tc=tc, pb=pb:
                               h2.matmul(bank(pb), wqk[:, 128 * ci:128 * ci + 128],
                                         ln1T[:, T * ci + 512 * tc:
                                              T * ci + 512 * tc + 512],
                                         start=(ci == 0), stop=(ci == IC - 1)))
                    rds.append(ival("ln1T", BF * (T * ci + 512 * tc),
                                    BF * (T * ci + 512 * tc + 512)))
                em.op("pe", mms, reads=rds, writes=[bankiv(pb)])
                em.op("act", lambda h2, tc=tc, pb=pb, raw=raw:
                      h2.activation(raw[:, 512 * tc: 512 * tc + 512],
                                    bank(pb), AF.Identity),
                      reads=[bankiv(pb)],
                      writes=[ival(rawn, BF * 512 * tc, BF * (512 * tc + 512))])
            return rawn

        def rope(rawn, dst, dst_name, par):
            src = qraw if rawn == "qraw" else kraw
            stg = kraw if rawn == "qraw" else qraw
            stgn = "kraw" if rawn == "qraw" else "qraw"
            iv_src = ival(rawn)
            iv_stg = ival(stgn)
            dstv = dst[:, T * par:T * par + T]
            iv_dst = ival(dst_name, T * BF * par, T * BF * (par + 1))
            iv_t = ival("tmp", 0, T * BF)
            nh = T // 1024
            em.op("dve", lambda h: h.tensor_tensor(tmp[:, 0:T], src[:, 0:T],
                                                   csS[:, :], ALU.mult),
                  reads=[iv_src, ival("csS")], writes=[iv_t])
            for hb in range(nh):
                mms = []
                for tc in range(2):
                    mms.append(lambda h, hb=hb, tc=tc:
                               h.matmul(bank(6 + tc),
                                        permS[:, :],
                                        src[:, 1024 * hb + 512 * tc:
                                            1024 * hb + 512 * tc + 512],
                                        start=True, stop=True))
                em.op("pe", mms, reads=[iv_src, ival("permS")],
                      writes=[bankiv(6, 2)])
                em.op("act", lambda h, hb=hb, stg=stg:
                      h.activation(stg[:, 1024 * hb:1024 * hb + 1024],
                                   psum[:, 3072:4096], AF.Identity),
                      reads=[bankiv(6, 2)], writes=[iv_stg])
                em.op("dve", lambda h, hb=hb, dstv=dstv, stg=stg:
                      h.tensor_tensor(dstv[:, 1024 * hb:1024 * hb + 1024],
                                      stg[:, 1024 * hb:1024 * hb + 1024],
                                      snS[:, 1024 * hb:1024 * hb + 1024],
                                      ALU.mult),
                      reads=[iv_stg, ival("snS")], writes=[iv_dst])
            em.op("dve", lambda h, dstv=dstv:
                  h.tensor_tensor(dstv, dstv, tmp[:, 0:T], ALU.add),
                  reads=[iv_t, iv_dst], writes=[iv_dst])

        def produce_group_v(b, g):
            W = C.HG * 128
            iv_w = ival("wvs")
            qld.issue(wvs[:, :IC * W], wv_bf.ap()[g],
                      reads=[ival("wv_bf", g * 128 * 2 * D * BF,
                                  (g + 1) * 128 * 2 * D * BF)],
                      writes=[iv_w])
            for ct in range(TT):
                pb = 2 + (ct % 2)
                mms, rds = [], [iv_w]
                for ci in range(IC):
                    mms.append(lambda h2, ci=ci, ct=ct, pb=pb, W=W:
                               h2.matmul(bank(pb, W),
                                         ln1T[:, T * ci + 128 * ct:
                                              T * ci + 128 * ct + 128],
                                         wvs[:, W * ci: W * ci + W],
                                         start=(ci == 0), stop=(ci == IC - 1)))
                    rds.append(ival("ln1T", BF * (T * ci + 128 * ct),
                                    BF * (T * ci + 128 * ct + 128)))
                em.op("pe", mms, reads=rds, writes=[bankiv(pb)])
                em.op("act", lambda h2, ct=ct, pb=pb, W=W:
                      h2.activation(vg[:, W * ct: W * ct + W], bank(pb, W),
                                    AF.Identity),
                      reads=[bankiv(pb)],
                      writes=[ival("vg", BF * W * ct, BF * W * (ct + 1))])

        def attend(b, h, par):
            hh = h % C.HG
            W = C.HG * 128
            qrh = qr[:, T * par: T * par + T]
            krh = kr[:, T * par: T * par + T]
            iv_qr = ival("qr", T * BF * par, T * BF * (par + 1))
            iv_kr = ival("kr", T * BF * par, T * BF * (par + 1))
            for qt in range(QT):
                nch = 4 * qt + 4
                ngrp = nch // 2
                def emit_st(g):
                    pb = 2 * (g % 2)
                    mms = []
                    for cc in range(2):
                        c = 2 * g + cc
                        mms.append(lambda h2, c=c, cc=cc, qt=qt, pb=pb:
                                   h2.matmul(bank(pb + cc),
                                             krh[:, 128 * c:128 * c + 128],
                                             qrh[:, 512 * qt:512 * qt + 512],
                                             start=True, stop=True))
                    em.op("pe", mms, reads=[iv_qr, iv_kr],
                          writes=[bankiv(pb, 2)])

                def emit_exp(g):
                    pb = 2 * (g % 2)
                    pc0 = 1024 * (g % 2)
                    iv_p = ival("pt", pc0 * BF, (pc0 + 1024) * BF)
                    em.op("act", lambda h2, pc0=pc0, pb=pb:
                          h2.activation(pt[:, pc0:pc0 + 1024], bank(pb, 1024),
                                        AF.Exp, scale=C.scale),
                          reads=[bankiv(pb, 2)], writes=[iv_p])
                    if g >= ngrp - 2:
                        mc0 = 1024 * (g - (ngrp - 2))
                        em.op("dve", lambda h2, pc0=pc0, mc0=mc0:
                              h2.tensor_tensor(pt[:, pc0:pc0 + 1024],
                                               pt[:, pc0:pc0 + 1024],
                                               maskS[:, mc0:mc0 + 1024],
                                               ALU.mult),
                              reads=[ival("maskS"), iv_p], writes=[iv_p])
                    return iv_p

                def emit_pv(g, iv_p):
                    pc0 = 1024 * (g % 2)
                    mms = []
                    for cc in range(2):
                        c = 2 * g + cc
                        mms.append(lambda h2, c=c, cc=cc, pc0=pc0, hh=hh,
                                   W=W, nch=nch:
                                   h2.matmul(bank(4),
                                             vg[:, W * c + 128 * hh:
                                                W * c + 128 * hh + 128],
                                             pt[:, pc0 + 512 * cc:
                                                pc0 + 512 * cc + 512],
                                             start=(c == 0),
                                             stop=(c == nch - 1)))
                    em.op("pe", mms, reads=[ival("vg"), iv_p],
                          writes=[bankiv(4)])
                    mms = []
                    for cc in range(2):
                        c = 2 * g + cc
                        mms.append(lambda h2, c=c, cc=cc, pc0=pc0, nch=nch:
                                   h2.matmul(psum[0:1, 2560:3072],
                                             onesS[:, 0:1],
                                             pt[:, pc0 + 512 * cc:
                                                pc0 + 512 * cc + 512],
                                             start=(c == 0),
                                             stop=(c == nch - 1)))
                    em.op("pe", mms, reads=[ival("onesS"), iv_p],
                          writes=[bankiv(5)])

                emit_st(0)
                iv_prev = emit_exp(0)
                for g in range(1, ngrp):
                    emit_st(g)
                    iv_p = emit_exp(g)
                    emit_pv(g - 1, iv_prev)
                    iv_prev = iv_p
                emit_pv(ngrp - 1, iv_prev)
                do_chore(3.0 * (qt + 1))
                iv_sr = ival("srow")
                em.op("act", lambda h2:
                      h2.activation(srow[0:1, 0:512], psum[0:1, 2560:3072],
                                    AF.Ln),
                      reads=[bankiv(5)], writes=[iv_sr])
                em.op("act", lambda h2:
                      h2.activation(srow[0:1, 0:512], srow[0:1, 0:512],
                                    AF.Exp, scale=-1.0),
                      reads=[iv_sr], writes=[iv_sr])
                qst.issue(srd.ap().unsqueeze(0), srow[0:1, 0:512],
                          reads=[iv_sr], writes=[ival("srd")])
                qst.issue(bcbuf[:, :],
                          srd.ap().unsqueeze(0).partition_broadcast(128),
                          reads=[ival("srd")], writes=[ival("bcbuf")])
                hc0 = BF * (T * h + 512 * qt)
                em.op("dve", lambda h2, h=h, qt=qt:
                      h2.tensor_tensor(headsT[:, T * h + 512 * qt:
                                              T * h + 512 * qt + 512],
                                       bank(4), bcbuf[:, :], ALU.mult),
                      reads=[bankiv(4), ival("bcbuf")],
                      writes=[ival("headsT", hc0, hc0 + 512 * BF)])

        def phase_B(b):
            load_ln1T()
            produce_group_v(b, 0)
            rope(produce_qk(b, 0, "q"), qr, "qr", 0)
            rope(produce_qk(b, 0, "k"), kr, "kr", 0)
            for h in range(H):
                if h + 1 < H:
                    par1 = (h + 1) % 2
                    rope(produce_qk(b, h + 1, "q"), qr, "qr", par1)
                    rope(produce_qk(b, h + 1, "k"), kr, "kr", par1)
                attend(b, h, h % 2)
                if h + 1 < H and (h + 1) % C.HG == 0:
                    produce_group_v(b, (h + 1) // C.HG)
                do_chore(25.0)

        def phase_C(b):
            for ci in range(IC):
                qld.issue(ln1T[:, T * ci: T * ci + D],
                          wo_bf.ap()[128 * ci:128 * ci + 128, :],
                          reads=[ival("wo_bf")],
                          writes=[ival("ln1T", BF * T * ci,
                                       BF * (T * ci + D))])
            nhalf = (D + 1023) // 1024
            rot = {"g": 0}
            for tt in range(TT):
                groups = []
                stat_chunks = []
                for hf in range(nhalf):
                    g = rot["g"]
                    rot["g"] = (g + 1) % 3
                    groups.append(g)
                    hw = min(1024, D - 1024 * hf)
                    noc = (hw + 511) // 512
                    mms, rds = [], []
                    for oc2 in range(noc):
                        w = min(512, hw - 512 * oc2)
                        oc = 2 * hf + oc2
                        for jc in range(IC):
                            mms.append(
                                lambda h2, oc=oc, jc=jc, w=w, tt=tt, g=g,
                                oc2=oc2:
                                h2.matmul(bank(2 * g + oc2, w),
                                          headsT[:, T * jc + 128 * tt:
                                                 T * jc + 128 * tt + 128],
                                          ln1T[:, T * jc + 512 * oc:
                                               T * jc + 512 * oc + w],
                                          start=(jc == 0),
                                          stop=(jc == IC - 1)))
                            rds.append(ival("headsT", BF * (T * jc + 128 * tt),
                                            BF * (T * jc + 128 * tt + 128)))
                            rds.append(ival("ln1T", BF * (T * jc + 512 * oc),
                                            BF * (T * jc + 512 * oc + w)))
                    em.op("pe", mms, reads=rds, writes=[bankiv(2 * g, noc)])
                    for oc2 in range(noc):
                        w = min(512, hw - 512 * oc2)
                        stat_chunks.append(
                            (psum[:, 512 * (2 * g + oc2):
                                  512 * (2 * g + oc2) + w],
                             ival(PS, 2048 * (2 * g + oc2),
                                  2048 * (2 * g + oc2) + 4 * w)))
                rstd, nbias, stiv = ln_stats(stat_chunks)
                iv_x = ival("qr", 0, D * 4)
                qld.issue(xa_f32, dx.ap()[b, 128 * tt:128 * tt + 128, :],
                          writes=[iv_x])
                iv_x1 = ival("kr", 0, D * 4)
                nrm = []
                rdn = [stiv]
                for hf in range(nhalf):
                    g = groups[hf]
                    hw = min(1024, D - 1024 * hf)
                    noc = (hw + 511) // 512
                    em.op("act", lambda h2, r=rstd, nb=nbias, g=g, hf=hf,
                          hw=hw:
                          h2.activation(xb_f32[:, 1024 * hf:1024 * hf + hw],
                                        psum[:, 512 * 2 * g:512 * 2 * g + hw],
                                        AF.Identity, bias=nb, scale=r),
                          reads=[bankiv(2 * g, noc), stiv], writes=[iv_x1])
                em.op("dve", lambda h2:
                      h2.tensor_tensor(xb_f32, xb_f32, xa_f32, ALU.add),
                      reads=[iv_x, iv_x1], writes=[iv_x1])
                qst.issue(x1s.ap()[b, 128 * tt:128 * tt + 128, :], xb_f32,
                          reads=[iv_x1], writes=[ival("x1s")])
                rstd3, nbias3, stiv3 = ln_stats(
                    sbuf_chunks(lambda c0, w: xb_f32[:, c0:c0 + w], "kr", 0,
                                D, 4))
                iv_l3 = ival("tmp", 0, D * BF)
                em.op("act", lambda h2, r=rstd3, nb=nbias3:
                      h2.activation(tmp[:, 0:D], xb_f32, AF.Identity,
                                    bias=nb, scale=r),
                      reads=[iv_x1, stiv3], writes=[iv_l3])
                qst.issue(ln3b.ap()[b, 128 * tt:128 * tt + 128, :], tmp[:, 0:D],
                          reads=[iv_l3], writes=[ival("ln3b")])
                do_chore(14.0)

        TBT, TBW = C.TBT, C.TB

        free_regions = [("ln1T", 0, IC * T), ("headsT", 0, H * T)]
        cur = {"ri": 0, "off": 0}

        def take(ncols):
            while True:
                name, base, cap = free_regions[cur["ri"]]
                if cur["off"] + ncols <= cap:
                    out = (name, base + cur["off"])
                    cur["off"] += ncols
                    return out
                cur["ri"] += 1
                cur["off"] = 0
                assert cur["ri"] < len(free_regions), "phase-D SBUF overflow"

        tens = {"ln1T": ln1T, "headsT": headsT, "pt": pt, "vg": vg,
                "wvs": wvs, "maskS": maskS, "csS": csS, "snS": snS}
        hT_loc = [take(TBW) for _ in range(C.OT)]
        ln3T_loc = [take(TBW) for _ in range(IC)]
        free_regions2 = [("pt", 0, 2048), ("vg", TBW, TT * C.HG * 128),
                         ("wvs", IC * 128, IC * C.HG * 128),
                         ("maskS", 0, 2048), ("csS", 0, T), ("snS", 0, T),
                         free_regions[cur["ri"]][:1] + (
                             free_regions[cur["ri"]][1] + cur["off"],
                             free_regions[cur["ri"]][2])]
        cur2 = {"ri": 0, "off": 0}

        def take2(ncols):
            while True:
                name, base, cap = free_regions2[cur2["ri"]]
                if base + cur2["off"] + ncols <= cap:
                    out = (name, base + cur2["off"])
                    cur2["off"] += ncols
                    return out
                cur2["ri"] += 1
                cur2["off"] = 0
                assert cur2["ri"] < len(free_regions2), "phase-D SBUF overflow2"

        dn_loc = [take2(D) for _ in range(TBT)]

        def view(loc, ncols):
            name, c0 = loc
            t = tens[name]
            return t[:, c0:c0 + ncols], ival(name, BF * c0, BF * (c0 + ncols))

        def ld_ln3T(b, tb, q):
            t0 = tb * TBW
            for ci in range(IC):
                v, iv = view(ln3T_loc[ci], TBW)
                q.issue(v, ln3b.ap()[b, t0:t0 + TBW, 128 * ci:128 * ci + 128],
                        reads=[ival("ln3b")], writes=[iv], transpose=True)

        def phase_D(b, tb, first, nxt):
            t0 = tb * TBW
            if first:
                ld_ln3T(b, tb, qld)
            nxc = TBW // 512
            for ot in range(C.OT):
                if b == 0 and tb == 0:
                    ensure_wgu_cols(min(C.INTER, 128 * ot + 128))
                iv_wg = ival("wqk")
                qld.issue(wqk[:, :].rearrange("p (c o) -> p c o", o=128),
                          wg_f.ap()[:, 128 * ot:128 * ot + 128]
                          .rearrange("(c p) o -> p c o", p=128),
                          reads=[ival("wg_f", 128 * ot * BF,
                                       (128 * ot + 128) * BF)],
                          writes=[iv_wg])
                iv_wu = ival("wvs", 0, IC * 128 * BF)
                qld.issue(wvs[:, 0:IC * 128].rearrange("p (c o) -> p c o", o=128),
                          wu_f.ap()[:, 128 * ot:128 * ot + 128]
                          .rearrange("(c p) o -> p c o", p=128),
                          reads=[ival("wu_f", 128 * ot * BF,
                                       (128 * ot + 128) * BF)],
                          writes=[iv_wu])
                for (iv_w, wt, pb) in ((iv_wg, wqk, 0), (iv_wu, wvs, 2)):
                    mms, rds = [], [iv_w]
                    for xc in range(nxc):
                        for ci in range(IC):
                            v3, iv3 = view(ln3T_loc[ci], TBW)
                            mms.append(
                                lambda h2, ci=ci, xc=xc, wt=wt, pb=pb, v3=v3:
                                h2.matmul(bank(pb + xc),
                                          wt[:, 128 * ci:128 * ci + 128],
                                          v3[:, 512 * xc:512 * xc + 512],
                                          start=(ci == 0), stop=(ci == IC - 1)))
                            rds.append(iv3)
                    em.op("pe", mms, reads=rds, writes=[bankiv(pb, nxc)])
                hv, hiv = view(hT_loc[ot], TBW)
                iv_sg = ival("vg", 0, TBW * BF)
                em.op("act", lambda h2:
                      h2.activation(vg[:, 0:TBW], bank(0, TBW), AF.Silu),
                      reads=[bankiv(0, nxc)], writes=[iv_sg])
                em.op("dve", lambda h2, hv=hv:
                      h2.tensor_tensor(hv, vg[:, 0:TBW], bank(2, TBW), ALU.mult),
                      reads=[iv_sg, bankiv(2, nxc)], writes=[hiv])
                do_chore(12.0)
            if nxt is not None:
                ld_ln3T(nxt[0], nxt[1], qst)
            for dc in range(C.DC):
                w = min(512, D - 512 * dc)
                for oc in range(C.OT):
                    if b == 0 and tb == 0:
                        ensure_wd_rows(min(C.INTER, 128 * oc + 128))
                    par = oc % min(4, T // 512)
                    iv_wd = ival("tmp", 512 * par * BF, (512 * par + w) * BF)
                    qld.issue(tmp[:, 512 * par:512 * par + w],
                              wd_f.ap()[128 * oc:128 * oc + 128,
                                        512 * dc:512 * dc + w],
                              reads=[ival("wd_f", 128 * oc * D * BF,
                                           (128 * oc + 128) * D * BF)],
                              writes=[iv_wd])
                    do_chore(3.0)
                    mms, rds = [], [iv_wd]
                    for t8 in range(TBT):
                        hv, hiv = view(hT_loc[oc], TBW)
                        mms.append(lambda h2, t8=t8, par=par, w=w, hv=hv,
                                   oc=oc:
                                   h2.matmul(bank(t8, w),
                                             hv[:, 128 * t8:128 * t8 + 128],
                                             tmp[:, 512 * par:512 * par + w],
                                             start=(oc == 0),
                                             stop=(oc == C.OT - 1)))
                        rds.append(hiv)
                    em.op("pe", mms, reads=rds,
                          writes=[bankiv(t8) for t8 in range(TBT)])
                for t8 in range(TBT):
                    dv, div = view(dn_loc[t8], D)
                    em.op("act", lambda h2, t8=t8, dc=dc, w=w, dv=dv:
                          h2.activation(dv[:, 512 * dc:512 * dc + w],
                                        bank(t8, w), AF.Identity),
                          reads=[bankiv(t8)],
                          writes=[(div[0], div[1] + BF * 512 * dc,
                                   div[1] + BF * (512 * dc + w))])
            for t8 in range(TBT):
                dv, div = view(dn_loc[t8], D)
                rstd4, nbias4, stiv4 = ln_stats(
                    sbuf_chunks(lambda c0, w2, dv=dv: dv[:, c0:c0 + w2],
                                div[0], div[1], D, BF))
                iv_x1 = ival("qr", 0, D * 4)
                qld.issue(xa_f32,
                          x1s.ap()[b, t0 + 128 * t8:t0 + 128 * t8 + 128, :],
                          reads=[ival("x1s")], writes=[iv_x1])
                iv_o = ival("kr", 0, D * 4)
                em.op("act", lambda h2, r=rstd4, nb=nbias4, dv=dv:
                      h2.activation(xb_f32, dv, AF.Identity, bias=nb, scale=r),
                      reads=[div, stiv4], writes=[iv_o])
                em.op("dve", lambda h2:
                      h2.tensor_tensor(xb_f32, xb_f32, xa_f32, ALU.add),
                      reads=[iv_x1, iv_o], writes=[iv_o])
                qst.issue(dy.ap()[b, t0 + 128 * t8:t0 + 128 * t8 + 128, :],
                          xb_f32, reads=[iv_o], writes=[ival("y")])

        for b in range(C.BPC):
            phase_A(b)
            phase_B(b)
            phase_C(b)
        bts = [(b, tb) for b in range(C.BPC) for tb in range(C.NTB)]
        for i, (b, tb) in enumerate(bts):
            phase_D(b, tb, i == 0,
                    bts[i + 1] if i + 1 < len(bts) else None)
        finish_chores()

        qst.final_waits("sp")
        qcast.final_waits("sp")
        qld.final_waits("sp")

        em.finalize(e(nc.Block()))
    return nc



def _host_consts(C: Cfg):
    j = np.arange(C.DH // 2, dtype=np.float64)
    theta = C.rope_base ** (-2.0 * j / C.DH)
    t = np.arange(C.T, dtype=np.float64)
    ang = np.outer(theta, t)
    cosT = np.concatenate([np.cos(ang), np.cos(ang)], axis=0).astype(np.float32)
    sinT = np.concatenate([-np.sin(ang), np.sin(ang)], axis=0).astype(np.float32)
    permT = np.zeros((128, 128), dtype=np.float32)
    for m in range(64):
        permT[m + 64, m] = 1.0
        permT[m, m + 64] = 1.0
    k = np.arange(128)[:, None]
    q = np.arange(512)[None, :]
    maskc = np.concatenate(
        [(q >= k + 128 * a).astype(np.float32) for a in range(4)], axis=1)
    return cosT, sinT, permT, maskc


def _deinterleave_cols(W_T, H, DH):
    perm = np.concatenate([h * DH + np.concatenate(
        [np.arange(0, DH, 2), np.arange(1, DH, 2)]) for h in range(H)])
    return np.ascontiguousarray(W_T[:, perm])


def _head_major(WT, W, D):
    IC = D // 128
    G = D // W
    return np.ascontiguousarray(
        WT.reshape(IC, 128, G, W).transpose(2, 1, 0, 3).reshape(G, 128, IC * W))


def make_in_maps(C, X, Wq, Wk, Wv, Wo, lut, gidx, uidx, didx):
    cosT, sinT, permT, maskc = _host_consts(C)
    wqT = _head_major(_deinterleave_cols(
        np.ascontiguousarray(Wq.T), C.H, C.DH), 128, C.D)
    wkT = _head_major(_deinterleave_cols(
        np.ascontiguousarray(Wk.T), C.H, C.DH), 128, C.D)
    wvT = _head_major(np.ascontiguousarray(Wv.T), 2 * 128, C.D)
    woT = np.ascontiguousarray(Wo.T)
    gT = np.ascontiguousarray(gidx.T)
    uT = np.ascontiguousarray(uidx.T)
    dT = np.ascontiguousarray(didx.T)
    in_maps = []
    for c in range(C.n_cores):
        in_maps.append({
            "x": np.ascontiguousarray(X[c * C.BPC:(c + 1) * C.BPC]),
            "wqT": wqT, "wkT": wkT, "wvT": wvT, "woT": woT,
            "lut": np.ascontiguousarray(lut),
            "gidxT": gT, "uidxT": uT, "didxT": dT,
            "csT": cosT, "snT": sinT, "permT": permT, "maskc": maskc,
        })
    return in_maps


def kernel(L0_attn_X, L0_attn_Wq, L0_attn_Wk, L0_attn_Wv, L0_attn_Wo,
           L0_ffn_lut, L0_ffn_gate_idx, L0_ffn_up_idx, L0_ffn_down_idx):
    C = Cfg()
    X = np.asarray(L0_attn_X, dtype=np.float32)
    in_maps = make_in_maps(C, X,
                           np.asarray(L0_attn_Wq, dtype=np.float32),
                           np.asarray(L0_attn_Wk, dtype=np.float32),
                           np.asarray(L0_attn_Wv, dtype=np.float32),
                           np.asarray(L0_attn_Wo, dtype=np.float32),
                           np.asarray(L0_ffn_lut, dtype=np.float32),
                           np.asarray(L0_ffn_gate_idx, dtype=np.int32),
                           np.asarray(L0_ffn_up_idx, dtype=np.int32),
                           np.asarray(L0_ffn_down_idx, dtype=np.int32))
    nc = build_program(C)
    res = run_bass_kernel_spmd(nc, in_maps, core_ids=list(range(C.n_cores)))
    return np.concatenate([r["y"] for r in res.results], axis=0)


# version 25
# speedup vs baseline: 1.0563x; 1.0296x over previous
import numpy as np
import ml_dtypes
from contextlib import ExitStack

import concourse.bass as bass
import concourse.mybir as mybir
from concourse.bass_utils import run_bass_kernel_spmd

F32 = mybir.dt.float32
BF16 = mybir.dt.bfloat16
I32 = mybir.dt.int32
AF = mybir.ActivationFunctionType
ALU = mybir.AluOpType

BF = 2


class Cfg:
    def __init__(self, n_cores=8, batch=16, T=2048, D=2048, H=16, INTER=5504,
                 LUT=16, tblock=1024, eps=1e-5, rope_base=10000.0):
        self.n_cores = n_cores
        self.batch = batch
        self.T = T
        self.D = D
        self.H = H
        self.DH = D // H
        assert self.DH == 128
        self.INTER = INTER
        self.LUT = LUT
        self.TB = tblock
        self.eps = eps
        self.rope_base = rope_base
        self.BPC = batch // n_cores
        self.IC = D // 128
        self.TT = T // 128
        self.QT = T // 512
        self.OT = INTER // 128
        self.DC = (D + 511) // 512
        assert D % 128 == 0 and INTER % 128 == 0 and T % 512 == 0
        assert D % n_cores == 0 and INTER % n_cores == 0 and T % tblock == 0
        self.gslice = D // n_cores
        self.dslice = INTER // n_cores
        self.scale = self.DH ** -0.5
        self.HG = 2
        assert H % self.HG == 0
        self.NTB = T // tblock
        self.TBT = tblock // 128
        assert self.TBT <= 8
        assert D * 4 <= 2 * T * BF


ENGINES = ("pe", "dve", "act", "pool", "sp")


class Em:

    def __init__(self, nc, stack):
        self.nc = nc
        self.stack = stack
        self.ops = {e: [] for e in ENGINES}
        self.cnt = {e: 0 for e in ENGINES}
        self.esem = {e: stack.enter_context(nc.semaphore(f"es_{e}")) for e in ENGINES}
        self.vc = {e: {} for e in ENGINES}
        self.self_waited = {e: 0 for e in ENGINES}
        self.snap = {e: {} for e in ENGINES}
        self.track = {}
        self.queues = {}
        self.n_waits = 0

    def _merge(self, eng, evvc):
        vc = self.vc[eng]
        for k, v in evvc.items():
            if vc.get(k, 0) < v:
                vc[k] = v

    def await_ev(self, eng, ev):
        axis, n = ev
        if axis == eng:
            if eng == "pe":
                return
            if self.self_waited[eng] < n:
                sem = self.esem[eng]
                self.ops[eng].append(lambda h, s=sem, v=n: h.wait_ge(s, v))
                self.self_waited[eng] = n
                self.n_waits += 1
            return
        if self.vc[eng].get(axis, 0) >= n:
            return
        if axis in ENGINES:
            sem, val = self.esem[axis], n
        else:
            qname, slot = axis
            q = self.queues[qname]
            sem, val = q.sems[slot], 16 * n
        self.ops[eng].append(lambda h, s=sem, v=val: h.wait_ge(s, v))
        self.n_waits += 1
        self._merge(eng, self.snap[axis][n])
        if self.vc[eng].get(axis, 0) < n:
            self.vc[eng][axis] = n

    def _t(self, name):
        if name not in self.track:
            self.track[name] = {"w": [], "r": []}
        return self.track[name]

    @staticmethod
    def _ov(lo, hi, rec):
        return rec[0] < hi and lo < rec[1]

    def _deps_for(self, reads, writes):
        deps = []
        for (name, lo, hi) in reads:
            t = self._t(name)
            deps += [rec[2] for rec in t["w"] if self._ov(lo, hi, rec)]
        for (name, lo, hi) in writes:
            t = self._t(name)
            deps += [rec[2] for rec in t["w"] if self._ov(lo, hi, rec)]
            deps += [rec[2] for rec in t["r"] if self._ov(lo, hi, rec)]
        return deps

    def _record(self, reads, writes, ev):
        for (name, lo, hi) in reads:
            self._t(name)["r"].append((lo, hi, ev))
        for (name, lo, hi) in writes:
            t = self._t(name)
            t["w"] = [rec for rec in t["w"] if not (rec[0] >= lo and rec[1] <= hi)]
            t["r"] = [rec for rec in t["r"] if not (rec[0] >= lo and rec[1] <= hi)]
            t["w"].append((lo, hi, ev))

    def _snapshot(self, eng, axis, n):
        vcs = dict(self.vc[eng])
        vcs[eng] = self.cnt[eng]
        vcs[axis] = n
        self.snap[axis][n] = vcs

    def op(self, eng, fns, reads=(), writes=()):
        if not isinstance(fns, (list, tuple)):
            fns = [fns]
        for ev in self._deps_for(reads, writes):
            self.await_ev(eng, ev)
        n = self.cnt[eng] + 1
        self.cnt[eng] = n
        sem = self.esem[eng]
        for f in fns[:-1]:
            self.ops[eng].append(f)
        f = fns[-1]
        self.ops[eng].append(lambda h, f=f, s=sem: f(h).then_inc(s, 1))
        self.vc[eng][eng] = n
        self._snapshot(eng, eng, n)
        ev = (eng, n)
        self._record(reads, writes, ev)
        return ev

    def finalize(self, block):
        hmap = {"pe": block.tensor, "dve": block.vector, "act": block.scalar,
                "pool": block.gpsimd, "sp": block.sync}
        for e in ENGINES:
            lst = self.ops[e]

            def body(h, lst=lst):
                for f in lst:
                    f(h)
            hmap[e](body)


class DmaQ:

    def __init__(self, em, eng, name, K=8):
        self.em = em
        self.eng = eng
        self.name = name
        self.K = K
        self.sems = [em.stack.enter_context(em.nc.semaphore(f"q_{name}{i}"))
                     for i in range(K)]
        self.n = 0
        em.queues[name] = self
        for i in range(K):
            em.snap[(name, i)] = {}

    def _emit(self, fn, reads, writes):
        em, eng = self.em, self.eng
        n = self.n + 1
        slot = (n - 1) % self.K
        occ = (n - 1) // self.K + 1
        axis = (self.name, slot)
        if occ > 1:
            em.await_ev(eng, (axis, occ - 1))
        for ev in em._deps_for(reads, writes):
            em.await_ev(eng, ev)
        sem = self.sems[slot]
        em.ops[eng].append(lambda h, fn=fn, s=sem: fn(h).then_inc(s, 16))
        self.n = n
        em._snapshot(eng, axis, occ)
        ev = (axis, occ)
        em._record(reads, writes, ev)
        return ev

    def issue(self, out_ap, in_ap, reads=(), writes=(), **kw):
        return self._emit(
            lambda h, o=out_ap, i=in_ap, kw=kw: h.dma_start(out=o, in_=i, **kw),
            reads, writes)

    def issue_fn(self, fn, reads=(), writes=()):
        return self._emit(fn, reads, writes)

    def final_waits(self, eng):
        em = self.em
        counts = [0] * self.K
        for j in range(1, self.n + 1):
            counts[(j - 1) % self.K] += 16
        for s in range(self.K):
            if counts[s]:
                em.ops[eng].append(
                    lambda h, sm=self.sems[s], v=counts[s]: h.wait_ge(sm, v))


def ival(name, lo=0, hi=1 << 40):
    return (name, lo, hi)


def build_program(cfg: Cfg):
    nc = bass.Bass("TRN2", target_bir_lowering=False)
    C = cfg
    T, D, H, IC, TT, QT = C.T, C.D, C.H, C.IC, C.TT, C.QT

    dx = nc.dram_tensor("x", [C.BPC, T, D], F32, kind="ExternalInput")
    dwq = nc.dram_tensor("wqT", [H, 128, D], F32, kind="ExternalInput")
    dwk = nc.dram_tensor("wkT", [H, 128, D], F32, kind="ExternalInput")
    dwv = nc.dram_tensor("wvT", [H // 2, 128, 2 * D], F32, kind="ExternalInput")
    dwo = nc.dram_tensor("woT", [D, D], F32, kind="ExternalInput")
    dlut = nc.dram_tensor("lut", [C.LUT], F32, kind="ExternalInput")
    dgid = nc.dram_tensor("gidxT", [D, C.INTER], I32, kind="ExternalInput")
    duid = nc.dram_tensor("uidxT", [D, C.INTER], I32, kind="ExternalInput")
    ddid = nc.dram_tensor("didxT", [C.INTER, D], I32, kind="ExternalInput")
    dcs = nc.dram_tensor("csT", [128, T], F32, kind="ExternalInput")
    dsn = nc.dram_tensor("snT", [128, T], F32, kind="ExternalInput")
    dperm = nc.dram_tensor("permT", [128, 128], F32, kind="ExternalInput")
    dmask = nc.dram_tensor("maskc", [128, 2048], F32, kind="ExternalInput")
    dy = nc.dram_tensor("y", [C.BPC, T, D], F32, kind="ExternalOutput")

    wq_bf = nc.dram_tensor("wq_bf", [H, 128, D], BF16)
    wk_bf = nc.dram_tensor("wk_bf", [H, 128, D], BF16)
    wv_bf = nc.dram_tensor("wv_bf", [H // 2, 128, 2 * D], BF16)
    wo_bf = nc.dram_tensor("wo_bf", [D, D], BF16)
    ln1b = nc.dram_tensor("ln1b", [T, D], BF16)
    srd = nc.dram_tensor("srd", [512], F32)
    ln3b = nc.dram_tensor("ln3b", [C.BPC, T, D], BF16)
    x1s = nc.dram_tensor("x1s", [C.BPC, T, D], F32)
    wg_f = nc.dram_tensor("wg_f", [D, C.INTER], BF16)
    wu_f = nc.dram_tensor("wu_f", [D, C.INTER], BF16)
    wd_f = nc.dram_tensor("wd_f", [C.INTER, D], BF16)

    with ExitStack() as stack:
        e = stack.enter_context
        ln1T = e(nc.sbuf_tensor("ln1T", [128, IC * T], BF16))
        headsT = e(nc.sbuf_tensor("headsT", [128, H * T], BF16))
        vg = e(nc.sbuf_tensor("vg", [128, TT * C.HG * 128], BF16))
        pt = e(nc.sbuf_tensor("pt", [128, 2048], BF16))
        qraw = e(nc.sbuf_tensor("qraw", [128, T], BF16))
        kraw = e(nc.sbuf_tensor("kraw", [128, T], BF16))
        wvs = e(nc.sbuf_tensor("wvs", [128, IC * C.HG * 128], BF16))
        wqk = e(nc.sbuf_tensor("wqk", [128, IC * 128], BF16))
        qr = e(nc.sbuf_tensor("qr", [128, 2 * T], BF16))
        kr = e(nc.sbuf_tensor("kr", [128, 2 * T], BF16))
        tmp = e(nc.sbuf_tensor("tmp", [128, T], BF16))
        csS = e(nc.sbuf_tensor("csS", [128, T], BF16))
        snS = e(nc.sbuf_tensor("snS", [128, T], BF16))
        permS = e(nc.sbuf_tensor("permS", [128, 128], BF16))
        maskS = e(nc.sbuf_tensor("maskS", [128, 2048], BF16))
        onesS = e(nc.sbuf_tensor("onesS", [128, 2], BF16))
        rows = e(nc.sbuf_tensor("rows", [128, 128], F32))
        srow = e(nc.sbuf_tensor("srow", [1, 512], F32))
        bcbuf = e(nc.sbuf_tensor("bcbuf", [128, 512], F32))
        lutS = e(nc.sbuf_tensor("lutS", [128, C.LUT], F32))
        bixb = e(nc.sbuf_tensor("bixb", [128, 1536], BF16))
        bacc = e(nc.sbuf_tensor("bacc", [128, 768], BF16))
        bsel = e(nc.sbuf_tensor("bsel", [128, 768], BF16))
        bac2 = e(nc.sbuf_tensor("bac2", [128, 768], BF16))
        bse2 = e(nc.sbuf_tensor("bse2", [128, 768], BF16))
        psum = e(nc.psum_tensor("psum", [128, 4096], F32))

        em = Em(nc, stack)
        qld = DmaQ(em, "sp", "qld", 8)
        qst = DmaQ(em, "act", "qst", 8)
        qcast = DmaQ(em, "pool", "qcast", 8)

        PS = "psum"

        def bank(i, w=512):
            return psum[:, 512 * i:512 * i + w]

        def bankiv(i, nb=1):
            return ival(PS, 2048 * i, 2048 * (i + nb))

        xa_f32 = qr.ap().bitcast(F32)[:, 0:D]
        xb_f32 = kr.ap().bitcast(F32)[:, 0:D]

        qcast.issue(csS[:, :], dcs.ap(), writes=[ival("csS")])
        qcast.issue(snS[:, :], dsn.ap(), writes=[ival("snS")])
        qcast.issue(permS[:, :], dperm.ap(), writes=[ival("permS")])
        qcast.issue(maskS[:, :], dmask.ap(), writes=[ival("maskS")])
        qcast.issue(lutS[:, :], dlut.ap().unsqueeze(0).partition_broadcast(128),
                    writes=[ival("lutS")])
        em.op("dve", lambda h: h.memset(onesS[:, :], 1.0), writes=[ival("onesS")])
        qcast.issue(wo_bf.ap(), dwo.ap(), writes=[ival("wo_bf")])
        qcast.issue(wq_bf.ap(), dwq.ap(), writes=[ival("wq_bf")])
        qcast.issue(wk_bf.ap(), dwk.ap(), writes=[ival("wk_bf")])
        qcast.issue(wv_bf.ap(), dwv.ap(), writes=[ival("wv_bf")])

        CW = 768

        def strip_chunks():
            out = []
            c0 = 0
            while c0 < C.INTER:
                w = min(CW, C.INTER - c0)
                for (didx_, dcon) in ((dgid, wg_f), (duid, wu_f)):
                    r0 = 0
                    while r0 < D:
                        p = min(128, D - r0)
                        out.append((didx_, dcon, r0, p, c0, w))
                        r0 += p
                c0 += w
            r0 = 0
            while r0 < C.INTER:
                p = min(128, C.INTER - r0)
                c0 = 0
                while c0 < D:
                    w = min(CW, D - c0)
                    out.append((ddid, wd_f, r0, p, c0, w))
                    c0 += w
                r0 += p
            return out

        def chore_gen():
            chunks = strip_chunks()

            def issue_load(i):
                didx_, dcon, r0, p, c0, w = chunks[i]
                par = i % 2
                bx = bixb[:, CW * par:CW * par + CW]
                iv_b = ival("bixb", CW * par * BF, (CW * par + CW) * BF)
                qcast.issue(bx[:p, :w], didx_.ap()[r0:r0 + p, c0:c0 + w],
                            writes=[iv_b])

            issue_load(0)
            for i, (didx_, dcon, r0, p, c0, w) in enumerate(chunks):
                par = i % 2
                bx = bixb[:, CW * par:CW * par + CW]
                iv_b = ival("bixb", CW * par * BF, (CW * par + CW) * BF)
                iv_a, iv_s = ival("bacc"), ival("bsel")
                iv_a2, iv_s2 = ival("bac2"), ival("bse2")

                def ts_(j, dst, iv_d):
                    em.op("dve", lambda h, p=p, w=w, j=j, dst=dst, bx=bx:
                          h.tensor_scalar(dst[:p, :w], bx[:p, :w], float(j),
                                          lutS[:p, j:j + 1],
                                          ALU.is_equal, ALU.mult),
                          reads=[iv_b, ival("lutS")], writes=[iv_d])

                def tt_(acc, stg, iv_acc, iv_stg):
                    em.op("dve", lambda h, p=p, w=w, acc=acc, stg=stg:
                          h.tensor_tensor(acc[:p, :w], acc[:p, :w],
                                          stg[:p, :w], ALU.add),
                          reads=[iv_stg, iv_acc], writes=[iv_acc])

                ts_(0, bacc, iv_a)
                ts_(8, bac2, iv_a2)
                ts_(1, bsel, iv_s)
                ts_(9, bse2, iv_s2)
                if i + 1 < len(chunks):
                    issue_load(i + 1)
                yield 1.1
                for k in range(1, 7):
                    tt_(bacc, bsel, iv_a, iv_s)
                    tt_(bac2, bse2, iv_a2, iv_s2)
                    ts_(k + 1, bsel, iv_s)
                    ts_(k + 9, bse2, iv_s2)
                    if k == 3:
                        warm(iv_a)
                    yield 1.2
                tt_(bacc, bsel, iv_a, iv_s)
                tt_(bac2, bse2, iv_a2, iv_s2)
                em.op("dve", lambda h, p=p, w=w:
                      h.tensor_tensor(bacc[:p, :w], bacc[:p, :w],
                                      bac2[:p, :w], ALU.add),
                      reads=[iv_a2, iv_a], writes=[iv_a])
                yield 1.1
                if dcon is wd_f:
                    ivw = (dcon.name, r0 * D * BF, (r0 + p) * D * BF)
                else:
                    ivw = (dcon.name, c0 * BF, (c0 + w) * BF)
                qcast.issue(dcon.ap()[r0:r0 + p, c0:c0 + w], bacc[:p, :w],
                            reads=[iv_a], writes=[ivw])
                if dcon is wd_f:
                    if c0 + w >= D:
                        chore_state["wd_rows"] = r0 + p
                elif dcon is wu_f and r0 + p >= D:
                    chore_state["wgu_cols"] = c0 + w
                chore_state["chunks"] += 1
                yield 0.3
        chore_state = {"done": False, "chunks": 0, "wgu_cols": 0, "wd_rows": 0}
        chore_state["gen"] = chore_gen()

        def do_chore(budget_us):
            while budget_us > 0 and not chore_state["done"]:
                try:
                    budget_us -= next(chore_state["gen"])
                except StopIteration:
                    chore_state["done"] = True

        def ensure_wgu_cols(n):
            while chore_state["wgu_cols"] < n and not chore_state["done"]:
                try:
                    next(chore_state["gen"])
                except StopIteration:
                    chore_state["done"] = True

        def ensure_wd_rows(n):
            while chore_state["wd_rows"] < n and not chore_state["done"]:
                try:
                    next(chore_state["gen"])
                except StopIteration:
                    chore_state["done"] = True

        def finish_chores():
            do_chore(1e9)




        slot_state = {"i": 0}

        warm_iv = ival(PS, 2048 * 7 + 4 * 448, 2048 * 7 + 4 * 512)
        warm_state = {"on": True}

        def warm(dep_iv):
            if not warm_state["on"]:
                return
            em.op("pe", lambda h:
                  h.matmul(psum[0:64, 4032:4096], permS[:, 0:64],
                           permS[:, 0:64], start=True, stop=True),
                  reads=[ival("permS"), dep_iv], writes=[warm_iv])

        def ln_stats(chunks):
            sl = 32 * (slot_state["i"] % 3)
            slot_state["i"] += 1
            nch = len(chunks)
            stat_iv = ival("rows", sl * 4, (sl + 8 + 6 * nch) * 4)
            sc = sl + 8
            for i, (ap, iv) in enumerate(chunks):
                em.op("dve", lambda h, ap=ap, i=i, sc=sc:
                      h.bn_stats(rows[:, sc + 6 * i: sc + 6 * i + 6], ap),
                      reads=[iv], writes=[stat_iv])
            warm(stat_iv)
            em.op("dve", lambda h, sl=sl, sc=sc, nch=nch:
                  h.bn_aggr(rows[:, sl:sl + 2], rows[:, sc:sc + 6 * nch]),
                  reads=[stat_iv], writes=[stat_iv])
            em.op("dve", lambda h, sl=sl:
                  h.tensor_scalar(rows[:, sl + 2:sl + 3], rows[:, sl + 1:sl + 2],
                                  C.eps, None, ALU.add),
                  reads=[stat_iv], writes=[stat_iv])
            em.op("act", lambda h, sl=sl:
                  h.activation(rows[:, sl + 3:sl + 4], rows[:, sl + 2:sl + 3],
                               AF.Ln),
                  reads=[stat_iv], writes=[stat_iv])
            em.op("act", lambda h, sl=sl:
                  h.activation(rows[:, sl + 4:sl + 5], rows[:, sl + 3:sl + 4],
                               AF.Exp, scale=-0.5),
                  reads=[stat_iv], writes=[stat_iv])
            em.op("dve", lambda h, sl=sl:
                  h.tensor_tensor(rows[:, sl + 5:sl + 6], rows[:, sl:sl + 1],
                                  rows[:, sl + 4:sl + 5], ALU.mult),
                  reads=[stat_iv], writes=[stat_iv])
            em.op("dve", lambda h, sl=sl:
                  h.tensor_scalar(rows[:, sl + 5:sl + 6], rows[:, sl + 5:sl + 6],
                                  -1.0, None, ALU.mult),
                  reads=[stat_iv], writes=[stat_iv])
            warm(stat_iv)
            return (rows[:, sl + 4:sl + 5], rows[:, sl + 5:sl + 6], stat_iv)

        def sbuf_chunks(ap_maker, name, base_bytes, n_elems, esize):
            out = []
            c0 = 0
            while c0 < n_elems:
                w = min(512, n_elems - c0)
                out.append((ap_maker(c0, w),
                            ival(name, base_bytes + c0 * esize,
                                 base_bytes + (c0 + w) * esize)))
                c0 += w
            return out

        def psum_chunks(bank0, n_elems):
            out = []
            c0 = 0
            while c0 < n_elems:
                w = min(512, n_elems - c0)
                out.append((psum[:, 512 * bank0 + c0: 512 * bank0 + c0 + w],
                            ival(PS, 2048 * bank0 + 4 * c0,
                                 2048 * bank0 + 4 * (c0 + w))))
                c0 += w
            return out

        def phase_A(b):
            for tt in range(TT):
                xap, xnm = (xa_f32, "qr") if tt % 2 == 0 else (xb_f32, "kr")
                iv_x = ival(xnm, 0, D * 4)
                qld.issue(xap, dx.ap()[b, 128 * tt:128 * tt + 128, :],
                          writes=[iv_x])
                ch = sbuf_chunks(lambda c0, w, xap=xap: xap[:, c0:c0 + w],
                                 xnm, 0, D, 4)
                rstd, nbias, stiv = ln_stats(ch)
                lnb, lnm = (tmp, "tmp") if tt % 2 == 0 else (pt, "pt")
                iv_l = ival(lnm, 0, D * BF)
                em.op("act", lambda h, xap=xap, r=rstd, nb=nbias, lnb=lnb:
                      h.activation(lnb[:, 0:D], xap, AF.Identity, bias=nb,
                                   scale=r),
                      reads=[iv_x, stiv], writes=[iv_l])
                qst.issue(ln1b.ap()[128 * tt:128 * tt + 128, :], lnb[:, 0:D],
                          reads=[iv_l],
                          writes=[ival("ln1b", 128 * tt * D * BF,
                                        (128 * tt + 128) * D * BF)])
                do_chore(4.0)

        def load_ln1T():
            for ci in range(IC):
                qld.issue(ln1T[:, T * ci:T * ci + T],
                          ln1b.ap()[:, 128 * ci:128 * ci + 128],
                          reads=[ival("ln1b")],
                          writes=[ival("ln1T", T * BF * ci, T * BF * (ci + 1))],
                          transpose=True)

        def produce_qk(b, h, which):
            dw = wq_bf if which == "q" else wk_bf
            iv_w = ival("wqk")
            qld.issue(wqk[:, :], dw.ap()[h],
                      reads=[ival(dw.name, h * 128 * D * BF,
                                  (h + 1) * 128 * D * BF)],
                      writes=[iv_w])
            raw = qraw if which == "q" else kraw
            rawn = "qraw" if which == "q" else "kraw"
            for tc in range(T // 512):
                pb = tc % 2
                mms, rds = [], [iv_w]
                for ci in range(IC):
                    mms.append(lambda h2, ci=ci, tc=tc, pb=pb:
                               h2.matmul(bank(pb), wqk[:, 128 * ci:128 * ci + 128],
                                         ln1T[:, T * ci + 512 * tc:
                                              T * ci + 512 * tc + 512],
                                         start=(ci == 0), stop=(ci == IC - 1)))
                    rds.append(ival("ln1T", BF * (T * ci + 512 * tc),
                                    BF * (T * ci + 512 * tc + 512)))
                em.op("pe", mms, reads=rds, writes=[bankiv(pb)])
                em.op("act", lambda h2, tc=tc, pb=pb, raw=raw:
                      h2.activation(raw[:, 512 * tc: 512 * tc + 512],
                                    bank(pb), AF.Identity),
                      reads=[bankiv(pb)],
                      writes=[ival(rawn, BF * 512 * tc, BF * (512 * tc + 512))])
            return rawn

        def rope(rawn, dst, dst_name, par):
            src = qraw if rawn == "qraw" else kraw
            stg = kraw if rawn == "qraw" else qraw
            stgn = "kraw" if rawn == "qraw" else "qraw"
            iv_src = ival(rawn)
            iv_stg = ival(stgn)
            dstv = dst[:, T * par:T * par + T]
            iv_dst = ival(dst_name, T * BF * par, T * BF * (par + 1))
            iv_t = ival("tmp", 0, T * BF)
            nh = T // 1024
            em.op("dve", lambda h: h.tensor_tensor(tmp[:, 0:T], src[:, 0:T],
                                                   csS[:, :], ALU.mult),
                  reads=[iv_src, ival("csS")], writes=[iv_t])
            for hb in range(nh):
                mms = []
                for tc in range(2):
                    mms.append(lambda h, hb=hb, tc=tc:
                               h.matmul(bank(6 + tc),
                                        permS[:, :],
                                        src[:, 1024 * hb + 512 * tc:
                                            1024 * hb + 512 * tc + 512],
                                        start=True, stop=True))
                em.op("pe", mms, reads=[iv_src, ival("permS")],
                      writes=[bankiv(6, 2)])
                em.op("act", lambda h, hb=hb, stg=stg:
                      h.activation(stg[:, 1024 * hb:1024 * hb + 1024],
                                   psum[:, 3072:4096], AF.Identity),
                      reads=[bankiv(6, 2)], writes=[iv_stg])
                em.op("dve", lambda h, hb=hb, dstv=dstv, stg=stg:
                      h.tensor_tensor(dstv[:, 1024 * hb:1024 * hb + 1024],
                                      stg[:, 1024 * hb:1024 * hb + 1024],
                                      snS[:, 1024 * hb:1024 * hb + 1024],
                                      ALU.mult),
                      reads=[iv_stg, ival("snS")], writes=[iv_dst])
            em.op("dve", lambda h, dstv=dstv:
                  h.tensor_tensor(dstv, dstv, tmp[:, 0:T], ALU.add),
                  reads=[iv_t, iv_dst], writes=[iv_dst])

        def produce_group_v(b, g):
            W = C.HG * 128
            iv_w = ival("wvs")
            qld.issue(wvs[:, :IC * W], wv_bf.ap()[g],
                      reads=[ival("wv_bf", g * 128 * 2 * D * BF,
                                  (g + 1) * 128 * 2 * D * BF)],
                      writes=[iv_w])
            for ct in range(TT):
                pb = 2 + (ct % 2)
                mms, rds = [], [iv_w]
                for ci in range(IC):
                    mms.append(lambda h2, ci=ci, ct=ct, pb=pb, W=W:
                               h2.matmul(bank(pb, W),
                                         ln1T[:, T * ci + 128 * ct:
                                              T * ci + 128 * ct + 128],
                                         wvs[:, W * ci: W * ci + W],
                                         start=(ci == 0), stop=(ci == IC - 1)))
                    rds.append(ival("ln1T", BF * (T * ci + 128 * ct),
                                    BF * (T * ci + 128 * ct + 128)))
                em.op("pe", mms, reads=rds, writes=[bankiv(pb)])
                em.op("act", lambda h2, ct=ct, pb=pb, W=W:
                      h2.activation(vg[:, W * ct: W * ct + W], bank(pb, W),
                                    AF.Identity),
                      reads=[bankiv(pb)],
                      writes=[ival("vg", BF * W * ct, BF * W * (ct + 1))])

        def attend(b, h, par):
            hh = h % C.HG
            W = C.HG * 128
            qrh = qr[:, T * par: T * par + T]
            krh = kr[:, T * par: T * par + T]
            iv_qr = ival("qr", T * BF * par, T * BF * (par + 1))
            iv_kr = ival("kr", T * BF * par, T * BF * (par + 1))
            for qt in range(QT):
                nch = 4 * qt + 4
                ngrp = nch // 2
                def emit_st(g):
                    pb = 2 * (g % 2)
                    mms = []
                    for cc in range(2):
                        c = 2 * g + cc
                        mms.append(lambda h2, c=c, cc=cc, qt=qt, pb=pb:
                                   h2.matmul(bank(pb + cc),
                                             krh[:, 128 * c:128 * c + 128],
                                             qrh[:, 512 * qt:512 * qt + 512],
                                             start=True, stop=True))
                    em.op("pe", mms, reads=[iv_qr, iv_kr],
                          writes=[bankiv(pb, 2)])

                def emit_exp(g):
                    pb = 2 * (g % 2)
                    pc0 = 1024 * (g % 2)
                    iv_p = ival("pt", pc0 * BF, (pc0 + 1024) * BF)
                    em.op("act", lambda h2, pc0=pc0, pb=pb:
                          h2.activation(pt[:, pc0:pc0 + 1024], bank(pb, 1024),
                                        AF.Exp, scale=C.scale),
                          reads=[bankiv(pb, 2)], writes=[iv_p])
                    if g >= ngrp - 2:
                        mc0 = 1024 * (g - (ngrp - 2))
                        em.op("dve", lambda h2, pc0=pc0, mc0=mc0:
                              h2.tensor_tensor(pt[:, pc0:pc0 + 1024],
                                               pt[:, pc0:pc0 + 1024],
                                               maskS[:, mc0:mc0 + 1024],
                                               ALU.mult),
                              reads=[ival("maskS"), iv_p], writes=[iv_p])
                    return iv_p

                def emit_pv(g, iv_p):
                    pc0 = 1024 * (g % 2)
                    mms = []
                    for cc in range(2):
                        c = 2 * g + cc
                        mms.append(lambda h2, c=c, cc=cc, pc0=pc0, hh=hh,
                                   W=W, nch=nch:
                                   h2.matmul(bank(4),
                                             vg[:, W * c + 128 * hh:
                                                W * c + 128 * hh + 128],
                                             pt[:, pc0 + 512 * cc:
                                                pc0 + 512 * cc + 512],
                                             start=(c == 0),
                                             stop=(c == nch - 1)))
                    em.op("pe", mms, reads=[ival("vg"), iv_p],
                          writes=[bankiv(4)])
                    mms = []
                    for cc in range(2):
                        c = 2 * g + cc
                        mms.append(lambda h2, c=c, cc=cc, pc0=pc0, nch=nch:
                                   h2.matmul(psum[0:1, 2560:3072],
                                             onesS[:, 0:1],
                                             pt[:, pc0 + 512 * cc:
                                                pc0 + 512 * cc + 512],
                                             start=(c == 0),
                                             stop=(c == nch - 1)))
                    em.op("pe", mms, reads=[ival("onesS"), iv_p],
                          writes=[bankiv(5)])

                emit_st(0)
                iv_prev = emit_exp(0)
                for g in range(1, ngrp):
                    emit_st(g)
                    iv_p = emit_exp(g)
                    emit_pv(g - 1, iv_prev)
                    iv_prev = iv_p
                emit_pv(ngrp - 1, iv_prev)
                do_chore(3.0 * (qt + 1))
                iv_sr = ival("srow")
                em.op("act", lambda h2:
                      h2.activation(srow[0:1, 0:512], psum[0:1, 2560:3072],
                                    AF.Ln),
                      reads=[bankiv(5)], writes=[iv_sr])
                em.op("act", lambda h2:
                      h2.activation(srow[0:1, 0:512], srow[0:1, 0:512],
                                    AF.Exp, scale=-1.0),
                      reads=[iv_sr], writes=[iv_sr])
                qst.issue(srd.ap().unsqueeze(0), srow[0:1, 0:512],
                          reads=[iv_sr], writes=[ival("srd")])
                qst.issue(bcbuf[:, :],
                          srd.ap().unsqueeze(0).partition_broadcast(128),
                          reads=[ival("srd")], writes=[ival("bcbuf")])
                hc0 = BF * (T * h + 512 * qt)
                em.op("dve", lambda h2, h=h, qt=qt:
                      h2.tensor_tensor(headsT[:, T * h + 512 * qt:
                                              T * h + 512 * qt + 512],
                                       bank(4), bcbuf[:, :], ALU.mult),
                      reads=[bankiv(4), ival("bcbuf")],
                      writes=[ival("headsT", hc0, hc0 + 512 * BF)])

        def phase_B(b):
            load_ln1T()
            produce_group_v(b, 0)
            rope(produce_qk(b, 0, "q"), qr, "qr", 0)
            rope(produce_qk(b, 0, "k"), kr, "kr", 0)
            for h in range(H):
                if h + 1 < H:
                    par1 = (h + 1) % 2
                    rope(produce_qk(b, h + 1, "q"), qr, "qr", par1)
                    rope(produce_qk(b, h + 1, "k"), kr, "kr", par1)
                attend(b, h, h % 2)
                if h + 1 < H and (h + 1) % C.HG == 0:
                    produce_group_v(b, (h + 1) // C.HG)
                do_chore(25.0)

        def phase_C(b):
            for ci in range(IC):
                qld.issue(ln1T[:, T * ci: T * ci + D],
                          wo_bf.ap()[128 * ci:128 * ci + 128, :],
                          reads=[ival("wo_bf")],
                          writes=[ival("ln1T", BF * T * ci,
                                       BF * (T * ci + D))])
            nhalf = (D + 1023) // 1024
            rot = {"g": 0}
            for tt in range(TT):
                groups = []
                stat_chunks = []
                for hf in range(nhalf):
                    g = rot["g"]
                    rot["g"] = (g + 1) % 3
                    groups.append(g)
                    hw = min(1024, D - 1024 * hf)
                    noc = (hw + 511) // 512
                    mms, rds = [], []
                    for oc2 in range(noc):
                        w = min(512, hw - 512 * oc2)
                        oc = 2 * hf + oc2
                        for jc in range(IC):
                            mms.append(
                                lambda h2, oc=oc, jc=jc, w=w, tt=tt, g=g,
                                oc2=oc2:
                                h2.matmul(bank(2 * g + oc2, w),
                                          headsT[:, T * jc + 128 * tt:
                                                 T * jc + 128 * tt + 128],
                                          ln1T[:, T * jc + 512 * oc:
                                               T * jc + 512 * oc + w],
                                          start=(jc == 0),
                                          stop=(jc == IC - 1)))
                            rds.append(ival("headsT", BF * (T * jc + 128 * tt),
                                            BF * (T * jc + 128 * tt + 128)))
                            rds.append(ival("ln1T", BF * (T * jc + 512 * oc),
                                            BF * (T * jc + 512 * oc + w)))
                    em.op("pe", mms, reads=rds, writes=[bankiv(2 * g, noc)])
                    for oc2 in range(noc):
                        w = min(512, hw - 512 * oc2)
                        stat_chunks.append(
                            (psum[:, 512 * (2 * g + oc2):
                                  512 * (2 * g + oc2) + w],
                             ival(PS, 2048 * (2 * g + oc2),
                                  2048 * (2 * g + oc2) + 4 * w)))
                rstd, nbias, stiv = ln_stats(stat_chunks)
                iv_x = ival("qr", 0, D * 4)
                qld.issue(xa_f32, dx.ap()[b, 128 * tt:128 * tt + 128, :],
                          writes=[iv_x])
                iv_x1 = ival("kr", 0, D * 4)
                nrm = []
                rdn = [stiv]
                for hf in range(nhalf):
                    g = groups[hf]
                    hw = min(1024, D - 1024 * hf)
                    noc = (hw + 511) // 512
                    em.op("act", lambda h2, r=rstd, nb=nbias, g=g, hf=hf,
                          hw=hw:
                          h2.activation(xb_f32[:, 1024 * hf:1024 * hf + hw],
                                        psum[:, 512 * 2 * g:512 * 2 * g + hw],
                                        AF.Identity, bias=nb, scale=r),
                          reads=[bankiv(2 * g, noc), stiv], writes=[iv_x1])
                em.op("dve", lambda h2:
                      h2.tensor_tensor(xb_f32, xb_f32, xa_f32, ALU.add),
                      reads=[iv_x, iv_x1], writes=[iv_x1])
                qst.issue(x1s.ap()[b, 128 * tt:128 * tt + 128, :], xb_f32,
                          reads=[iv_x1], writes=[ival("x1s")])
                rstd3, nbias3, stiv3 = ln_stats(
                    sbuf_chunks(lambda c0, w: xb_f32[:, c0:c0 + w], "kr", 0,
                                D, 4))
                iv_l3 = ival("tmp", 0, D * BF)
                em.op("act", lambda h2, r=rstd3, nb=nbias3:
                      h2.activation(tmp[:, 0:D], xb_f32, AF.Identity,
                                    bias=nb, scale=r),
                      reads=[iv_x1, stiv3], writes=[iv_l3])
                qst.issue(ln3b.ap()[b, 128 * tt:128 * tt + 128, :], tmp[:, 0:D],
                          reads=[iv_l3], writes=[ival("ln3b")])
                do_chore(14.0)

        TBT, TBW = C.TBT, C.TB

        free_regions = [("ln1T", 0, IC * T), ("headsT", 0, H * T)]
        cur = {"ri": 0, "off": 0}

        def take(ncols):
            while True:
                name, base, cap = free_regions[cur["ri"]]
                if cur["off"] + ncols <= cap:
                    out = (name, base + cur["off"])
                    cur["off"] += ncols
                    return out
                cur["ri"] += 1
                cur["off"] = 0
                assert cur["ri"] < len(free_regions), "phase-D SBUF overflow"

        tens = {"ln1T": ln1T, "headsT": headsT, "pt": pt, "vg": vg,
                "wvs": wvs, "maskS": maskS, "csS": csS, "snS": snS}
        hT_loc = [take(TBW) for _ in range(C.OT)]
        ln3T_loc = [take(TBW) for _ in range(IC)]
        free_regions2 = [("pt", 0, 2048), ("vg", TBW, TT * C.HG * 128),
                         ("wvs", IC * 128, IC * C.HG * 128),
                         ("maskS", 0, 2048), ("csS", 0, T), ("snS", 0, T),
                         free_regions[cur["ri"]][:1] + (
                             free_regions[cur["ri"]][1] + cur["off"],
                             free_regions[cur["ri"]][2])]
        cur2 = {"ri": 0, "off": 0}

        def take2(ncols):
            while True:
                name, base, cap = free_regions2[cur2["ri"]]
                if base + cur2["off"] + ncols <= cap:
                    out = (name, base + cur2["off"])
                    cur2["off"] += ncols
                    return out
                cur2["ri"] += 1
                cur2["off"] = 0
                assert cur2["ri"] < len(free_regions2), "phase-D SBUF overflow2"

        dn_loc = [take2(D) for _ in range(TBT)]

        def view(loc, ncols):
            name, c0 = loc
            t = tens[name]
            return t[:, c0:c0 + ncols], ival(name, BF * c0, BF * (c0 + ncols))

        def ld_ln3T(b, tb, q):
            t0 = tb * TBW
            for ci in range(IC):
                v, iv = view(ln3T_loc[ci], TBW)
                q.issue(v, ln3b.ap()[b, t0:t0 + TBW, 128 * ci:128 * ci + 128],
                        reads=[ival("ln3b")], writes=[iv], transpose=True)

        def phase_D(b, tb, first, nxt):
            t0 = tb * TBW
            if first:
                ld_ln3T(b, tb, qld)
            nxc = TBW // 512
            for ot in range(C.OT):
                if b == 0 and tb == 0:
                    ensure_wgu_cols(min(C.INTER, 128 * ot + 128))
                iv_wg = ival("wqk")
                qld.issue(wqk[:, :].rearrange("p (c o) -> p c o", o=128),
                          wg_f.ap()[:, 128 * ot:128 * ot + 128]
                          .rearrange("(c p) o -> p c o", p=128),
                          reads=[ival("wg_f", 128 * ot * BF,
                                       (128 * ot + 128) * BF)],
                          writes=[iv_wg])
                iv_wu = ival("wvs", 0, IC * 128 * BF)
                qld.issue(wvs[:, 0:IC * 128].rearrange("p (c o) -> p c o", o=128),
                          wu_f.ap()[:, 128 * ot:128 * ot + 128]
                          .rearrange("(c p) o -> p c o", p=128),
                          reads=[ival("wu_f", 128 * ot * BF,
                                       (128 * ot + 128) * BF)],
                          writes=[iv_wu])
                for (iv_w, wt, pb) in ((iv_wg, wqk, 0), (iv_wu, wvs, 2)):
                    mms, rds = [], [iv_w]
                    for xc in range(nxc):
                        for ci in range(IC):
                            v3, iv3 = view(ln3T_loc[ci], TBW)
                            mms.append(
                                lambda h2, ci=ci, xc=xc, wt=wt, pb=pb, v3=v3:
                                h2.matmul(bank(pb + xc),
                                          wt[:, 128 * ci:128 * ci + 128],
                                          v3[:, 512 * xc:512 * xc + 512],
                                          start=(ci == 0), stop=(ci == IC - 1)))
                            rds.append(iv3)
                    em.op("pe", mms, reads=rds, writes=[bankiv(pb, nxc)])
                hv, hiv = view(hT_loc[ot], TBW)
                iv_sg = ival("vg", 0, TBW * BF)
                em.op("act", lambda h2:
                      h2.activation(vg[:, 0:TBW], bank(0, TBW), AF.Silu),
                      reads=[bankiv(0, nxc)], writes=[iv_sg])
                em.op("dve", lambda h2, hv=hv:
                      h2.tensor_tensor(hv, vg[:, 0:TBW], bank(2, TBW), ALU.mult),
                      reads=[iv_sg, bankiv(2, nxc)], writes=[hiv])
                do_chore(12.0)
            if nxt is not None:
                ld_ln3T(nxt[0], nxt[1], qst)
            warm_state["on"] = False
            for dc in range(C.DC):
                w = min(512, D - 512 * dc)
                for oc in range(C.OT):
                    if b == 0 and tb == 0:
                        ensure_wd_rows(min(C.INTER, 128 * oc + 128))
                    par = oc % min(4, T // 512)
                    iv_wd = ival("tmp", 512 * par * BF, (512 * par + w) * BF)
                    qld.issue(tmp[:, 512 * par:512 * par + w],
                              wd_f.ap()[128 * oc:128 * oc + 128,
                                        512 * dc:512 * dc + w],
                              reads=[ival("wd_f", 128 * oc * D * BF,
                                           (128 * oc + 128) * D * BF)],
                              writes=[iv_wd])
                    do_chore(3.0)
                    mms, rds = [], [iv_wd]
                    for t8 in range(TBT):
                        hv, hiv = view(hT_loc[oc], TBW)
                        mms.append(lambda h2, t8=t8, par=par, w=w, hv=hv,
                                   oc=oc:
                                   h2.matmul(bank(t8, w),
                                             hv[:, 128 * t8:128 * t8 + 128],
                                             tmp[:, 512 * par:512 * par + w],
                                             start=(oc == 0),
                                             stop=(oc == C.OT - 1)))
                        rds.append(hiv)
                    em.op("pe", mms, reads=rds,
                          writes=[bankiv(t8) for t8 in range(TBT)])
                for t8 in range(TBT):
                    dv, div = view(dn_loc[t8], D)
                    em.op("act", lambda h2, t8=t8, dc=dc, w=w, dv=dv:
                          h2.activation(dv[:, 512 * dc:512 * dc + w],
                                        bank(t8, w), AF.Identity),
                          reads=[bankiv(t8)],
                          writes=[(div[0], div[1] + BF * 512 * dc,
                                   div[1] + BF * (512 * dc + w))])
            warm_state["on"] = True
            for t8 in range(TBT):
                dv, div = view(dn_loc[t8], D)
                rstd4, nbias4, stiv4 = ln_stats(
                    sbuf_chunks(lambda c0, w2, dv=dv: dv[:, c0:c0 + w2],
                                div[0], div[1], D, BF))
                iv_x1 = ival("qr", 0, D * 4)
                qld.issue(xa_f32,
                          x1s.ap()[b, t0 + 128 * t8:t0 + 128 * t8 + 128, :],
                          reads=[ival("x1s")], writes=[iv_x1])
                iv_o = ival("kr", 0, D * 4)
                em.op("act", lambda h2, r=rstd4, nb=nbias4, dv=dv:
                      h2.activation(xb_f32, dv, AF.Identity, bias=nb, scale=r),
                      reads=[div, stiv4], writes=[iv_o])
                em.op("dve", lambda h2:
                      h2.tensor_tensor(xb_f32, xb_f32, xa_f32, ALU.add),
                      reads=[iv_x1, iv_o], writes=[iv_o])
                qst.issue(dy.ap()[b, t0 + 128 * t8:t0 + 128 * t8 + 128, :],
                          xb_f32, reads=[iv_o], writes=[ival("y")])

        for b in range(C.BPC):
            phase_A(b)
            phase_B(b)
            phase_C(b)
        bts = [(b, tb) for b in range(C.BPC) for tb in range(C.NTB)]
        for i, (b, tb) in enumerate(bts):
            phase_D(b, tb, i == 0,
                    bts[i + 1] if i + 1 < len(bts) else None)
        finish_chores()

        qst.final_waits("sp")
        qcast.final_waits("sp")
        qld.final_waits("sp")

        em.finalize(e(nc.Block()))
    return nc



def _host_consts(C: Cfg):
    j = np.arange(C.DH // 2, dtype=np.float64)
    theta = C.rope_base ** (-2.0 * j / C.DH)
    t = np.arange(C.T, dtype=np.float64)
    ang = np.outer(theta, t)
    cosT = np.concatenate([np.cos(ang), np.cos(ang)], axis=0).astype(np.float32)
    sinT = np.concatenate([-np.sin(ang), np.sin(ang)], axis=0).astype(np.float32)
    permT = np.zeros((128, 128), dtype=np.float32)
    for m in range(64):
        permT[m + 64, m] = 1.0
        permT[m, m + 64] = 1.0
    k = np.arange(128)[:, None]
    q = np.arange(512)[None, :]
    maskc = np.concatenate(
        [(q >= k + 128 * a).astype(np.float32) for a in range(4)], axis=1)
    return cosT, sinT, permT, maskc


def _deinterleave_cols(W_T, H, DH):
    perm = np.concatenate([h * DH + np.concatenate(
        [np.arange(0, DH, 2), np.arange(1, DH, 2)]) for h in range(H)])
    return np.ascontiguousarray(W_T[:, perm])


def _head_major(WT, W, D):
    IC = D // 128
    G = D // W
    return np.ascontiguousarray(
        WT.reshape(IC, 128, G, W).transpose(2, 1, 0, 3).reshape(G, 128, IC * W))


def make_in_maps(C, X, Wq, Wk, Wv, Wo, lut, gidx, uidx, didx):
    cosT, sinT, permT, maskc = _host_consts(C)
    wqT = _head_major(_deinterleave_cols(
        np.ascontiguousarray(Wq.T), C.H, C.DH), 128, C.D)
    wkT = _head_major(_deinterleave_cols(
        np.ascontiguousarray(Wk.T), C.H, C.DH), 128, C.D)
    wvT = _head_major(np.ascontiguousarray(Wv.T), 2 * 128, C.D)
    woT = np.ascontiguousarray(Wo.T)
    gT = np.ascontiguousarray(gidx.T)
    uT = np.ascontiguousarray(uidx.T)
    dT = np.ascontiguousarray(didx.T)
    in_maps = []
    for c in range(C.n_cores):
        in_maps.append({
            "x": np.ascontiguousarray(X[c * C.BPC:(c + 1) * C.BPC]),
            "wqT": wqT, "wkT": wkT, "wvT": wvT, "woT": woT,
            "lut": np.ascontiguousarray(lut),
            "gidxT": gT, "uidxT": uT, "didxT": dT,
            "csT": cosT, "snT": sinT, "permT": permT, "maskc": maskc,
        })
    return in_maps


def kernel(L0_attn_X, L0_attn_Wq, L0_attn_Wk, L0_attn_Wv, L0_attn_Wo,
           L0_ffn_lut, L0_ffn_gate_idx, L0_ffn_up_idx, L0_ffn_down_idx):
    C = Cfg()
    X = np.asarray(L0_attn_X, dtype=np.float32)
    in_maps = make_in_maps(C, X,
                           np.asarray(L0_attn_Wq, dtype=np.float32),
                           np.asarray(L0_attn_Wk, dtype=np.float32),
                           np.asarray(L0_attn_Wv, dtype=np.float32),
                           np.asarray(L0_attn_Wo, dtype=np.float32),
                           np.asarray(L0_ffn_lut, dtype=np.float32),
                           np.asarray(L0_ffn_gate_idx, dtype=np.int32),
                           np.asarray(L0_ffn_up_idx, dtype=np.int32),
                           np.asarray(L0_ffn_down_idx, dtype=np.int32))
    nc = build_program(C)
    res = run_bass_kernel_spmd(nc, in_maps, core_ids=list(range(C.n_cores)))
    return np.concatenate([r["y"] for r in res.results], axis=0)


# version 26
# speedup vs baseline: 1.1614x; 1.0995x over previous
import numpy as np
import ml_dtypes
from contextlib import ExitStack

import concourse.bass as bass
import concourse.mybir as mybir
from concourse.bass_utils import run_bass_kernel_spmd

F32 = mybir.dt.float32
BF16 = mybir.dt.bfloat16
I32 = mybir.dt.int32
AF = mybir.ActivationFunctionType
ALU = mybir.AluOpType

BF = 2


class Cfg:
    def __init__(self, n_cores=8, batch=16, T=2048, D=2048, H=16, INTER=5504,
                 LUT=16, tblock=1024, eps=1e-5, rope_base=10000.0):
        self.n_cores = n_cores
        self.batch = batch
        self.T = T
        self.D = D
        self.H = H
        self.DH = D // H
        assert self.DH == 128
        self.INTER = INTER
        self.LUT = LUT
        self.TB = tblock
        self.eps = eps
        self.rope_base = rope_base
        self.BPC = batch // n_cores
        self.IC = D // 128
        self.TT = T // 128
        self.QT = T // 512
        self.OT = INTER // 128
        self.DC = (D + 511) // 512
        assert D % 128 == 0 and INTER % 128 == 0 and T % 512 == 0
        assert D % n_cores == 0 and INTER % n_cores == 0 and T % tblock == 0
        self.gslice = D // n_cores
        self.dslice = INTER // n_cores
        self.scale = self.DH ** -0.5
        self.HG = 2
        assert H % self.HG == 0
        self.NTB = T // tblock
        self.TBT = tblock // 128
        assert self.TBT <= 8
        assert D * 4 <= 2 * T * BF


ENGINES = ("pe", "dve", "act", "pool", "sp")


class Em:

    def __init__(self, nc, stack):
        self.nc = nc
        self.stack = stack
        self.ops = {e: [] for e in ENGINES}
        self.cnt = {e: 0 for e in ENGINES}
        self.esem = {e: stack.enter_context(nc.semaphore(f"es_{e}")) for e in ENGINES}
        self.vc = {e: {} for e in ENGINES}
        self.self_waited = {e: 0 for e in ENGINES}
        self.snap = {e: {} for e in ENGINES}
        self.track = {}
        self.queues = {}
        self.n_waits = 0

    def _merge(self, eng, evvc):
        vc = self.vc[eng]
        for k, v in evvc.items():
            if vc.get(k, 0) < v:
                vc[k] = v

    def await_ev(self, eng, ev):
        axis, n = ev
        if axis == eng:
            if eng == "pe":
                return
            if self.self_waited[eng] < n:
                sem = self.esem[eng]
                self.ops[eng].append(lambda h, s=sem, v=n: h.wait_ge(s, v))
                self.self_waited[eng] = n
                self.n_waits += 1
            return
        if self.vc[eng].get(axis, 0) >= n:
            return
        if axis in ENGINES:
            sem, val = self.esem[axis], n
        else:
            qname, slot = axis
            q = self.queues[qname]
            sem, val = q.sems[slot], 16 * n
        self.ops[eng].append(lambda h, s=sem, v=val: h.wait_ge(s, v))
        self.n_waits += 1
        self._merge(eng, self.snap[axis][n])
        if self.vc[eng].get(axis, 0) < n:
            self.vc[eng][axis] = n

    def _t(self, name):
        if name not in self.track:
            self.track[name] = {"w": [], "r": []}
        return self.track[name]

    @staticmethod
    def _ov(lo, hi, rec):
        return rec[0] < hi and lo < rec[1]

    def _deps_for(self, reads, writes):
        deps = []
        for (name, lo, hi) in reads:
            t = self._t(name)
            deps += [rec[2] for rec in t["w"] if self._ov(lo, hi, rec)]
        for (name, lo, hi) in writes:
            t = self._t(name)
            deps += [rec[2] for rec in t["w"] if self._ov(lo, hi, rec)]
            deps += [rec[2] for rec in t["r"] if self._ov(lo, hi, rec)]
        return deps

    def _record(self, reads, writes, ev):
        for (name, lo, hi) in reads:
            self._t(name)["r"].append((lo, hi, ev))
        for (name, lo, hi) in writes:
            t = self._t(name)
            t["w"] = [rec for rec in t["w"] if not (rec[0] >= lo and rec[1] <= hi)]
            t["r"] = [rec for rec in t["r"] if not (rec[0] >= lo and rec[1] <= hi)]
            t["w"].append((lo, hi, ev))

    def _snapshot(self, eng, axis, n):
        vcs = dict(self.vc[eng])
        vcs[eng] = self.cnt[eng]
        vcs[axis] = n
        self.snap[axis][n] = vcs

    def op(self, eng, fns, reads=(), writes=()):
        if not isinstance(fns, (list, tuple)):
            fns = [fns]
        for ev in self._deps_for(reads, writes):
            self.await_ev(eng, ev)
        n = self.cnt[eng] + 1
        self.cnt[eng] = n
        sem = self.esem[eng]
        for f in fns[:-1]:
            self.ops[eng].append(f)
        f = fns[-1]
        self.ops[eng].append(lambda h, f=f, s=sem: f(h).then_inc(s, 1))
        self.vc[eng][eng] = n
        self._snapshot(eng, eng, n)
        ev = (eng, n)
        self._record(reads, writes, ev)
        return ev

    def finalize(self, block):
        hmap = {"pe": block.tensor, "dve": block.vector, "act": block.scalar,
                "pool": block.gpsimd, "sp": block.sync}
        for e in ENGINES:
            lst = self.ops[e]

            def body(h, lst=lst):
                for f in lst:
                    f(h)
            hmap[e](body)


class DmaQ:

    def __init__(self, em, eng, name, K=8):
        self.em = em
        self.eng = eng
        self.name = name
        self.K = K
        self.sems = [em.stack.enter_context(em.nc.semaphore(f"q_{name}{i}"))
                     for i in range(K)]
        self.n = 0
        em.queues[name] = self
        for i in range(K):
            em.snap[(name, i)] = {}

    def _emit(self, fn, reads, writes):
        em, eng = self.em, self.eng
        n = self.n + 1
        slot = (n - 1) % self.K
        occ = (n - 1) // self.K + 1
        axis = (self.name, slot)
        if occ > 1:
            em.await_ev(eng, (axis, occ - 1))
        for ev in em._deps_for(reads, writes):
            em.await_ev(eng, ev)
        sem = self.sems[slot]
        em.ops[eng].append(lambda h, fn=fn, s=sem: fn(h).then_inc(s, 16))
        self.n = n
        em._snapshot(eng, axis, occ)
        ev = (axis, occ)
        em._record(reads, writes, ev)
        return ev

    def issue(self, out_ap, in_ap, reads=(), writes=(), **kw):
        return self._emit(
            lambda h, o=out_ap, i=in_ap, kw=kw: h.dma_start(out=o, in_=i, **kw),
            reads, writes)

    def issue_fn(self, fn, reads=(), writes=()):
        return self._emit(fn, reads, writes)

    def final_waits(self, eng):
        em = self.em
        counts = [0] * self.K
        for j in range(1, self.n + 1):
            counts[(j - 1) % self.K] += 16
        for s in range(self.K):
            if counts[s]:
                em.ops[eng].append(
                    lambda h, sm=self.sems[s], v=counts[s]: h.wait_ge(sm, v))


def ival(name, lo=0, hi=1 << 40):
    return (name, lo, hi)


def build_program(cfg: Cfg):
    nc = bass.Bass("TRN2", target_bir_lowering=False)
    C = cfg
    T, D, H, IC, TT, QT = C.T, C.D, C.H, C.IC, C.TT, C.QT

    dx = nc.dram_tensor("x", [C.BPC, T, D], F32, kind="ExternalInput")
    dwq = nc.dram_tensor("wqT", [H, 128, D], F32, kind="ExternalInput")
    dwk = nc.dram_tensor("wkT", [H, 128, D], F32, kind="ExternalInput")
    dwv = nc.dram_tensor("wvT", [H // 2, 128, 2 * D], F32, kind="ExternalInput")
    dwo = nc.dram_tensor("woT", [D, D], F32, kind="ExternalInput")
    dlut = nc.dram_tensor("lut", [C.LUT], F32, kind="ExternalInput")
    dgid = nc.dram_tensor("gidxT", [D, C.INTER], I32, kind="ExternalInput")
    duid = nc.dram_tensor("uidxT", [D, C.INTER], I32, kind="ExternalInput")
    ddid = nc.dram_tensor("didxT", [C.INTER, D], I32, kind="ExternalInput")
    dcs = nc.dram_tensor("csT", [128, T], F32, kind="ExternalInput")
    dsn = nc.dram_tensor("snT", [128, T], F32, kind="ExternalInput")
    dperm = nc.dram_tensor("permT", [128, 128], F32, kind="ExternalInput")
    dmask = nc.dram_tensor("maskc", [128, 2048], F32, kind="ExternalInput")
    dy = nc.dram_tensor("y", [C.BPC, T, D], F32, kind="ExternalOutput")

    wq_bf = nc.dram_tensor("wq_bf", [H, 128, D], BF16)
    wk_bf = nc.dram_tensor("wk_bf", [H, 128, D], BF16)
    wv_bf = nc.dram_tensor("wv_bf", [H // 2, 128, 2 * D], BF16)
    wo_bf = nc.dram_tensor("wo_bf", [D, D], BF16)
    ln1b = nc.dram_tensor("ln1b", [T, D], BF16)
    srd = nc.dram_tensor("srd", [512], F32)
    ln3b = nc.dram_tensor("ln3b", [C.BPC, T, D], BF16)
    x1s = nc.dram_tensor("x1s", [C.BPC, T, D], F32)
    wg_f = nc.dram_tensor("wg_f", [D, C.INTER], BF16)
    wu_f = nc.dram_tensor("wu_f", [D, C.INTER], BF16)
    wd_f = nc.dram_tensor("wd_f", [C.INTER, D], BF16)

    with ExitStack() as stack:
        e = stack.enter_context
        ln1T = e(nc.sbuf_tensor("ln1T", [128, IC * T], BF16))
        headsT = e(nc.sbuf_tensor("headsT", [128, H * T], BF16))
        vg = e(nc.sbuf_tensor("vg", [128, TT * C.HG * 128], BF16))
        pt = e(nc.sbuf_tensor("pt", [128, 2048], BF16))
        qraw = e(nc.sbuf_tensor("qraw", [128, T], BF16))
        kraw = e(nc.sbuf_tensor("kraw", [128, T], BF16))
        wvs = e(nc.sbuf_tensor("wvs", [128, IC * C.HG * 128], BF16))
        wqk = e(nc.sbuf_tensor("wqk", [128, IC * 128], BF16))
        qr = e(nc.sbuf_tensor("qr", [128, 2 * T], BF16))
        kr = e(nc.sbuf_tensor("kr", [128, 2 * T], BF16))
        tmp = e(nc.sbuf_tensor("tmp", [128, T], BF16))
        csS = e(nc.sbuf_tensor("csS", [128, T], BF16))
        snS = e(nc.sbuf_tensor("snS", [128, T], BF16))
        permS = e(nc.sbuf_tensor("permS", [128, 128], BF16))
        maskS = e(nc.sbuf_tensor("maskS", [128, 2048], BF16))
        onesS = e(nc.sbuf_tensor("onesS", [128, 2], BF16))
        rows = e(nc.sbuf_tensor("rows", [128, 128], F32))
        srow = e(nc.sbuf_tensor("srow", [1, 512], F32))
        bcbuf = e(nc.sbuf_tensor("bcbuf", [128, 512], F32))
        lutS = e(nc.sbuf_tensor("lutS", [128, C.LUT], F32))
        bixb = e(nc.sbuf_tensor("bixb", [128, 1536], BF16))
        bacc = e(nc.sbuf_tensor("bacc", [128, 768], BF16))
        bsel = e(nc.sbuf_tensor("bsel", [128, 768], BF16))
        bac2 = e(nc.sbuf_tensor("bac2", [128, 768], BF16))
        bse2 = e(nc.sbuf_tensor("bse2", [128, 768], BF16))
        psum = e(nc.psum_tensor("psum", [128, 4096], F32))

        em = Em(nc, stack)
        qld = DmaQ(em, "sp", "qld", 8)
        qst = DmaQ(em, "act", "qst", 8)
        qcast = DmaQ(em, "pool", "qcast", 8)

        PS = "psum"

        def bank(i, w=512):
            return psum[:, 512 * i:512 * i + w]

        def bankiv(i, nb=1):
            return ival(PS, 2048 * i, 2048 * (i + nb))

        xa_f32 = qr.ap().bitcast(F32)[:, 0:D]
        xb_f32 = kr.ap().bitcast(F32)[:, 0:D]

        qcast.issue(csS[:, :], dcs.ap(), writes=[ival("csS")])
        qcast.issue(snS[:, :], dsn.ap(), writes=[ival("snS")])
        qcast.issue(permS[:, :], dperm.ap(), writes=[ival("permS")])
        qcast.issue(maskS[:, :], dmask.ap(), writes=[ival("maskS")])
        qcast.issue(lutS[:, :], dlut.ap().unsqueeze(0).partition_broadcast(128),
                    writes=[ival("lutS")])
        em.op("dve", lambda h: h.memset(onesS[:, :], 1.0), writes=[ival("onesS")])
        qcast.issue(wo_bf.ap(), dwo.ap(), writes=[ival("wo_bf")])
        qcast.issue(wq_bf.ap(), dwq.ap(), writes=[ival("wq_bf")])
        qcast.issue(wk_bf.ap(), dwk.ap(), writes=[ival("wk_bf")])
        qcast.issue(wv_bf.ap(), dwv.ap(), writes=[ival("wv_bf")])

        CW = 768

        def strip_chunks():
            out = []
            c0 = 0
            while c0 < C.INTER:
                w = min(CW, C.INTER - c0)
                for (didx_, dcon) in ((dgid, wg_f), (duid, wu_f)):
                    r0 = 0
                    while r0 < D:
                        p = min(128, D - r0)
                        out.append((didx_, dcon, r0, p, c0, w))
                        r0 += p
                c0 += w
            r0 = 0
            while r0 < C.INTER:
                p = min(128, C.INTER - r0)
                c0 = 0
                while c0 < D:
                    w = min(CW, D - c0)
                    out.append((ddid, wd_f, r0, p, c0, w))
                    c0 += w
                r0 += p
            return out

        def chore_gen():
            chunks = strip_chunks()

            def issue_load(i):
                didx_, dcon, r0, p, c0, w = chunks[i]
                par = i % 2
                bx = bixb[:, CW * par:CW * par + CW]
                iv_b = ival("bixb", CW * par * BF, (CW * par + CW) * BF)
                qcast.issue(bx[:p, :w], didx_.ap()[r0:r0 + p, c0:c0 + w],
                            writes=[iv_b])

            issue_load(0)
            for i, (didx_, dcon, r0, p, c0, w) in enumerate(chunks):
                par = i % 2
                bx = bixb[:, CW * par:CW * par + CW]
                iv_b = ival("bixb", CW * par * BF, (CW * par + CW) * BF)
                iv_a, iv_s = ival("bacc"), ival("bsel")
                iv_a2, iv_s2 = ival("bac2"), ival("bse2")

                def ts_(j, dst, iv_d):
                    em.op("dve", lambda h, p=p, w=w, j=j, dst=dst, bx=bx:
                          h.tensor_scalar(dst[:p, :w], bx[:p, :w], float(j),
                                          lutS[:p, j:j + 1],
                                          ALU.is_equal, ALU.mult),
                          reads=[iv_b, ival("lutS")], writes=[iv_d])

                def tt_(acc, stg, iv_acc, iv_stg):
                    em.op("dve", lambda h, p=p, w=w, acc=acc, stg=stg:
                          h.tensor_tensor(acc[:p, :w], acc[:p, :w],
                                          stg[:p, :w], ALU.add),
                          reads=[iv_stg, iv_acc], writes=[iv_acc])

                ts_(0, bacc, iv_a)
                ts_(8, bac2, iv_a2)
                ts_(1, bsel, iv_s)
                ts_(9, bse2, iv_s2)
                if i + 1 < len(chunks):
                    issue_load(i + 1)
                yield 1.1
                for k in range(1, 7):
                    tt_(bacc, bsel, iv_a, iv_s)
                    tt_(bac2, bse2, iv_a2, iv_s2)
                    ts_(k + 1, bsel, iv_s)
                    ts_(k + 9, bse2, iv_s2)
                    yield 1.2
                tt_(bacc, bsel, iv_a, iv_s)
                tt_(bac2, bse2, iv_a2, iv_s2)
                em.op("dve", lambda h, p=p, w=w:
                      h.tensor_tensor(bacc[:p, :w], bacc[:p, :w],
                                      bac2[:p, :w], ALU.add),
                      reads=[iv_a2, iv_a], writes=[iv_a])
                yield 1.1
                if dcon is wd_f:
                    ivw = (dcon.name, r0 * D * BF, (r0 + p) * D * BF)
                else:
                    ivw = (dcon.name, c0 * BF, (c0 + w) * BF)
                qcast.issue(dcon.ap()[r0:r0 + p, c0:c0 + w], bacc[:p, :w],
                            reads=[iv_a], writes=[ivw])
                if dcon is wd_f:
                    if c0 + w >= D:
                        chore_state["wd_rows"] = r0 + p
                elif dcon is wu_f and r0 + p >= D:
                    chore_state["wgu_cols"] = c0 + w
                chore_state["chunks"] += 1
                yield 0.3
        chore_state = {"done": False, "chunks": 0, "wgu_cols": 0, "wd_rows": 0}
        chore_state["gen"] = chore_gen()

        def do_chore(budget_us):
            while budget_us > 0 and not chore_state["done"]:
                try:
                    budget_us -= next(chore_state["gen"])
                except StopIteration:
                    chore_state["done"] = True

        def ensure_wgu_cols(n):
            while chore_state["wgu_cols"] < n and not chore_state["done"]:
                try:
                    next(chore_state["gen"])
                except StopIteration:
                    chore_state["done"] = True

        def ensure_wd_rows(n):
            while chore_state["wd_rows"] < n and not chore_state["done"]:
                try:
                    next(chore_state["gen"])
                except StopIteration:
                    chore_state["done"] = True

        def finish_chores():
            do_chore(1e9)




        slot_state = {"i": 0}

        def ln_stats(chunks):
            sl = 32 * (slot_state["i"] % 3)
            slot_state["i"] += 1
            nch = len(chunks)
            stat_iv = ival("rows", sl * 4, (sl + 8 + 6 * nch) * 4)
            sc = sl + 8
            for i, (ap, iv) in enumerate(chunks):
                em.op("dve", lambda h, ap=ap, i=i, sc=sc:
                      h.bn_stats(rows[:, sc + 6 * i: sc + 6 * i + 6], ap),
                      reads=[iv], writes=[stat_iv])
            em.op("dve", lambda h, sl=sl, sc=sc, nch=nch:
                  h.bn_aggr(rows[:, sl:sl + 2], rows[:, sc:sc + 6 * nch]),
                  reads=[stat_iv], writes=[stat_iv])
            em.op("dve", lambda h, sl=sl:
                  h.tensor_scalar(rows[:, sl + 2:sl + 3], rows[:, sl + 1:sl + 2],
                                  C.eps, None, ALU.add),
                  reads=[stat_iv], writes=[stat_iv])
            em.op("act", lambda h, sl=sl:
                  h.activation(rows[:, sl + 3:sl + 4], rows[:, sl + 2:sl + 3],
                               AF.Ln),
                  reads=[stat_iv], writes=[stat_iv])
            em.op("act", lambda h, sl=sl:
                  h.activation(rows[:, sl + 4:sl + 5], rows[:, sl + 3:sl + 4],
                               AF.Exp, scale=-0.5),
                  reads=[stat_iv], writes=[stat_iv])
            em.op("dve", lambda h, sl=sl:
                  h.tensor_tensor(rows[:, sl + 5:sl + 6], rows[:, sl:sl + 1],
                                  rows[:, sl + 4:sl + 5], ALU.mult),
                  reads=[stat_iv], writes=[stat_iv])
            em.op("dve", lambda h, sl=sl:
                  h.tensor_scalar(rows[:, sl + 5:sl + 6], rows[:, sl + 5:sl + 6],
                                  -1.0, None, ALU.mult),
                  reads=[stat_iv], writes=[stat_iv])
            return (rows[:, sl + 4:sl + 5], rows[:, sl + 5:sl + 6], stat_iv)

        def sbuf_chunks(ap_maker, name, base_bytes, n_elems, esize):
            out = []
            c0 = 0
            while c0 < n_elems:
                w = min(512, n_elems - c0)
                out.append((ap_maker(c0, w),
                            ival(name, base_bytes + c0 * esize,
                                 base_bytes + (c0 + w) * esize)))
                c0 += w
            return out

        def psum_chunks(bank0, n_elems):
            out = []
            c0 = 0
            while c0 < n_elems:
                w = min(512, n_elems - c0)
                out.append((psum[:, 512 * bank0 + c0: 512 * bank0 + c0 + w],
                            ival(PS, 2048 * bank0 + 4 * c0,
                                 2048 * bank0 + 4 * (c0 + w))))
                c0 += w
            return out

        def phase_A(b):
            for tt in range(TT):
                xap, xnm = (xa_f32, "qr") if tt % 2 == 0 else (xb_f32, "kr")
                iv_x = ival(xnm, 0, D * 4)
                qld.issue(xap, dx.ap()[b, 128 * tt:128 * tt + 128, :],
                          writes=[iv_x])
                ch = sbuf_chunks(lambda c0, w, xap=xap: xap[:, c0:c0 + w],
                                 xnm, 0, D, 4)
                rstd, nbias, stiv = ln_stats(ch)
                lnb, lnm = (tmp, "tmp") if tt % 2 == 0 else (pt, "pt")
                iv_l = ival(lnm, 0, D * BF)
                em.op("act", lambda h, xap=xap, r=rstd, nb=nbias, lnb=lnb:
                      h.activation(lnb[:, 0:D], xap, AF.Identity, bias=nb,
                                   scale=r),
                      reads=[iv_x, stiv], writes=[iv_l])
                qst.issue(ln1b.ap()[128 * tt:128 * tt + 128, :], lnb[:, 0:D],
                          reads=[iv_l],
                          writes=[ival("ln1b", 128 * tt * D * BF,
                                        (128 * tt + 128) * D * BF)])
                do_chore(4.0)

        def load_ln1T():
            for ci in range(IC):
                qld.issue(ln1T[:, T * ci:T * ci + T],
                          ln1b.ap()[:, 128 * ci:128 * ci + 128],
                          reads=[ival("ln1b")],
                          writes=[ival("ln1T", T * BF * ci, T * BF * (ci + 1))],
                          transpose=True)

        def produce_qk(b, h, which):
            dw = wq_bf if which == "q" else wk_bf
            iv_w = ival("wqk")
            qld.issue(wqk[:, :], dw.ap()[h],
                      reads=[ival(dw.name, h * 128 * D * BF,
                                  (h + 1) * 128 * D * BF)],
                      writes=[iv_w])
            raw = qraw if which == "q" else kraw
            rawn = "qraw" if which == "q" else "kraw"
            for tc in range(T // 512):
                pb = tc % 2
                mms, rds = [], [iv_w]
                for ci in range(IC):
                    mms.append(lambda h2, ci=ci, tc=tc, pb=pb:
                               h2.matmul(bank(pb), wqk[:, 128 * ci:128 * ci + 128],
                                         ln1T[:, T * ci + 512 * tc:
                                              T * ci + 512 * tc + 512],
                                         start=(ci == 0), stop=(ci == IC - 1)))
                    rds.append(ival("ln1T", BF * (T * ci + 512 * tc),
                                    BF * (T * ci + 512 * tc + 512)))
                em.op("pe", mms, reads=rds, writes=[bankiv(pb)])
                em.op("act", lambda h2, tc=tc, pb=pb, raw=raw:
                      h2.activation(raw[:, 512 * tc: 512 * tc + 512],
                                    bank(pb), AF.Identity),
                      reads=[bankiv(pb)],
                      writes=[ival(rawn, BF * 512 * tc, BF * (512 * tc + 512))])
            return rawn

        def rope(rawn, dst, dst_name, par):
            src = qraw if rawn == "qraw" else kraw
            stg = kraw if rawn == "qraw" else qraw
            stgn = "kraw" if rawn == "qraw" else "qraw"
            iv_src = ival(rawn)
            iv_stg = ival(stgn)
            dstv = dst[:, T * par:T * par + T]
            iv_dst = ival(dst_name, T * BF * par, T * BF * (par + 1))
            iv_t = ival("tmp", 0, T * BF)
            nh = T // 1024
            em.op("dve", lambda h: h.tensor_tensor(tmp[:, 0:T], src[:, 0:T],
                                                   csS[:, :], ALU.mult),
                  reads=[iv_src, ival("csS")], writes=[iv_t])
            for hb in range(nh):
                mms = []
                for tc in range(2):
                    mms.append(lambda h, hb=hb, tc=tc:
                               h.matmul(bank(6 + tc),
                                        permS[:, :],
                                        src[:, 1024 * hb + 512 * tc:
                                            1024 * hb + 512 * tc + 512],
                                        start=True, stop=True))
                em.op("pe", mms, reads=[iv_src, ival("permS")],
                      writes=[bankiv(6, 2)])
                em.op("act", lambda h, hb=hb, stg=stg:
                      h.activation(stg[:, 1024 * hb:1024 * hb + 1024],
                                   psum[:, 3072:4096], AF.Identity),
                      reads=[bankiv(6, 2)], writes=[iv_stg])
                em.op("dve", lambda h, hb=hb, dstv=dstv, stg=stg:
                      h.tensor_tensor(dstv[:, 1024 * hb:1024 * hb + 1024],
                                      stg[:, 1024 * hb:1024 * hb + 1024],
                                      snS[:, 1024 * hb:1024 * hb + 1024],
                                      ALU.mult),
                      reads=[iv_stg, ival("snS")], writes=[iv_dst])
            em.op("dve", lambda h, dstv=dstv:
                  h.tensor_tensor(dstv, dstv, tmp[:, 0:T], ALU.add),
                  reads=[iv_t, iv_dst], writes=[iv_dst])

        def produce_group_v(b, g):
            W = C.HG * 128
            iv_w = ival("wvs")
            qld.issue(wvs[:, :IC * W], wv_bf.ap()[g],
                      reads=[ival("wv_bf", g * 128 * 2 * D * BF,
                                  (g + 1) * 128 * 2 * D * BF)],
                      writes=[iv_w])
            for ct in range(TT):
                pb = 2 + (ct % 2)
                mms, rds = [], [iv_w]
                for ci in range(IC):
                    mms.append(lambda h2, ci=ci, ct=ct, pb=pb, W=W:
                               h2.matmul(bank(pb, W),
                                         ln1T[:, T * ci + 128 * ct:
                                              T * ci + 128 * ct + 128],
                                         wvs[:, W * ci: W * ci + W],
                                         start=(ci == 0), stop=(ci == IC - 1)))
                    rds.append(ival("ln1T", BF * (T * ci + 128 * ct),
                                    BF * (T * ci + 128 * ct + 128)))
                em.op("pe", mms, reads=rds, writes=[bankiv(pb)])
                em.op("act", lambda h2, ct=ct, pb=pb, W=W:
                      h2.activation(vg[:, W * ct: W * ct + W], bank(pb, W),
                                    AF.Identity),
                      reads=[bankiv(pb)],
                      writes=[ival("vg", BF * W * ct, BF * W * (ct + 1))])

        def attend(b, h, par):
            hh = h % C.HG
            W = C.HG * 128
            qrh = qr[:, T * par: T * par + T]
            krh = kr[:, T * par: T * par + T]
            iv_qr = ival("qr", T * BF * par, T * BF * (par + 1))
            iv_kr = ival("kr", T * BF * par, T * BF * (par + 1))
            for qt in range(QT):
                nch = 4 * qt + 4
                ngrp = nch // 2
                def emit_st(g):
                    pb = 2 * (g % 2)
                    mms = []
                    for cc in range(2):
                        c = 2 * g + cc
                        mms.append(lambda h2, c=c, cc=cc, qt=qt, pb=pb:
                                   h2.matmul(bank(pb + cc),
                                             krh[:, 128 * c:128 * c + 128],
                                             qrh[:, 512 * qt:512 * qt + 512],
                                             start=True, stop=True))
                    em.op("pe", mms, reads=[iv_qr, iv_kr],
                          writes=[bankiv(pb, 2)])

                def emit_exp(g):
                    pb = 2 * (g % 2)
                    pc0 = 1024 * (g % 2)
                    iv_p = ival("pt", pc0 * BF, (pc0 + 1024) * BF)
                    em.op("act", lambda h2, pc0=pc0, pb=pb:
                          h2.activation(pt[:, pc0:pc0 + 1024], bank(pb, 1024),
                                        AF.Exp, scale=C.scale),
                          reads=[bankiv(pb, 2)], writes=[iv_p])
                    if g >= ngrp - 2:
                        mc0 = 1024 * (g - (ngrp - 2))
                        em.op("dve", lambda h2, pc0=pc0, mc0=mc0:
                              h2.tensor_tensor(pt[:, pc0:pc0 + 1024],
                                               pt[:, pc0:pc0 + 1024],
                                               maskS[:, mc0:mc0 + 1024],
                                               ALU.mult),
                              reads=[ival("maskS"), iv_p], writes=[iv_p])
                    return iv_p

                def emit_pv(g, iv_p):
                    pc0 = 1024 * (g % 2)
                    mms = []
                    for cc in range(2):
                        c = 2 * g + cc
                        mms.append(lambda h2, c=c, cc=cc, pc0=pc0, hh=hh,
                                   W=W, nch=nch:
                                   h2.matmul(bank(4),
                                             vg[:, W * c + 128 * hh:
                                                W * c + 128 * hh + 128],
                                             pt[:, pc0 + 512 * cc:
                                                pc0 + 512 * cc + 512],
                                             start=(c == 0),
                                             stop=(c == nch - 1)))
                    em.op("pe", mms, reads=[ival("vg"), iv_p],
                          writes=[bankiv(4)])
                    mms = []
                    for cc in range(2):
                        c = 2 * g + cc
                        mms.append(lambda h2, c=c, cc=cc, pc0=pc0, nch=nch:
                                   h2.matmul(psum[0:1, 2560:3072],
                                             onesS[:, 0:1],
                                             pt[:, pc0 + 512 * cc:
                                                pc0 + 512 * cc + 512],
                                             start=(c == 0),
                                             stop=(c == nch - 1)))
                    em.op("pe", mms, reads=[ival("onesS"), iv_p],
                          writes=[bankiv(5)])

                emit_st(0)
                iv_prev = emit_exp(0)
                for g in range(1, ngrp):
                    emit_st(g)
                    iv_p = emit_exp(g)
                    emit_pv(g - 1, iv_prev)
                    iv_prev = iv_p
                emit_pv(ngrp - 1, iv_prev)
                do_chore(3.0 * (qt + 1))
                iv_sr = ival("srow")
                em.op("act", lambda h2:
                      h2.activation(srow[0:1, 0:512], psum[0:1, 2560:3072],
                                    AF.Ln),
                      reads=[bankiv(5)], writes=[iv_sr])
                em.op("act", lambda h2:
                      h2.activation(srow[0:1, 0:512], srow[0:1, 0:512],
                                    AF.Exp, scale=-1.0),
                      reads=[iv_sr], writes=[iv_sr])
                qst.issue(srd.ap().unsqueeze(0), srow[0:1, 0:512],
                          reads=[iv_sr], writes=[ival("srd")])
                qst.issue(bcbuf[:, :],
                          srd.ap().unsqueeze(0).partition_broadcast(128),
                          reads=[ival("srd")], writes=[ival("bcbuf")])
                hc0 = BF * (T * h + 512 * qt)
                em.op("dve", lambda h2, h=h, qt=qt:
                      h2.tensor_tensor(headsT[:, T * h + 512 * qt:
                                              T * h + 512 * qt + 512],
                                       bank(4), bcbuf[:, :], ALU.mult),
                      reads=[bankiv(4), ival("bcbuf")],
                      writes=[ival("headsT", hc0, hc0 + 512 * BF)])

        def phase_B(b):
            load_ln1T()
            produce_group_v(b, 0)
            rope(produce_qk(b, 0, "q"), qr, "qr", 0)
            rope(produce_qk(b, 0, "k"), kr, "kr", 0)
            for h in range(H):
                if h + 1 < H:
                    par1 = (h + 1) % 2
                    rope(produce_qk(b, h + 1, "q"), qr, "qr", par1)
                    rope(produce_qk(b, h + 1, "k"), kr, "kr", par1)
                attend(b, h, h % 2)
                if h + 1 < H and (h + 1) % C.HG == 0:
                    produce_group_v(b, (h + 1) // C.HG)
                do_chore(34.0)

        def phase_C(b):
            for ci in range(IC):
                qld.issue(ln1T[:, T * ci: T * ci + D],
                          wo_bf.ap()[128 * ci:128 * ci + 128, :],
                          reads=[ival("wo_bf")],
                          writes=[ival("ln1T", BF * T * ci,
                                       BF * (T * ci + D))])
            nhalf = (D + 1023) // 1024
            rot = {"g": 0}
            for tt in range(TT):
                groups = []
                stat_chunks = []
                for hf in range(nhalf):
                    g = rot["g"]
                    rot["g"] = (g + 1) % 3
                    groups.append(g)
                    hw = min(1024, D - 1024 * hf)
                    noc = (hw + 511) // 512
                    mms, rds = [], []
                    for oc2 in range(noc):
                        w = min(512, hw - 512 * oc2)
                        oc = 2 * hf + oc2
                        for jc in range(IC):
                            mms.append(
                                lambda h2, oc=oc, jc=jc, w=w, tt=tt, g=g,
                                oc2=oc2:
                                h2.matmul(bank(2 * g + oc2, w),
                                          headsT[:, T * jc + 128 * tt:
                                                 T * jc + 128 * tt + 128],
                                          ln1T[:, T * jc + 512 * oc:
                                               T * jc + 512 * oc + w],
                                          start=(jc == 0),
                                          stop=(jc == IC - 1)))
                            rds.append(ival("headsT", BF * (T * jc + 128 * tt),
                                            BF * (T * jc + 128 * tt + 128)))
                            rds.append(ival("ln1T", BF * (T * jc + 512 * oc),
                                            BF * (T * jc + 512 * oc + w)))
                    em.op("pe", mms, reads=rds, writes=[bankiv(2 * g, noc)])
                    for oc2 in range(noc):
                        w = min(512, hw - 512 * oc2)
                        stat_chunks.append(
                            (psum[:, 512 * (2 * g + oc2):
                                  512 * (2 * g + oc2) + w],
                             ival(PS, 2048 * (2 * g + oc2),
                                  2048 * (2 * g + oc2) + 4 * w)))
                rstd, nbias, stiv = ln_stats(stat_chunks)
                iv_x = ival("qr", 0, D * 4)
                qld.issue(xa_f32, dx.ap()[b, 128 * tt:128 * tt + 128, :],
                          writes=[iv_x])
                iv_x1 = ival("kr", 0, D * 4)
                nrm = []
                rdn = [stiv]
                for hf in range(nhalf):
                    g = groups[hf]
                    hw = min(1024, D - 1024 * hf)
                    noc = (hw + 511) // 512
                    em.op("act", lambda h2, r=rstd, nb=nbias, g=g, hf=hf,
                          hw=hw:
                          h2.activation(xb_f32[:, 1024 * hf:1024 * hf + hw],
                                        psum[:, 512 * 2 * g:512 * 2 * g + hw],
                                        AF.Identity, bias=nb, scale=r),
                          reads=[bankiv(2 * g, noc), stiv], writes=[iv_x1])
                em.op("dve", lambda h2:
                      h2.tensor_tensor(xb_f32, xb_f32, xa_f32, ALU.add),
                      reads=[iv_x, iv_x1], writes=[iv_x1])
                qst.issue(x1s.ap()[b, 128 * tt:128 * tt + 128, :], xb_f32,
                          reads=[iv_x1], writes=[ival("x1s")])
                rstd3, nbias3, stiv3 = ln_stats(
                    sbuf_chunks(lambda c0, w: xb_f32[:, c0:c0 + w], "kr", 0,
                                D, 4))
                iv_l3 = ival("tmp", 0, D * BF)
                em.op("act", lambda h2, r=rstd3, nb=nbias3:
                      h2.activation(tmp[:, 0:D], xb_f32, AF.Identity,
                                    bias=nb, scale=r),
                      reads=[iv_x1, stiv3], writes=[iv_l3])
                qst.issue(ln3b.ap()[b, 128 * tt:128 * tt + 128, :], tmp[:, 0:D],
                          reads=[iv_l3], writes=[ival("ln3b")])
                do_chore(22.0)

        TBT, TBW = C.TBT, C.TB

        free_regions = [("ln1T", 0, IC * T), ("headsT", 0, H * T)]
        cur = {"ri": 0, "off": 0}

        def take(ncols):
            while True:
                name, base, cap = free_regions[cur["ri"]]
                if cur["off"] + ncols <= cap:
                    out = (name, base + cur["off"])
                    cur["off"] += ncols
                    return out
                cur["ri"] += 1
                cur["off"] = 0
                assert cur["ri"] < len(free_regions), "phase-D SBUF overflow"

        tens = {"ln1T": ln1T, "headsT": headsT, "pt": pt, "vg": vg,
                "wvs": wvs, "maskS": maskS, "csS": csS, "snS": snS}
        hT_loc = [take(TBW) for _ in range(C.OT)]
        ln3T_loc = [take(TBW) for _ in range(IC)]
        free_regions2 = [("pt", 0, 2048), ("vg", TBW, TT * C.HG * 128),
                         ("wvs", IC * 128, IC * C.HG * 128),
                         ("maskS", 0, 2048), ("csS", 0, T), ("snS", 0, T),
                         free_regions[cur["ri"]][:1] + (
                             free_regions[cur["ri"]][1] + cur["off"],
                             free_regions[cur["ri"]][2])]
        cur2 = {"ri": 0, "off": 0}

        def take2(ncols):
            while True:
                name, base, cap = free_regions2[cur2["ri"]]
                if base + cur2["off"] + ncols <= cap:
                    out = (name, base + cur2["off"])
                    cur2["off"] += ncols
                    return out
                cur2["ri"] += 1
                cur2["off"] = 0
                assert cur2["ri"] < len(free_regions2), "phase-D SBUF overflow2"

        dn_loc = [take2(D) for _ in range(TBT)]

        def view(loc, ncols):
            name, c0 = loc
            t = tens[name]
            return t[:, c0:c0 + ncols], ival(name, BF * c0, BF * (c0 + ncols))

        def ld_ln3T(b, tb, q):
            t0 = tb * TBW
            for ci in range(IC):
                v, iv = view(ln3T_loc[ci], TBW)
                q.issue(v, ln3b.ap()[b, t0:t0 + TBW, 128 * ci:128 * ci + 128],
                        reads=[ival("ln3b")], writes=[iv], transpose=True)

        def phase_D(b, tb, first, nxt):
            t0 = tb * TBW
            if first:
                ld_ln3T(b, tb, qld)
            nxc = TBW // 512
            for ot in range(C.OT):
                if b == 0 and tb == 0:
                    ensure_wgu_cols(min(C.INTER, 128 * ot + 128))
                iv_wg = ival("wqk")
                qld.issue(wqk[:, :].rearrange("p (c o) -> p c o", o=128),
                          wg_f.ap()[:, 128 * ot:128 * ot + 128]
                          .rearrange("(c p) o -> p c o", p=128),
                          reads=[ival("wg_f", 128 * ot * BF,
                                       (128 * ot + 128) * BF)],
                          writes=[iv_wg])
                iv_wu = ival("wvs", 0, IC * 128 * BF)
                qld.issue(wvs[:, 0:IC * 128].rearrange("p (c o) -> p c o", o=128),
                          wu_f.ap()[:, 128 * ot:128 * ot + 128]
                          .rearrange("(c p) o -> p c o", p=128),
                          reads=[ival("wu_f", 128 * ot * BF,
                                       (128 * ot + 128) * BF)],
                          writes=[iv_wu])
                for (iv_w, wt, pb) in ((iv_wg, wqk, 0), (iv_wu, wvs, 2)):
                    mms, rds = [], [iv_w]
                    for xc in range(nxc):
                        for ci in range(IC):
                            v3, iv3 = view(ln3T_loc[ci], TBW)
                            mms.append(
                                lambda h2, ci=ci, xc=xc, wt=wt, pb=pb, v3=v3:
                                h2.matmul(bank(pb + xc),
                                          wt[:, 128 * ci:128 * ci + 128],
                                          v3[:, 512 * xc:512 * xc + 512],
                                          start=(ci == 0), stop=(ci == IC - 1)))
                            rds.append(iv3)
                    em.op("pe", mms, reads=rds, writes=[bankiv(pb, nxc)])
                hv, hiv = view(hT_loc[ot], TBW)
                iv_sg = ival("vg", 0, TBW * BF)
                em.op("act", lambda h2:
                      h2.activation(vg[:, 0:TBW], bank(0, TBW), AF.Silu),
                      reads=[bankiv(0, nxc)], writes=[iv_sg])
                em.op("dve", lambda h2, hv=hv:
                      h2.tensor_tensor(hv, vg[:, 0:TBW], bank(2, TBW), ALU.mult),
                      reads=[iv_sg, bankiv(2, nxc)], writes=[hiv])
                do_chore(12.0)
            if nxt is not None:
                ld_ln3T(nxt[0], nxt[1], qst)
            for dc in range(C.DC):
                w = min(512, D - 512 * dc)
                for oc in range(C.OT):
                    if b == 0 and tb == 0:
                        ensure_wd_rows(min(C.INTER, 128 * oc + 128))
                    par = oc % min(4, T // 512)
                    iv_wd = ival("tmp", 512 * par * BF, (512 * par + w) * BF)
                    qld.issue(tmp[:, 512 * par:512 * par + w],
                              wd_f.ap()[128 * oc:128 * oc + 128,
                                        512 * dc:512 * dc + w],
                              reads=[ival("wd_f", 128 * oc * D * BF,
                                           (128 * oc + 128) * D * BF)],
                              writes=[iv_wd])
                    do_chore(3.0)
                    mms, rds = [], [iv_wd]
                    for t8 in range(TBT):
                        hv, hiv = view(hT_loc[oc], TBW)
                        mms.append(lambda h2, t8=t8, par=par, w=w, hv=hv,
                                   oc=oc:
                                   h2.matmul(bank(t8, w),
                                             hv[:, 128 * t8:128 * t8 + 128],
                                             tmp[:, 512 * par:512 * par + w],
                                             start=(oc == 0),
                                             stop=(oc == C.OT - 1)))
                        rds.append(hiv)
                    em.op("pe", mms, reads=rds,
                          writes=[bankiv(t8) for t8 in range(TBT)])
                for t8 in range(TBT):
                    dv, div = view(dn_loc[t8], D)
                    em.op("act", lambda h2, t8=t8, dc=dc, w=w, dv=dv:
                          h2.activation(dv[:, 512 * dc:512 * dc + w],
                                        bank(t8, w), AF.Identity),
                          reads=[bankiv(t8)],
                          writes=[(div[0], div[1] + BF * 512 * dc,
                                   div[1] + BF * (512 * dc + w))])
            for t8 in range(TBT):
                dv, div = view(dn_loc[t8], D)
                rstd4, nbias4, stiv4 = ln_stats(
                    sbuf_chunks(lambda c0, w2, dv=dv: dv[:, c0:c0 + w2],
                                div[0], div[1], D, BF))
                iv_x1 = ival("qr", 0, D * 4)
                qcast.issue(xa_f32,
                            x1s.ap()[b, t0 + 128 * t8:t0 + 128 * t8 + 128, :],
                            reads=[ival("x1s")], writes=[iv_x1])
                iv_o = ival("kr", 0, D * 4)
                em.op("act", lambda h2, r=rstd4, nb=nbias4, dv=dv:
                      h2.activation(xb_f32, dv, AF.Identity, bias=nb, scale=r),
                      reads=[div, stiv4], writes=[iv_o])
                em.op("dve", lambda h2:
                      h2.tensor_tensor(xb_f32, xb_f32, xa_f32, ALU.add),
                      reads=[iv_x1, iv_o], writes=[iv_o])
                qst.issue(dy.ap()[b, t0 + 128 * t8:t0 + 128 * t8 + 128, :],
                          xb_f32, reads=[iv_o], writes=[ival("y")])

        for b in range(C.BPC):
            phase_A(b)
            phase_B(b)
            phase_C(b)
        bts = [(b, tb) for b in range(C.BPC) for tb in range(C.NTB)]
        for i, (b, tb) in enumerate(bts):
            phase_D(b, tb, i == 0,
                    bts[i + 1] if i + 1 < len(bts) else None)
        finish_chores()

        qst.final_waits("sp")
        qcast.final_waits("sp")
        qld.final_waits("sp")

        em.finalize(e(nc.Block()))
    return nc



def _host_consts(C: Cfg):
    j = np.arange(C.DH // 2, dtype=np.float64)
    theta = C.rope_base ** (-2.0 * j / C.DH)
    t = np.arange(C.T, dtype=np.float64)
    ang = np.outer(theta, t)
    cosT = np.concatenate([np.cos(ang), np.cos(ang)], axis=0).astype(np.float32)
    sinT = np.concatenate([-np.sin(ang), np.sin(ang)], axis=0).astype(np.float32)
    permT = np.zeros((128, 128), dtype=np.float32)
    for m in range(64):
        permT[m + 64, m] = 1.0
        permT[m, m + 64] = 1.0
    k = np.arange(128)[:, None]
    q = np.arange(512)[None, :]
    maskc = np.concatenate(
        [(q >= k + 128 * a).astype(np.float32) for a in range(4)], axis=1)
    return cosT, sinT, permT, maskc


def _deinterleave_cols(W_T, H, DH):
    perm = np.concatenate([h * DH + np.concatenate(
        [np.arange(0, DH, 2), np.arange(1, DH, 2)]) for h in range(H)])
    return np.ascontiguousarray(W_T[:, perm])


def _head_major(WT, W, D):
    IC = D // 128
    G = D // W
    return np.ascontiguousarray(
        WT.reshape(IC, 128, G, W).transpose(2, 1, 0, 3).reshape(G, 128, IC * W))


def make_in_maps(C, X, Wq, Wk, Wv, Wo, lut, gidx, uidx, didx):
    cosT, sinT, permT, maskc = _host_consts(C)
    wqT = _head_major(_deinterleave_cols(
        np.ascontiguousarray(Wq.T), C.H, C.DH), 128, C.D)
    wkT = _head_major(_deinterleave_cols(
        np.ascontiguousarray(Wk.T), C.H, C.DH), 128, C.D)
    wvT = _head_major(np.ascontiguousarray(Wv.T), 2 * 128, C.D)
    woT = np.ascontiguousarray(Wo.T)
    gT = np.ascontiguousarray(gidx.T)
    uT = np.ascontiguousarray(uidx.T)
    dT = np.ascontiguousarray(didx.T)
    in_maps = []
    for c in range(C.n_cores):
        in_maps.append({
            "x": np.ascontiguousarray(X[c * C.BPC:(c + 1) * C.BPC]),
            "wqT": wqT, "wkT": wkT, "wvT": wvT, "woT": woT,
            "lut": np.ascontiguousarray(lut),
            "gidxT": gT, "uidxT": uT, "didxT": dT,
            "csT": cosT, "snT": sinT, "permT": permT, "maskc": maskc,
        })
    return in_maps


def kernel(L0_attn_X, L0_attn_Wq, L0_attn_Wk, L0_attn_Wv, L0_attn_Wo,
           L0_ffn_lut, L0_ffn_gate_idx, L0_ffn_up_idx, L0_ffn_down_idx):
    C = Cfg()
    X = np.asarray(L0_attn_X, dtype=np.float32)
    in_maps = make_in_maps(C, X,
                           np.asarray(L0_attn_Wq, dtype=np.float32),
                           np.asarray(L0_attn_Wk, dtype=np.float32),
                           np.asarray(L0_attn_Wv, dtype=np.float32),
                           np.asarray(L0_attn_Wo, dtype=np.float32),
                           np.asarray(L0_ffn_lut, dtype=np.float32),
                           np.asarray(L0_ffn_gate_idx, dtype=np.int32),
                           np.asarray(L0_ffn_up_idx, dtype=np.int32),
                           np.asarray(L0_ffn_down_idx, dtype=np.int32))
    nc = build_program(C)
    res = run_bass_kernel_spmd(nc, in_maps, core_ids=list(range(C.n_cores)))
    return np.concatenate([r["y"] for r in res.results], axis=0)


# version 27 (final)
# speedup vs baseline: 1.2284x; 1.0577x over previous
import numpy as np
import ml_dtypes
from contextlib import ExitStack

import concourse.bass as bass
import concourse.mybir as mybir
from concourse.bass_utils import run_bass_kernel_spmd

F32 = mybir.dt.float32
BF16 = mybir.dt.bfloat16
I32 = mybir.dt.int32
AF = mybir.ActivationFunctionType
ALU = mybir.AluOpType

BF = 2


class Cfg:
    def __init__(self, n_cores=8, batch=16, T=2048, D=2048, H=16, INTER=5504,
                 LUT=16, tblock=1024, eps=1e-5, rope_base=10000.0):
        self.n_cores = n_cores
        self.batch = batch
        self.T = T
        self.D = D
        self.H = H
        self.DH = D // H
        assert self.DH == 128
        self.INTER = INTER
        self.LUT = LUT
        self.TB = tblock
        self.eps = eps
        self.rope_base = rope_base
        self.BPC = batch // n_cores
        self.IC = D // 128
        self.TT = T // 128
        self.QT = T // 512
        self.OT = INTER // 128
        self.DC = (D + 511) // 512
        assert D % 128 == 0 and INTER % 128 == 0 and T % 512 == 0
        assert D % n_cores == 0 and INTER % n_cores == 0 and T % tblock == 0
        self.gslice = D // n_cores
        self.dslice = INTER // n_cores
        self.scale = self.DH ** -0.5
        self.HG = 2
        assert H % self.HG == 0
        self.NTB = T // tblock
        self.TBT = tblock // 128
        assert self.TBT <= 8
        assert D * 4 <= 2 * T * BF


ENGINES = ("pe", "dve", "act", "pool", "sp")


class Em:

    def __init__(self, nc, stack):
        self.nc = nc
        self.stack = stack
        self.ops = {e: [] for e in ENGINES}
        self.cnt = {e: 0 for e in ENGINES}
        self.esem = {e: stack.enter_context(nc.semaphore(f"es_{e}")) for e in ENGINES}
        self.vc = {e: {} for e in ENGINES}
        self.self_waited = {e: 0 for e in ENGINES}
        self.snap = {e: {} for e in ENGINES}
        self.track = {}
        self.queues = {}
        self.n_waits = 0

    def _merge(self, eng, evvc):
        vc = self.vc[eng]
        for k, v in evvc.items():
            if vc.get(k, 0) < v:
                vc[k] = v

    def await_ev(self, eng, ev):
        axis, n = ev
        if axis == eng:
            if eng == "pe":
                return
            if self.self_waited[eng] < n:
                sem = self.esem[eng]
                self.ops[eng].append(lambda h, s=sem, v=n: h.wait_ge(s, v))
                self.self_waited[eng] = n
                self.n_waits += 1
            return
        if self.vc[eng].get(axis, 0) >= n:
            return
        if axis in ENGINES:
            sem, val = self.esem[axis], n
        else:
            qname, slot = axis
            q = self.queues[qname]
            sem, val = q.sems[slot], 16 * n
        self.ops[eng].append(lambda h, s=sem, v=val: h.wait_ge(s, v))
        self.n_waits += 1
        self._merge(eng, self.snap[axis][n])
        if self.vc[eng].get(axis, 0) < n:
            self.vc[eng][axis] = n

    def _t(self, name):
        if name not in self.track:
            self.track[name] = {"w": [], "r": []}
        return self.track[name]

    @staticmethod
    def _ov(lo, hi, rec):
        return rec[0] < hi and lo < rec[1]

    def _deps_for(self, reads, writes):
        deps = []
        for (name, lo, hi) in reads:
            t = self._t(name)
            deps += [rec[2] for rec in t["w"] if self._ov(lo, hi, rec)]
        for (name, lo, hi) in writes:
            t = self._t(name)
            deps += [rec[2] for rec in t["w"] if self._ov(lo, hi, rec)]
            deps += [rec[2] for rec in t["r"] if self._ov(lo, hi, rec)]
        return deps

    def _record(self, reads, writes, ev):
        for (name, lo, hi) in reads:
            self._t(name)["r"].append((lo, hi, ev))
        for (name, lo, hi) in writes:
            t = self._t(name)
            t["w"] = [rec for rec in t["w"] if not (rec[0] >= lo and rec[1] <= hi)]
            t["r"] = [rec for rec in t["r"] if not (rec[0] >= lo and rec[1] <= hi)]
            t["w"].append((lo, hi, ev))

    def _snapshot(self, eng, axis, n):
        vcs = dict(self.vc[eng])
        vcs[eng] = self.cnt[eng]
        vcs[axis] = n
        self.snap[axis][n] = vcs

    def op(self, eng, fns, reads=(), writes=()):
        if not isinstance(fns, (list, tuple)):
            fns = [fns]
        for ev in self._deps_for(reads, writes):
            self.await_ev(eng, ev)
        n = self.cnt[eng] + 1
        self.cnt[eng] = n
        sem = self.esem[eng]
        for f in fns[:-1]:
            self.ops[eng].append(f)
        f = fns[-1]
        self.ops[eng].append(lambda h, f=f, s=sem: f(h).then_inc(s, 1))
        self.vc[eng][eng] = n
        self._snapshot(eng, eng, n)
        ev = (eng, n)
        self._record(reads, writes, ev)
        return ev

    def finalize(self, block):
        hmap = {"pe": block.tensor, "dve": block.vector, "act": block.scalar,
                "pool": block.gpsimd, "sp": block.sync}
        for e in ENGINES:
            lst = self.ops[e]

            def body(h, lst=lst):
                for f in lst:
                    f(h)
            hmap[e](body)


class DmaQ:

    def __init__(self, em, eng, name, K=8):
        self.em = em
        self.eng = eng
        self.name = name
        self.K = K
        self.sems = [em.stack.enter_context(em.nc.semaphore(f"q_{name}{i}"))
                     for i in range(K)]
        self.n = 0
        em.queues[name] = self
        for i in range(K):
            em.snap[(name, i)] = {}

    def _emit(self, fn, reads, writes):
        em, eng = self.em, self.eng
        n = self.n + 1
        slot = (n - 1) % self.K
        occ = (n - 1) // self.K + 1
        axis = (self.name, slot)
        if occ > 1:
            em.await_ev(eng, (axis, occ - 1))
        for ev in em._deps_for(reads, writes):
            em.await_ev(eng, ev)
        sem = self.sems[slot]
        em.ops[eng].append(lambda h, fn=fn, s=sem: fn(h).then_inc(s, 16))
        self.n = n
        em._snapshot(eng, axis, occ)
        ev = (axis, occ)
        em._record(reads, writes, ev)
        return ev

    def issue(self, out_ap, in_ap, reads=(), writes=(), **kw):
        return self._emit(
            lambda h, o=out_ap, i=in_ap, kw=kw: h.dma_start(out=o, in_=i, **kw),
            reads, writes)

    def issue_fn(self, fn, reads=(), writes=()):
        return self._emit(fn, reads, writes)

    def final_waits(self, eng):
        em = self.em
        counts = [0] * self.K
        for j in range(1, self.n + 1):
            counts[(j - 1) % self.K] += 16
        for s in range(self.K):
            if counts[s]:
                em.ops[eng].append(
                    lambda h, sm=self.sems[s], v=counts[s]: h.wait_ge(sm, v))


def ival(name, lo=0, hi=1 << 40):
    return (name, lo, hi)


def build_program(cfg: Cfg):
    nc = bass.Bass("TRN2", target_bir_lowering=False)
    C = cfg
    T, D, H, IC, TT, QT = C.T, C.D, C.H, C.IC, C.TT, C.QT

    dx = nc.dram_tensor("x", [C.BPC, T, D], F32, kind="ExternalInput")
    dwq = nc.dram_tensor("wqT", [H, 128, D], F32, kind="ExternalInput")
    dwk = nc.dram_tensor("wkT", [H, 128, D], F32, kind="ExternalInput")
    dwv = nc.dram_tensor("wvT", [H // 2, 128, 2 * D], F32, kind="ExternalInput")
    dwo = nc.dram_tensor("woT", [D, D], F32, kind="ExternalInput")
    dlut = nc.dram_tensor("lut", [C.LUT], F32, kind="ExternalInput")
    dgid = nc.dram_tensor("gidxT", [D, C.INTER], I32, kind="ExternalInput")
    duid = nc.dram_tensor("uidxT", [D, C.INTER], I32, kind="ExternalInput")
    ddid = nc.dram_tensor("didxT", [C.INTER, D], I32, kind="ExternalInput")
    dcs = nc.dram_tensor("csT", [128, T], F32, kind="ExternalInput")
    dsn = nc.dram_tensor("snT", [128, T], F32, kind="ExternalInput")
    dperm = nc.dram_tensor("permT", [128, 128], F32, kind="ExternalInput")
    dmask = nc.dram_tensor("maskc", [128, 2048], F32, kind="ExternalInput")
    dy = nc.dram_tensor("y", [C.BPC, T, D], F32, kind="ExternalOutput")

    wq_bf = nc.dram_tensor("wq_bf", [H, 128, D], BF16)
    wk_bf = nc.dram_tensor("wk_bf", [H, 128, D], BF16)
    wv_bf = nc.dram_tensor("wv_bf", [H // 2, 128, 2 * D], BF16)
    wo_bf = nc.dram_tensor("wo_bf", [D, D], BF16)
    ln1b = nc.dram_tensor("ln1b", [T, D], BF16)
    srd = nc.dram_tensor("srd", [512], F32)
    ln3b = nc.dram_tensor("ln3b", [C.BPC, T, D], BF16)
    x1s = nc.dram_tensor("x1s", [C.BPC, T, D], F32)
    wg_f = nc.dram_tensor("wg_f", [D, C.INTER], BF16)
    wu_f = nc.dram_tensor("wu_f", [D, C.INTER], BF16)
    wd_f = nc.dram_tensor("wd_f", [C.INTER, D], BF16)

    with ExitStack() as stack:
        e = stack.enter_context
        ln1T = e(nc.sbuf_tensor("ln1T", [128, IC * T], BF16))
        headsT = e(nc.sbuf_tensor("headsT", [128, H * T], BF16))
        vg = e(nc.sbuf_tensor("vg", [128, TT * C.HG * 128], BF16))
        pt = e(nc.sbuf_tensor("pt", [128, 2048], BF16))
        qraw = e(nc.sbuf_tensor("qraw", [128, T], BF16))
        kraw = e(nc.sbuf_tensor("kraw", [128, T], BF16))
        wvs = e(nc.sbuf_tensor("wvs", [128, IC * C.HG * 128], BF16))
        wqk = e(nc.sbuf_tensor("wqk", [128, IC * 128], BF16))
        qr = e(nc.sbuf_tensor("qr", [128, 2 * T], BF16))
        kr = e(nc.sbuf_tensor("kr", [128, 2 * T], BF16))
        tmp = e(nc.sbuf_tensor("tmp", [128, T], BF16))
        csS = e(nc.sbuf_tensor("csS", [128, T], BF16))
        snS = e(nc.sbuf_tensor("snS", [128, T], BF16))
        permS = e(nc.sbuf_tensor("permS", [128, 128], BF16))
        maskS = e(nc.sbuf_tensor("maskS", [128, 2048], BF16))
        onesS = e(nc.sbuf_tensor("onesS", [128, 2], BF16))
        rows = e(nc.sbuf_tensor("rows", [128, 128], F32))
        srow = e(nc.sbuf_tensor("srow", [1, 512], F32))
        bcbuf = e(nc.sbuf_tensor("bcbuf", [128, 512], F32))
        lutS = e(nc.sbuf_tensor("lutS", [128, C.LUT], F32))
        bixb = e(nc.sbuf_tensor("bixb", [128, 1536], BF16))
        bacc = e(nc.sbuf_tensor("bacc", [128, 768], BF16))
        bsel = e(nc.sbuf_tensor("bsel", [128, 768], BF16))
        bac2 = e(nc.sbuf_tensor("bac2", [128, 768], BF16))
        bse2 = e(nc.sbuf_tensor("bse2", [128, 768], BF16))
        psum = e(nc.psum_tensor("psum", [128, 4096], F32))

        em = Em(nc, stack)
        qld = DmaQ(em, "sp", "qld", 8)
        qst = DmaQ(em, "act", "qst", 8)
        qcast = DmaQ(em, "pool", "qcast", 8)

        PS = "psum"

        def bank(i, w=512):
            return psum[:, 512 * i:512 * i + w]

        def bankiv(i, nb=1):
            return ival(PS, 2048 * i, 2048 * (i + nb))

        xa_f32 = qr.ap().bitcast(F32)[:, 0:D]
        xb_f32 = kr.ap().bitcast(F32)[:, 0:D]

        qcast.issue(csS[:, :], dcs.ap(), writes=[ival("csS")])
        qcast.issue(snS[:, :], dsn.ap(), writes=[ival("snS")])
        qcast.issue(permS[:, :], dperm.ap(), writes=[ival("permS")])
        qcast.issue(maskS[:, :], dmask.ap(), writes=[ival("maskS")])
        qcast.issue(lutS[:, :], dlut.ap().unsqueeze(0).partition_broadcast(128),
                    writes=[ival("lutS")])
        em.op("dve", lambda h: h.memset(onesS[:, :], 1.0), writes=[ival("onesS")])
        qcast.issue(wo_bf.ap(), dwo.ap(), writes=[ival("wo_bf")])
        qcast.issue(wq_bf.ap(), dwq.ap(), writes=[ival("wq_bf")])
        qcast.issue(wk_bf.ap(), dwk.ap(), writes=[ival("wk_bf")])
        qcast.issue(wv_bf.ap(), dwv.ap(), writes=[ival("wv_bf")])

        CW = 768

        def strip_chunks():
            out = []
            c0 = 0
            while c0 < C.INTER:
                w = min(CW, C.INTER - c0)
                for (didx_, dcon) in ((dgid, wg_f), (duid, wu_f)):
                    r0 = 0
                    while r0 < D:
                        p = min(128, D - r0)
                        out.append((didx_, dcon, r0, p, c0, w))
                        r0 += p
                c0 += w
            r0 = 0
            while r0 < C.INTER:
                p = min(128, C.INTER - r0)
                c0 = 0
                while c0 < D:
                    w = min(CW, D - c0)
                    out.append((ddid, wd_f, r0, p, c0, w))
                    c0 += w
                r0 += p
            return out

        def chore_gen():
            chunks = strip_chunks()

            def issue_load(i):
                didx_, dcon, r0, p, c0, w = chunks[i]
                par = i % 2
                bx = bixb[:, CW * par:CW * par + CW]
                iv_b = ival("bixb", CW * par * BF, (CW * par + CW) * BF)
                qcast.issue(bx[:p, :w], didx_.ap()[r0:r0 + p, c0:c0 + w],
                            writes=[iv_b])

            issue_load(0)
            for i, (didx_, dcon, r0, p, c0, w) in enumerate(chunks):
                par = i % 2
                bx = bixb[:, CW * par:CW * par + CW]
                iv_b = ival("bixb", CW * par * BF, (CW * par + CW) * BF)
                iv_a, iv_s = ival("bacc"), ival("bsel")
                iv_a2, iv_s2 = ival("bac2"), ival("bse2")

                def ts_(j, dst, iv_d):
                    em.op("dve", lambda h, p=p, w=w, j=j, dst=dst, bx=bx:
                          h.tensor_scalar(dst[:p, :w], bx[:p, :w], float(j),
                                          lutS[:p, j:j + 1],
                                          ALU.is_equal, ALU.mult),
                          reads=[iv_b, ival("lutS")], writes=[iv_d])

                def tt_(acc, stg, iv_acc, iv_stg):
                    em.op("dve", lambda h, p=p, w=w, acc=acc, stg=stg:
                          h.tensor_tensor(acc[:p, :w], acc[:p, :w],
                                          stg[:p, :w], ALU.add),
                          reads=[iv_stg, iv_acc], writes=[iv_acc])

                ts_(0, bacc, iv_a)
                ts_(8, bac2, iv_a2)
                ts_(1, bsel, iv_s)
                ts_(9, bse2, iv_s2)
                if i + 1 < len(chunks):
                    issue_load(i + 1)
                yield 1.1
                for k in range(1, 7):
                    tt_(bacc, bsel, iv_a, iv_s)
                    tt_(bac2, bse2, iv_a2, iv_s2)
                    ts_(k + 1, bsel, iv_s)
                    ts_(k + 9, bse2, iv_s2)
                    yield 1.2
                tt_(bacc, bsel, iv_a, iv_s)
                tt_(bac2, bse2, iv_a2, iv_s2)
                em.op("dve", lambda h, p=p, w=w:
                      h.tensor_tensor(bacc[:p, :w], bacc[:p, :w],
                                      bac2[:p, :w], ALU.add),
                      reads=[iv_a2, iv_a], writes=[iv_a])
                yield 1.1
                if dcon is wd_f:
                    ivw = (dcon.name, r0 * D * BF, (r0 + p) * D * BF)
                else:
                    ivw = (dcon.name, c0 * BF, (c0 + w) * BF)
                qcast.issue(dcon.ap()[r0:r0 + p, c0:c0 + w], bacc[:p, :w],
                            reads=[iv_a], writes=[ivw])
                if dcon is wd_f:
                    if c0 + w >= D:
                        chore_state["wd_rows"] = r0 + p
                elif dcon is wu_f and r0 + p >= D:
                    chore_state["wgu_cols"] = c0 + w
                chore_state["chunks"] += 1
                yield 0.3
        chore_state = {"done": False, "chunks": 0, "wgu_cols": 0, "wd_rows": 0}
        chore_state["gen"] = chore_gen()

        def do_chore(budget_us):
            while budget_us > 0 and not chore_state["done"]:
                try:
                    budget_us -= next(chore_state["gen"])
                except StopIteration:
                    chore_state["done"] = True

        def ensure_wgu_cols(n):
            while chore_state["wgu_cols"] < n and not chore_state["done"]:
                try:
                    next(chore_state["gen"])
                except StopIteration:
                    chore_state["done"] = True

        def ensure_wd_rows(n):
            while chore_state["wd_rows"] < n and not chore_state["done"]:
                try:
                    next(chore_state["gen"])
                except StopIteration:
                    chore_state["done"] = True

        def finish_chores():
            do_chore(1e9)




        slot_state = {"i": 0}

        def ln_stats(chunks):
            sl = 32 * (slot_state["i"] % 3)
            slot_state["i"] += 1
            nch = len(chunks)
            stat_iv = ival("rows", sl * 4, (sl + 8 + 6 * nch) * 4)
            sc = sl + 8
            for i, (ap, iv) in enumerate(chunks):
                em.op("dve", lambda h, ap=ap, i=i, sc=sc:
                      h.bn_stats(rows[:, sc + 6 * i: sc + 6 * i + 6], ap),
                      reads=[iv], writes=[stat_iv])
            em.op("dve", lambda h, sl=sl, sc=sc, nch=nch:
                  h.bn_aggr(rows[:, sl:sl + 2], rows[:, sc:sc + 6 * nch]),
                  reads=[stat_iv], writes=[stat_iv])
            em.op("dve", lambda h, sl=sl:
                  h.tensor_scalar(rows[:, sl + 2:sl + 3], rows[:, sl + 1:sl + 2],
                                  C.eps, None, ALU.add),
                  reads=[stat_iv], writes=[stat_iv])
            em.op("act", lambda h, sl=sl:
                  h.activation(rows[:, sl + 3:sl + 4], rows[:, sl + 2:sl + 3],
                               AF.Ln),
                  reads=[stat_iv], writes=[stat_iv])
            em.op("act", lambda h, sl=sl:
                  h.activation(rows[:, sl + 4:sl + 5], rows[:, sl + 3:sl + 4],
                               AF.Exp, scale=-0.5),
                  reads=[stat_iv], writes=[stat_iv])
            em.op("dve", lambda h, sl=sl:
                  h.tensor_tensor(rows[:, sl + 5:sl + 6], rows[:, sl:sl + 1],
                                  rows[:, sl + 4:sl + 5], ALU.mult),
                  reads=[stat_iv], writes=[stat_iv])
            em.op("dve", lambda h, sl=sl:
                  h.tensor_scalar(rows[:, sl + 5:sl + 6], rows[:, sl + 5:sl + 6],
                                  -1.0, None, ALU.mult),
                  reads=[stat_iv], writes=[stat_iv])
            return (rows[:, sl + 4:sl + 5], rows[:, sl + 5:sl + 6], stat_iv)

        def sbuf_chunks(ap_maker, name, base_bytes, n_elems, esize):
            out = []
            c0 = 0
            while c0 < n_elems:
                w = min(512, n_elems - c0)
                out.append((ap_maker(c0, w),
                            ival(name, base_bytes + c0 * esize,
                                 base_bytes + (c0 + w) * esize)))
                c0 += w
            return out

        def psum_chunks(bank0, n_elems):
            out = []
            c0 = 0
            while c0 < n_elems:
                w = min(512, n_elems - c0)
                out.append((psum[:, 512 * bank0 + c0: 512 * bank0 + c0 + w],
                            ival(PS, 2048 * bank0 + 4 * c0,
                                 2048 * bank0 + 4 * (c0 + w))))
                c0 += w
            return out

        def phase_A(b):
            for tt in range(TT):
                xap, xnm = (xa_f32, "qr") if tt % 2 == 0 else (xb_f32, "kr")
                iv_x = ival(xnm, 0, D * 4)
                qld.issue(xap, dx.ap()[b, 128 * tt:128 * tt + 128, :],
                          writes=[iv_x])
                ch = sbuf_chunks(lambda c0, w, xap=xap: xap[:, c0:c0 + w],
                                 xnm, 0, D, 4)
                rstd, nbias, stiv = ln_stats(ch)
                lnb, lnm = (tmp, "tmp") if tt % 2 == 0 else (pt, "pt")
                iv_l = ival(lnm, 0, D * BF)
                em.op("act", lambda h, xap=xap, r=rstd, nb=nbias, lnb=lnb:
                      h.activation(lnb[:, 0:D], xap, AF.Identity, bias=nb,
                                   scale=r),
                      reads=[iv_x, stiv], writes=[iv_l])
                qst.issue(ln1b.ap()[128 * tt:128 * tt + 128, :], lnb[:, 0:D],
                          reads=[iv_l],
                          writes=[ival("ln1b", 128 * tt * D * BF,
                                        (128 * tt + 128) * D * BF)])
                do_chore(4.0)

        def load_ln1T():
            for ci in range(IC):
                qld.issue(ln1T[:, T * ci:T * ci + T],
                          ln1b.ap()[:, 128 * ci:128 * ci + 128],
                          reads=[ival("ln1b")],
                          writes=[ival("ln1T", T * BF * ci, T * BF * (ci + 1))],
                          transpose=True)

        def produce_qk(b, h, which):
            dw = wq_bf if which == "q" else wk_bf
            iv_w = ival("wqk")
            qld.issue(wqk[:, :], dw.ap()[h],
                      reads=[ival(dw.name, h * 128 * D * BF,
                                  (h + 1) * 128 * D * BF)],
                      writes=[iv_w])
            raw = qraw if which == "q" else kraw
            rawn = "qraw" if which == "q" else "kraw"
            for tc in range(T // 512):
                pb = tc % 2
                mms, rds = [], [iv_w]
                for ci in range(IC):
                    mms.append(lambda h2, ci=ci, tc=tc, pb=pb:
                               h2.matmul(bank(pb), wqk[:, 128 * ci:128 * ci + 128],
                                         ln1T[:, T * ci + 512 * tc:
                                              T * ci + 512 * tc + 512],
                                         start=(ci == 0), stop=(ci == IC - 1)))
                    rds.append(ival("ln1T", BF * (T * ci + 512 * tc),
                                    BF * (T * ci + 512 * tc + 512)))
                em.op("pe", mms, reads=rds, writes=[bankiv(pb)])
                em.op("act", lambda h2, tc=tc, pb=pb, raw=raw:
                      h2.activation(raw[:, 512 * tc: 512 * tc + 512],
                                    bank(pb), AF.Identity),
                      reads=[bankiv(pb)],
                      writes=[ival(rawn, BF * 512 * tc, BF * (512 * tc + 512))])
                do_chore(3.5)
            return rawn

        def rope(rawn, dst, dst_name, par):
            src = qraw if rawn == "qraw" else kraw
            stg = kraw if rawn == "qraw" else qraw
            stgn = "kraw" if rawn == "qraw" else "qraw"
            iv_src = ival(rawn)
            iv_stg = ival(stgn)
            dstv = dst[:, T * par:T * par + T]
            iv_dst = ival(dst_name, T * BF * par, T * BF * (par + 1))
            iv_t = ival("tmp", 0, T * BF)
            nh = T // 1024
            em.op("dve", lambda h: h.tensor_tensor(tmp[:, 0:T], src[:, 0:T],
                                                   csS[:, :], ALU.mult),
                  reads=[iv_src, ival("csS")], writes=[iv_t])
            for hb in range(nh):
                mms = []
                for tc in range(2):
                    mms.append(lambda h, hb=hb, tc=tc:
                               h.matmul(bank(6 + tc),
                                        permS[:, :],
                                        src[:, 1024 * hb + 512 * tc:
                                            1024 * hb + 512 * tc + 512],
                                        start=True, stop=True))
                em.op("pe", mms, reads=[iv_src, ival("permS")],
                      writes=[bankiv(6, 2)])
                em.op("act", lambda h, hb=hb, stg=stg:
                      h.activation(stg[:, 1024 * hb:1024 * hb + 1024],
                                   psum[:, 3072:4096], AF.Identity),
                      reads=[bankiv(6, 2)], writes=[iv_stg])
                em.op("dve", lambda h, hb=hb, dstv=dstv, stg=stg:
                      h.tensor_tensor(dstv[:, 1024 * hb:1024 * hb + 1024],
                                      stg[:, 1024 * hb:1024 * hb + 1024],
                                      snS[:, 1024 * hb:1024 * hb + 1024],
                                      ALU.mult),
                      reads=[iv_stg, ival("snS")], writes=[iv_dst])
            em.op("dve", lambda h, dstv=dstv:
                  h.tensor_tensor(dstv, dstv, tmp[:, 0:T], ALU.add),
                  reads=[iv_t, iv_dst], writes=[iv_dst])

        def produce_group_v(b, g):
            W = C.HG * 128
            iv_w = ival("wvs")
            qld.issue(wvs[:, :IC * W], wv_bf.ap()[g],
                      reads=[ival("wv_bf", g * 128 * 2 * D * BF,
                                  (g + 1) * 128 * 2 * D * BF)],
                      writes=[iv_w])
            for ct in range(TT):
                pb = 2 + (ct % 2)
                mms, rds = [], [iv_w]
                for ci in range(IC):
                    mms.append(lambda h2, ci=ci, ct=ct, pb=pb, W=W:
                               h2.matmul(bank(pb, W),
                                         ln1T[:, T * ci + 128 * ct:
                                              T * ci + 128 * ct + 128],
                                         wvs[:, W * ci: W * ci + W],
                                         start=(ci == 0), stop=(ci == IC - 1)))
                    rds.append(ival("ln1T", BF * (T * ci + 128 * ct),
                                    BF * (T * ci + 128 * ct + 128)))
                em.op("pe", mms, reads=rds, writes=[bankiv(pb)])
                em.op("act", lambda h2, ct=ct, pb=pb, W=W:
                      h2.activation(vg[:, W * ct: W * ct + W], bank(pb, W),
                                    AF.Identity),
                      reads=[bankiv(pb)],
                      writes=[ival("vg", BF * W * ct, BF * W * (ct + 1))])

        def attend(b, h, par):
            hh = h % C.HG
            W = C.HG * 128
            qrh = qr[:, T * par: T * par + T]
            krh = kr[:, T * par: T * par + T]
            iv_qr = ival("qr", T * BF * par, T * BF * (par + 1))
            iv_kr = ival("kr", T * BF * par, T * BF * (par + 1))
            for qt in range(QT):
                nch = 4 * qt + 4
                ngrp = nch // 2
                def emit_st(g):
                    pb = 2 * (g % 2)
                    mms = []
                    for cc in range(2):
                        c = 2 * g + cc
                        mms.append(lambda h2, c=c, cc=cc, qt=qt, pb=pb:
                                   h2.matmul(bank(pb + cc),
                                             krh[:, 128 * c:128 * c + 128],
                                             qrh[:, 512 * qt:512 * qt + 512],
                                             start=True, stop=True))
                    em.op("pe", mms, reads=[iv_qr, iv_kr],
                          writes=[bankiv(pb, 2)])

                def emit_exp(g):
                    pb = 2 * (g % 2)
                    pc0 = 1024 * (g % 2)
                    iv_p = ival("pt", pc0 * BF, (pc0 + 1024) * BF)
                    em.op("act", lambda h2, pc0=pc0, pb=pb:
                          h2.activation(pt[:, pc0:pc0 + 1024], bank(pb, 1024),
                                        AF.Exp, scale=C.scale),
                          reads=[bankiv(pb, 2)], writes=[iv_p])
                    if g >= ngrp - 2:
                        mc0 = 1024 * (g - (ngrp - 2))
                        em.op("dve", lambda h2, pc0=pc0, mc0=mc0:
                              h2.tensor_tensor(pt[:, pc0:pc0 + 1024],
                                               pt[:, pc0:pc0 + 1024],
                                               maskS[:, mc0:mc0 + 1024],
                                               ALU.mult),
                              reads=[ival("maskS"), iv_p], writes=[iv_p])
                    return iv_p

                def emit_pv(g, iv_p):
                    pc0 = 1024 * (g % 2)
                    mms = []
                    for cc in range(2):
                        c = 2 * g + cc
                        mms.append(lambda h2, c=c, cc=cc, pc0=pc0, hh=hh,
                                   W=W, nch=nch:
                                   h2.matmul(bank(4),
                                             vg[:, W * c + 128 * hh:
                                                W * c + 128 * hh + 128],
                                             pt[:, pc0 + 512 * cc:
                                                pc0 + 512 * cc + 512],
                                             start=(c == 0),
                                             stop=(c == nch - 1)))
                    em.op("pe", mms, reads=[ival("vg"), iv_p],
                          writes=[bankiv(4)])
                    mms = []
                    for cc in range(2):
                        c = 2 * g + cc
                        mms.append(lambda h2, c=c, cc=cc, pc0=pc0, nch=nch:
                                   h2.matmul(psum[0:1, 2560:3072],
                                             onesS[:, 0:1],
                                             pt[:, pc0 + 512 * cc:
                                                pc0 + 512 * cc + 512],
                                             start=(c == 0),
                                             stop=(c == nch - 1)))
                    em.op("pe", mms, reads=[ival("onesS"), iv_p],
                          writes=[bankiv(5)])

                emit_st(0)
                iv_prev = emit_exp(0)
                for g in range(1, ngrp):
                    emit_st(g)
                    iv_p = emit_exp(g)
                    emit_pv(g - 1, iv_prev)
                    iv_prev = iv_p
                emit_pv(ngrp - 1, iv_prev)
                do_chore(3.0 * (qt + 1))
                iv_sr = ival("srow")
                em.op("act", lambda h2:
                      h2.activation(srow[0:1, 0:512], psum[0:1, 2560:3072],
                                    AF.Ln),
                      reads=[bankiv(5)], writes=[iv_sr])
                em.op("act", lambda h2:
                      h2.activation(srow[0:1, 0:512], srow[0:1, 0:512],
                                    AF.Exp, scale=-1.0),
                      reads=[iv_sr], writes=[iv_sr])
                qst.issue(srd.ap().unsqueeze(0), srow[0:1, 0:512],
                          reads=[iv_sr], writes=[ival("srd")])
                qst.issue(bcbuf[:, :],
                          srd.ap().unsqueeze(0).partition_broadcast(128),
                          reads=[ival("srd")], writes=[ival("bcbuf")])
                hc0 = BF * (T * h + 512 * qt)
                em.op("dve", lambda h2, h=h, qt=qt:
                      h2.tensor_tensor(headsT[:, T * h + 512 * qt:
                                              T * h + 512 * qt + 512],
                                       bank(4), bcbuf[:, :], ALU.mult),
                      reads=[bankiv(4), ival("bcbuf")],
                      writes=[ival("headsT", hc0, hc0 + 512 * BF)])

        def phase_B(b):
            load_ln1T()
            produce_group_v(b, 0)
            rope(produce_qk(b, 0, "q"), qr, "qr", 0)
            rope(produce_qk(b, 0, "k"), kr, "kr", 0)
            for h in range(H):
                if h + 1 < H:
                    par1 = (h + 1) % 2
                    rope(produce_qk(b, h + 1, "q"), qr, "qr", par1)
                    rope(produce_qk(b, h + 1, "k"), kr, "kr", par1)
                attend(b, h, h % 2)
                if h + 1 < H and (h + 1) % C.HG == 0:
                    produce_group_v(b, (h + 1) // C.HG)
                do_chore(12.0)

        def phase_C(b):
            for ci in range(IC):
                qld.issue(ln1T[:, T * ci: T * ci + D],
                          wo_bf.ap()[128 * ci:128 * ci + 128, :],
                          reads=[ival("wo_bf")],
                          writes=[ival("ln1T", BF * T * ci,
                                       BF * (T * ci + D))])
            nhalf = (D + 1023) // 1024
            rot = {"g": 0}
            for tt in range(TT):
                groups = []
                stat_chunks = []
                for hf in range(nhalf):
                    g = rot["g"]
                    rot["g"] = (g + 1) % 3
                    groups.append(g)
                    hw = min(1024, D - 1024 * hf)
                    noc = (hw + 511) // 512
                    mms, rds = [], []
                    for oc2 in range(noc):
                        w = min(512, hw - 512 * oc2)
                        oc = 2 * hf + oc2
                        for jc in range(IC):
                            mms.append(
                                lambda h2, oc=oc, jc=jc, w=w, tt=tt, g=g,
                                oc2=oc2:
                                h2.matmul(bank(2 * g + oc2, w),
                                          headsT[:, T * jc + 128 * tt:
                                                 T * jc + 128 * tt + 128],
                                          ln1T[:, T * jc + 512 * oc:
                                               T * jc + 512 * oc + w],
                                          start=(jc == 0),
                                          stop=(jc == IC - 1)))
                            rds.append(ival("headsT", BF * (T * jc + 128 * tt),
                                            BF * (T * jc + 128 * tt + 128)))
                            rds.append(ival("ln1T", BF * (T * jc + 512 * oc),
                                            BF * (T * jc + 512 * oc + w)))
                    em.op("pe", mms, reads=rds, writes=[bankiv(2 * g, noc)])
                    for oc2 in range(noc):
                        w = min(512, hw - 512 * oc2)
                        stat_chunks.append(
                            (psum[:, 512 * (2 * g + oc2):
                                  512 * (2 * g + oc2) + w],
                             ival(PS, 2048 * (2 * g + oc2),
                                  2048 * (2 * g + oc2) + 4 * w)))
                do_chore(7.0)
                rstd, nbias, stiv = ln_stats(stat_chunks)
                iv_x = ival("qr", 0, D * 4)
                qld.issue(xa_f32, dx.ap()[b, 128 * tt:128 * tt + 128, :],
                          writes=[iv_x])
                iv_x1 = ival("kr", 0, D * 4)
                nrm = []
                rdn = [stiv]
                for hf in range(nhalf):
                    g = groups[hf]
                    hw = min(1024, D - 1024 * hf)
                    noc = (hw + 511) // 512
                    em.op("act", lambda h2, r=rstd, nb=nbias, g=g, hf=hf,
                          hw=hw:
                          h2.activation(xb_f32[:, 1024 * hf:1024 * hf + hw],
                                        psum[:, 512 * 2 * g:512 * 2 * g + hw],
                                        AF.Identity, bias=nb, scale=r),
                          reads=[bankiv(2 * g, noc), stiv], writes=[iv_x1])
                em.op("dve", lambda h2:
                      h2.tensor_tensor(xb_f32, xb_f32, xa_f32, ALU.add),
                      reads=[iv_x, iv_x1], writes=[iv_x1])
                qst.issue(x1s.ap()[b, 128 * tt:128 * tt + 128, :], xb_f32,
                          reads=[iv_x1], writes=[ival("x1s")])
                rstd3, nbias3, stiv3 = ln_stats(
                    sbuf_chunks(lambda c0, w: xb_f32[:, c0:c0 + w], "kr", 0,
                                D, 4))
                iv_l3 = ival("tmp", 0, D * BF)
                em.op("act", lambda h2, r=rstd3, nb=nbias3:
                      h2.activation(tmp[:, 0:D], xb_f32, AF.Identity,
                                    bias=nb, scale=r),
                      reads=[iv_x1, stiv3], writes=[iv_l3])
                qst.issue(ln3b.ap()[b, 128 * tt:128 * tt + 128, :], tmp[:, 0:D],
                          reads=[iv_l3], writes=[ival("ln3b")])
                do_chore(7.0)

        TBT, TBW = C.TBT, C.TB

        free_regions = [("ln1T", 0, IC * T), ("headsT", 0, H * T)]
        cur = {"ri": 0, "off": 0}

        def take(ncols):
            while True:
                name, base, cap = free_regions[cur["ri"]]
                if cur["off"] + ncols <= cap:
                    out = (name, base + cur["off"])
                    cur["off"] += ncols
                    return out
                cur["ri"] += 1
                cur["off"] = 0
                assert cur["ri"] < len(free_regions), "phase-D SBUF overflow"

        tens = {"ln1T": ln1T, "headsT": headsT, "pt": pt, "vg": vg,
                "wvs": wvs, "maskS": maskS, "csS": csS, "snS": snS}
        hT_loc = [take(TBW) for _ in range(C.OT)]
        ln3T_loc = [take(TBW) for _ in range(IC)]
        free_regions2 = [("pt", 0, 2048), ("vg", TBW, TT * C.HG * 128),
                         ("wvs", IC * 128, IC * C.HG * 128),
                         ("maskS", 0, 2048), ("csS", 0, T), ("snS", 0, T),
                         free_regions[cur["ri"]][:1] + (
                             free_regions[cur["ri"]][1] + cur["off"],
                             free_regions[cur["ri"]][2])]
        cur2 = {"ri": 0, "off": 0}

        def take2(ncols):
            while True:
                name, base, cap = free_regions2[cur2["ri"]]
                if base + cur2["off"] + ncols <= cap:
                    out = (name, base + cur2["off"])
                    cur2["off"] += ncols
                    return out
                cur2["ri"] += 1
                cur2["off"] = 0
                assert cur2["ri"] < len(free_regions2), "phase-D SBUF overflow2"

        dn_loc = [take2(D) for _ in range(TBT)]

        def view(loc, ncols):
            name, c0 = loc
            t = tens[name]
            return t[:, c0:c0 + ncols], ival(name, BF * c0, BF * (c0 + ncols))

        def ld_ln3T(b, tb, q):
            t0 = tb * TBW
            for ci in range(IC):
                v, iv = view(ln3T_loc[ci], TBW)
                q.issue(v, ln3b.ap()[b, t0:t0 + TBW, 128 * ci:128 * ci + 128],
                        reads=[ival("ln3b")], writes=[iv], transpose=True)

        def phase_D(b, tb, first, nxt):
            t0 = tb * TBW
            if first:
                ld_ln3T(b, tb, qld)
            nxc = TBW // 512
            for ot in range(C.OT):
                if b == 0 and tb == 0:
                    ensure_wgu_cols(min(C.INTER, 128 * ot + 128))
                iv_wg = ival("wqk")
                qld.issue(wqk[:, :].rearrange("p (c o) -> p c o", o=128),
                          wg_f.ap()[:, 128 * ot:128 * ot + 128]
                          .rearrange("(c p) o -> p c o", p=128),
                          reads=[ival("wg_f", 128 * ot * BF,
                                       (128 * ot + 128) * BF)],
                          writes=[iv_wg])
                iv_wu = ival("wvs", 0, IC * 128 * BF)
                qld.issue(wvs[:, 0:IC * 128].rearrange("p (c o) -> p c o", o=128),
                          wu_f.ap()[:, 128 * ot:128 * ot + 128]
                          .rearrange("(c p) o -> p c o", p=128),
                          reads=[ival("wu_f", 128 * ot * BF,
                                       (128 * ot + 128) * BF)],
                          writes=[iv_wu])
                for (iv_w, wt, pb) in ((iv_wg, wqk, 0), (iv_wu, wvs, 2)):
                    mms, rds = [], [iv_w]
                    for xc in range(nxc):
                        for ci in range(IC):
                            v3, iv3 = view(ln3T_loc[ci], TBW)
                            mms.append(
                                lambda h2, ci=ci, xc=xc, wt=wt, pb=pb, v3=v3:
                                h2.matmul(bank(pb + xc),
                                          wt[:, 128 * ci:128 * ci + 128],
                                          v3[:, 512 * xc:512 * xc + 512],
                                          start=(ci == 0), stop=(ci == IC - 1)))
                            rds.append(iv3)
                    em.op("pe", mms, reads=rds, writes=[bankiv(pb, nxc)])
                hv, hiv = view(hT_loc[ot], TBW)
                iv_sg = ival("vg", 0, TBW * BF)
                em.op("act", lambda h2:
                      h2.activation(vg[:, 0:TBW], bank(0, TBW), AF.Silu),
                      reads=[bankiv(0, nxc)], writes=[iv_sg])
                em.op("dve", lambda h2, hv=hv:
                      h2.tensor_tensor(hv, vg[:, 0:TBW], bank(2, TBW), ALU.mult),
                      reads=[iv_sg, bankiv(2, nxc)], writes=[hiv])
                do_chore(12.0)
            if nxt is not None:
                ld_ln3T(nxt[0], nxt[1], qst)
            for dc in range(C.DC):
                w = min(512, D - 512 * dc)
                for oc in range(C.OT):
                    if b == 0 and tb == 0:
                        ensure_wd_rows(min(C.INTER, 128 * oc + 128))
                    par = oc % min(4, T // 512)
                    iv_wd = ival("tmp", 512 * par * BF, (512 * par + w) * BF)
                    qld.issue(tmp[:, 512 * par:512 * par + w],
                              wd_f.ap()[128 * oc:128 * oc + 128,
                                        512 * dc:512 * dc + w],
                              reads=[ival("wd_f", 128 * oc * D * BF,
                                           (128 * oc + 128) * D * BF)],
                              writes=[iv_wd])
                    do_chore(3.0)
                    mms, rds = [], [iv_wd]
                    for t8 in range(TBT):
                        hv, hiv = view(hT_loc[oc], TBW)
                        mms.append(lambda h2, t8=t8, par=par, w=w, hv=hv,
                                   oc=oc:
                                   h2.matmul(bank(t8, w),
                                             hv[:, 128 * t8:128 * t8 + 128],
                                             tmp[:, 512 * par:512 * par + w],
                                             start=(oc == 0),
                                             stop=(oc == C.OT - 1)))
                        rds.append(hiv)
                    em.op("pe", mms, reads=rds,
                          writes=[bankiv(t8) for t8 in range(TBT)])
                for t8 in range(TBT):
                    dv, div = view(dn_loc[t8], D)
                    em.op("act", lambda h2, t8=t8, dc=dc, w=w, dv=dv:
                          h2.activation(dv[:, 512 * dc:512 * dc + w],
                                        bank(t8, w), AF.Identity),
                          reads=[bankiv(t8)],
                          writes=[(div[0], div[1] + BF * 512 * dc,
                                   div[1] + BF * (512 * dc + w))])
            for t8 in range(TBT):
                dv, div = view(dn_loc[t8], D)
                rstd4, nbias4, stiv4 = ln_stats(
                    sbuf_chunks(lambda c0, w2, dv=dv: dv[:, c0:c0 + w2],
                                div[0], div[1], D, BF))
                iv_x1 = ival("qr", 0, D * 4)
                qcast.issue(xa_f32,
                            x1s.ap()[b, t0 + 128 * t8:t0 + 128 * t8 + 128, :],
                            reads=[ival("x1s")], writes=[iv_x1])
                iv_o = ival("kr", 0, D * 4)
                em.op("act", lambda h2, r=rstd4, nb=nbias4, dv=dv:
                      h2.activation(xb_f32, dv, AF.Identity, bias=nb, scale=r),
                      reads=[div, stiv4], writes=[iv_o])
                em.op("dve", lambda h2:
                      h2.tensor_tensor(xb_f32, xb_f32, xa_f32, ALU.add),
                      reads=[iv_x1, iv_o], writes=[iv_o])
                qst.issue(dy.ap()[b, t0 + 128 * t8:t0 + 128 * t8 + 128, :],
                          xb_f32, reads=[iv_o], writes=[ival("y")])

        for b in range(C.BPC):
            phase_A(b)
            phase_B(b)
            phase_C(b)
        bts = [(b, tb) for b in range(C.BPC) for tb in range(C.NTB)]
        for i, (b, tb) in enumerate(bts):
            phase_D(b, tb, i == 0,
                    bts[i + 1] if i + 1 < len(bts) else None)
        finish_chores()

        qst.final_waits("sp")
        qcast.final_waits("sp")
        qld.final_waits("sp")

        em.finalize(e(nc.Block()))
    return nc



def _host_consts(C: Cfg):
    j = np.arange(C.DH // 2, dtype=np.float64)
    theta = C.rope_base ** (-2.0 * j / C.DH)
    t = np.arange(C.T, dtype=np.float64)
    ang = np.outer(theta, t)
    cosT = np.concatenate([np.cos(ang), np.cos(ang)], axis=0).astype(np.float32)
    sinT = np.concatenate([-np.sin(ang), np.sin(ang)], axis=0).astype(np.float32)
    permT = np.zeros((128, 128), dtype=np.float32)
    for m in range(64):
        permT[m + 64, m] = 1.0
        permT[m, m + 64] = 1.0
    k = np.arange(128)[:, None]
    q = np.arange(512)[None, :]
    maskc = np.concatenate(
        [(q >= k + 128 * a).astype(np.float32) for a in range(4)], axis=1)
    return cosT, sinT, permT, maskc


def _deinterleave_cols(W_T, H, DH):
    perm = np.concatenate([h * DH + np.concatenate(
        [np.arange(0, DH, 2), np.arange(1, DH, 2)]) for h in range(H)])
    return np.ascontiguousarray(W_T[:, perm])


def _head_major(WT, W, D):
    IC = D // 128
    G = D // W
    return np.ascontiguousarray(
        WT.reshape(IC, 128, G, W).transpose(2, 1, 0, 3).reshape(G, 128, IC * W))


def make_in_maps(C, X, Wq, Wk, Wv, Wo, lut, gidx, uidx, didx):
    cosT, sinT, permT, maskc = _host_consts(C)
    wqT = _head_major(_deinterleave_cols(
        np.ascontiguousarray(Wq.T), C.H, C.DH), 128, C.D)
    wkT = _head_major(_deinterleave_cols(
        np.ascontiguousarray(Wk.T), C.H, C.DH), 128, C.D)
    wvT = _head_major(np.ascontiguousarray(Wv.T), 2 * 128, C.D)
    woT = np.ascontiguousarray(Wo.T)
    gT = np.ascontiguousarray(gidx.T)
    uT = np.ascontiguousarray(uidx.T)
    dT = np.ascontiguousarray(didx.T)
    in_maps = []
    for c in range(C.n_cores):
        in_maps.append({
            "x": np.ascontiguousarray(X[c * C.BPC:(c + 1) * C.BPC]),
            "wqT": wqT, "wkT": wkT, "wvT": wvT, "woT": woT,
            "lut": np.ascontiguousarray(lut),
            "gidxT": gT, "uidxT": uT, "didxT": dT,
            "csT": cosT, "snT": sinT, "permT": permT, "maskc": maskc,
        })
    return in_maps


def kernel(L0_attn_X, L0_attn_Wq, L0_attn_Wk, L0_attn_Wv, L0_attn_Wo,
           L0_ffn_lut, L0_ffn_gate_idx, L0_ffn_up_idx, L0_ffn_down_idx):
    C = Cfg()
    X = np.asarray(L0_attn_X, dtype=np.float32)
    in_maps = make_in_maps(C, X,
                           np.asarray(L0_attn_Wq, dtype=np.float32),
                           np.asarray(L0_attn_Wk, dtype=np.float32),
                           np.asarray(L0_attn_Wv, dtype=np.float32),
                           np.asarray(L0_attn_Wo, dtype=np.float32),
                           np.asarray(L0_ffn_lut, dtype=np.float32),
                           np.asarray(L0_ffn_gate_idx, dtype=np.int32),
                           np.asarray(L0_ffn_up_idx, dtype=np.int32),
                           np.asarray(L0_ffn_down_idx, dtype=np.int32))
    nc = build_program(C)
    res = run_bass_kernel_spmd(nc, in_maps, core_ids=list(range(C.n_cores)))
    return np.concatenate([r["y"] for r in res.results], axis=0)
